# version 02
# baseline (speedup 1.0000x reference)
import numpy as np
import ml_dtypes
from contextlib import ExitStack
import concourse.bass as bass
import concourse.mybir as mybir
from concourse.bass_utils import run_bass_kernel_spmd
from concourse.bass_core_rust import IndirectOffsetOnAxis

F32 = mybir.dt.float32
BF16 = mybir.dt.bfloat16
I32 = mybir.dt.int32
AF = mybir.ActivationFunctionType
ALU = mybir.AluOpType
BF16NP = ml_dtypes.bfloat16

NCORES = 8
D = 4096
B_SZ = 4
S = 4096
TC = 2048
EPS = 1e-5


def build_ln_launch(with_rms: bool, ntiles: int = TC // 128):
    nc = bass.Bass("TRN2", target_bir_lowering=False)
    xa = nc.dram_tensor("xa", [ntiles * 128, D], F32, kind="ExternalInput").ap()
    if with_rms:
        xb = nc.dram_tensor("xb", [ntiles * 128, D], F32, kind="ExternalInput").ap()
        xs = nc.dram_tensor("xs", [ntiles * 128, D], F32, kind="ExternalOutput").ap()
        h = nc.dram_tensor("h", [ntiles * 128, D], BF16, kind="ExternalOutput").ap()
    xn = nc.dram_tensor("xn", [ntiles * 128, D], BF16, kind="ExternalOutput").ap()

    nst = 3 if with_rms else 1
    nld = 2 if with_rms else 1
    naff = 2 if with_rms else 1

    with (
        nc.sbuf_tensor("xin", [128, 2 * D], F32) as xin,
        nc.sbuf_tensor("xbin", [128, (2 * D) if with_rms else 1], F32) as xbin,
        nc.sbuf_tensor("xnt", [128, 2 * D], BF16) as xnt,
        nc.sbuf_tensor("ht", [128, (2 * D) if with_rms else 1], BF16) as ht,
        nc.sbuf_tensor("stats", [128, 2 * 48], F32) as stats,
        nc.sbuf_tensor("mv", [128, 2 * 2], F32) as mv,
        nc.sbuf_tensor("tiny", [128, 16], F32) as tiny,
        nc.sbuf_tensor("cst", [128, 2], F32) as cst,
        nc.semaphore("s_fen") as s_fen,
        nc.semaphore("s_ld0") as s_ld0,
        nc.semaphore("s_ld1") as s_ld1,
        nc.semaphore("s_bn") as s_bn,
        nc.semaphore("s_rs") as s_rs,
        nc.semaphore("s_aff") as s_aff,
        nc.semaphore("s_st0") as s_st0,
        nc.semaphore("s_st1") as s_st1,
        nc.Block() as block,
    ):
        s_ld = [s_ld0, s_ld1]
        s_st = [s_st0, s_st1]

        @block.sync
        def _(sync):
            for i in range(ntiles):
                p = i % 2
                if i >= 2:
                    sync.wait_ge(s_aff, naff * (i - 1))
                    if with_rms:
                        sync.wait_ge(s_st[p], 16 * nst * (i // 2))
                sync.dma_start(out=xin[:, p * D:(p + 1) * D],
                               in_=xa[i * 128:(i + 1) * 128, :]).then_inc(s_ld[p], 16)
                if with_rms:
                    sync.dma_start(out=xbin[:, p * D:(p + 1) * D],
                                   in_=xb[i * 128:(i + 1) * 128, :]).then_inc(s_ld[p], 16)

        @block.vector
        def _(v):
            nfen = [0]
            v.memset(cst[:, 0:1], EPS)
            v.memset(cst[:, 1:2], EPS * EPS).then_inc(s_fen, 1)
            nfen[0] += 1
            v.wait_ge(s_fen, nfen[0])
            for i in range(ntiles):
                p = i % 2
                q = p * 8
                v.wait_ge(s_ld[p], 16 * nld * (i // 2 + 1))
                if i >= 2:
                    v.wait_ge(s_rs, i - 1)
                if with_rms:
                    v.tensor_add(xin[:, p * D:(p + 1) * D], xin[:, p * D:(p + 1) * D],
                                 xbin[:, p * D:(p + 1) * D]).then_inc(s_fen, 1)
                    nfen[0] += 1
                    v.wait_ge(s_fen, nfen[0])
                for j in range(8):
                    r = v.bn_stats(stats[:, p * 48 + j * 6: p * 48 + (j + 1) * 6],
                                   xin[:, p * D + j * 512: p * D + (j + 1) * 512])
                r.then_inc(s_fen, 1)
                nfen[0] += 1
                v.wait_ge(s_fen, nfen[0])
                v.bn_aggr(mv[:, p * 2:(p + 1) * 2], stats[:, p * 48:(p + 1) * 48]).then_inc(s_bn, 1)
                v.wait_ge(s_rs, i + 1)
                if i >= 2:
                    v.wait_ge(s_st[p], 16 * nst * (i // 2))
                r = v.tensor_scalar(xnt[:, p * D:(p + 1) * D], xin[:, p * D:(p + 1) * D],
                                    tiny[:, q + 2:q + 3], tiny[:, q + 0:q + 1],
                                    ALU.add, ALU.mult)
                if with_rms:
                    r = v.tensor_scalar(ht[:, p * D:(p + 1) * D], xin[:, p * D:(p + 1) * D],
                                        tiny[:, q + 2:q + 3], tiny[:, q + 1:q + 2],
                                        ALU.add, ALU.mult)
                r.then_inc(s_aff, naff)

        @block.scalar
        def _(s):
            for i in range(ntiles):
                p = i % 2
                q = p * 8
                s.wait_ge(s_bn, i + 1)
                s.activation(tiny[:, q + 5:q + 6], mv[:, p * 2 + 1: p * 2 + 2], AF.Ln, bias=cst[:, 0:1])
                s.activation(tiny[:, q + 0:q + 1], tiny[:, q + 5:q + 6], AF.Exp, scale=-0.5)
                s.activation(tiny[:, q + 2:q + 3], mv[:, p * 2: p * 2 + 1], AF.Identity, scale=-1.0)
                if with_rms:
                    s.activation(tiny[:, q + 5:q + 6], mv[:, p * 2 + 1: p * 2 + 2],
                                 AF.Ln, scale=1.0 + EPS, bias=cst[:, 1:2])
                    s.activation(tiny[:, q + 1:q + 2], tiny[:, q + 5:q + 6], AF.Exp, scale=-0.5)
                s.copy(tiny[:, q + 7:q + 8], tiny[:, q + 0:q + 1]).then_inc(s_rs, 1)

        @block.gpsimd
        def _(g):
            for i in range(ntiles):
                p = i % 2
                g.wait_ge(s_aff, naff * (i + 1))
                g.dma_start(out=xn[i * 128:(i + 1) * 128, :],
                            in_=xnt[:, p * D:(p + 1) * D]).then_inc(s_st[p], 16)
                if with_rms:
                    g.dma_start(out=h[i * 128:(i + 1) * 128, :],
                                in_=ht[:, p * D:(p + 1) * D]).then_inc(s_st[p], 16)
                    g.dma_start(out=xs[i * 128:(i + 1) * 128, :],
                                in_=xin[:, p * D:(p + 1) * D]).then_inc(s_st[p], 16)
            g.wait_ge(s_st0, 16 * nst * ((ntiles + 1) // 2))
            g.wait_ge(s_st1, 16 * nst * (ntiles // 2))
    return nc



def build_l0_launch(T: int = TC, debug=False):
    assert T % 512 == 0
    TB = T // 512
    nc = bass.Bass("TRN2", target_bir_lowering=False)
    if debug:
        dbm = nc.dram_tensor("dbm", [128, 128], F32, kind="ExternalOutput").ap()
        dws = nc.dram_tensor("dws", [32, 128, D], BF16, kind="ExternalOutput").ap()
    xn0f = nc.dram_tensor("xn0f", [D, T], BF16, kind="ExternalInput").ap()
    Wnk = nc.dram_tensor("Wnk", [D, D], F32, kind="ExternalInput").ap()
    Qnk = nc.dram_tensor("Qnk", [D, D], I32, kind="ExternalInput").ap()
    maskA = nc.dram_tensor("maskA", [D, D], I32, kind="ExternalInput").ap()
    szr = nc.dram_tensor("szr", [128, 32 * 32 * 2], F32, kind="ExternalInput").ap()
    Ebm = nc.dram_tensor("Ebm", [128, 32 * 128], BF16, kind="ExternalInput").ap()
    Pbm = nc.dram_tensor("Pbm", [128, 32 * 128], BF16, kind="ExternalInput").ap()
    identb = nc.dram_tensor("identb", [128, 128], BF16, kind="ExternalInput").ap()
    o0f = nc.dram_tensor("o0f", [D, T], F32, kind="ExternalOutput").ap()

    with ExitStack() as ctx:
        xsb = ctx.enter_context(nc.sbuf_tensor("xsb", [128, 32 * T], BF16))
        wbf = ctx.enter_context(nc.sbuf_tensor("wbf", [128, 2 * D], BF16))
        qbf = ctx.enter_context(nc.sbuf_tensor("qbf", [128, 2 * D], BF16))
        deq = ctx.enter_context(nc.sbuf_tensor("deq", [128, D], BF16))
        wsum = ctx.enter_context(nc.sbuf_tensor("wsum", [128, D], BF16))
        wT = ctx.enter_context(nc.sbuf_tensor("wT", [128, 2 * D], BF16))
        szc = ctx.enter_context(nc.sbuf_tensor("szc", [128, 2 * 64], F32))
        zpc = ctx.enter_context(nc.sbuf_tensor("zpc", [128, 2 * 32], F32))
        esb = ctx.enter_context(nc.sbuf_tensor("esb", [128, 2 * 128], BF16))
        idb = ctx.enter_context(nc.sbuf_tensor("idb", [128, 128], BF16))
        mred = ctx.enter_context(nc.sbuf_tensor("mred", [128, 2 * 128], BF16))
        bm = ctx.enter_context(nc.sbuf_tensor("bm", [128, 128], BF16))
        psl = ctx.enter_context(nc.sbuf_tensor("psl", [128, 128], BF16))
        osb = ctx.enter_context(nc.sbuf_tensor("osb", [128, T], F32))
        psA = ctx.enter_context(nc.psum_tensor("psA", [128, 512], F32))
        psB = ctx.enter_context(nc.psum_tensor("psB", [128, 512], F32))
        psC = ctx.enter_context(nc.psum_tensor("psC", [128, 512], F32))
        psD = ctx.enter_context(nc.psum_tensor("psD", [128, 512], F32))
        psM = ctx.enter_context(nc.psum_tensor("psM", [128, 512], F32))
        psT0 = ctx.enter_context(nc.psum_tensor("psT0", [128, 1024], BF16))
        psT1 = ctx.enter_context(nc.psum_tensor("psT1", [128, 1024], BF16))
        s_cld = ctx.enter_context(nc.semaphore("s_cld"))
        s_fen = ctx.enter_context(nc.semaphore("s_fen"))
        s_mld0 = ctx.enter_context(nc.semaphore("s_mld0"))
        s_mld1 = ctx.enter_context(nc.semaphore("s_mld1"))
        s_mrd = ctx.enter_context(nc.semaphore("s_mrd"))
        s_msum = ctx.enter_context(nc.semaphore("s_msum"))
        s_bm = ctx.enter_context(nc.semaphore("s_bm"))
        s_pld = ctx.enter_context(nc.semaphore("s_pld"))
        s_eld0 = ctx.enter_context(nc.semaphore("s_eld0"))
        s_eld1 = ctx.enter_context(nc.semaphore("s_eld1"))
        s_szl0 = ctx.enter_context(nc.semaphore("s_szl0"))
        s_szl1 = ctx.enter_context(nc.semaphore("s_szl1"))
        s_wld0 = ctx.enter_context(nc.semaphore("s_wld0"))
        s_wld1 = ctx.enter_context(nc.semaphore("s_wld1"))
        s_mx = ctx.enter_context(nc.semaphore("s_mx"))
        s_wsum = ctx.enter_context(nc.semaphore("s_wsum"))
        s_tp = ctx.enter_context(nc.semaphore("s_tp"))
        s_tx = ctx.enter_context(nc.semaphore("s_tx"))
        s_mm = ctx.enter_context(nc.semaphore("s_mm"))
        s_ev = ctx.enter_context(nc.semaphore("s_ev"))
        s_st = ctx.enter_context(nc.semaphore("s_st"))
        if debug:
            dbmb = ctx.enter_context(nc.sbuf_tensor("dbmb", [128, 128], F32))
            s_dbw = ctx.enter_context(nc.semaphore("s_dbw"))
            s_dbg = ctx.enter_context(nc.semaphore("s_dbg"))
        block = ctx.enter_context(nc.Block())
        psT = [psT0, psT1]
        psMM = [psA, psB, psC, psD]
        s_mld = [s_mld0, s_mld1]
        s_eld = [s_eld0, s_eld1]
        s_szl = [s_szl0, s_szl1]
        s_wld = [s_wld0, s_wld1]

        @block.sync
        def _(sync):
            sync.dma_start(out=idb[:], in_=identb[:]).then_inc(s_cld, 16)
            for kc in range(32):
                sync.dma_start(out=xsb[:, kc * T:(kc + 1) * T],
                               in_=xn0f[kc * 128:(kc + 1) * 128, :]).then_inc(s_cld, 16)
            for t in range(32):
                pe = t % 2
                if t >= 2:
                    sync.wait_ge(s_msum, t - 1)
                sync.dma_start(out=esb[:, pe * 128:(pe + 1) * 128],
                               in_=Ebm[:, t * 128:(t + 1) * 128]).then_inc(s_eld[pe], 16)
            if debug:
                sync.wait_ge(s_dbw, 1)
                sync.dma_start(out=dbm[:], in_=dbmb[:]).then_inc(s_st, 16)
            for ncn in range(32):
                if debug:
                    sync.wait_ge(s_wsum, ncn + 1)
                    sync.dma_start(out=dws[ncn], in_=wsum[:]).then_inc(s_dbg, 16)
                sync.wait_ge(s_ev, ncn + 1)
                sync.dma_start(out=o0f[ncn * 128:(ncn + 1) * 128, :],
                               in_=osb[:]).then_inc(s_st, 16)
            sync.wait_ge(s_st, 16 * (32 + 1 if debug else 32))

        @block.gpsimd
        def _(g):
            for t in range(32):
                p = t % 2
                if t >= 2:
                    g.wait_ge(s_mrd, t - 1)
                g.dma_start(out=wbf[:, p * D:(p + 1) * D],
                            in_=maskA[t * 128:(t + 1) * 128, :]).then_inc(s_mld[p], 16)
            for ncn in range(32):
                p = ncn % 2
                if ncn >= 1:
                    g.wait_ge(s_mx, ncn)
                g.dma_start(out=psl[:], in_=Pbm[:, ncn * 128:(ncn + 1) * 128]).then_inc(s_pld, 16)
                if ncn >= 2:
                    g.wait_ge(s_wsum, ncn - 1)
                g.dma_start(out=szc[:, p * 64:(p + 1) * 64],
                            in_=szr[:, ncn * 64:(ncn + 1) * 64]).then_inc(s_szl[p], 16)
                if ncn == 0:
                    g.wait_ge(s_bm, 1)
                if ncn >= 2:
                    g.wait_ge(s_wsum, ncn - 1)
                g.dma_start(out=wbf[:, p * D:(p + 1) * D],
                            in_=Wnk[ncn * 128:(ncn + 1) * 128, :]).then_inc(s_wld[p], 16)
                g.dma_start(out=qbf[:, p * D:(p + 1) * D],
                            in_=Qnk[ncn * 128:(ncn + 1) * 128, :]).then_inc(s_wld[p], 16)

        @block.vector
        def _(v):
            nfen = [0]

            def fence(r):
                r.then_inc(s_fen, 1)
                nfen[0] += 1
                v.wait_ge(s_fen, nfen[0])
            for t in range(32):
                p = t % 2
                v.wait_ge(s_mld[p], 16 * (t // 2 + 1))
                if t >= 2:
                    v.wait_ge(s_msum, t - 1)
                with nc.allow_low_precision(reason="mask 0/1 sums<=32 exact in bf16"):
                    v.reduce_sum(mred[:, p * 128:(p + 1) * 128],
                                 wbf[:, p * D:(p + 1) * D].rearrange("p (a b) -> p a b", b=32),
                                 axis=mybir.AxisListType.X).then_inc(s_mrd, 1)
            v.wait_ge(s_msum, 32)
            v.tensor_scalar(bm[:], psM[:, 0:128], 0.0, None, ALU.is_gt).then_inc(s_bm, 1)
            if debug:
                v.tensor_copy(dbmb[:], bm[:]).then_inc(s_dbw, 1)
            for ncn in range(32):
                p = ncn % 2
                v.wait_ge(s_wld[p], 32 * (ncn // 2 + 1))
                v.wait_ge(s_szl[p], 16 * (ncn // 2 + 1))
                szc3 = szc[:, p * 64:(p + 1) * 64].rearrange("p (g c) -> p g c", c=2)
                fence(v.scalar_tensor_tensor(zpc[:, p * 32:(p + 1) * 32], szc3[:, :, 0], -8.0,
                                             szc3[:, :, 1], ALU.mult, ALU.add))
                for gix in range(32):
                    v.tensor_scalar(deq[:, gix * 128:(gix + 1) * 128],
                                    qbf[:, p * D + gix * 128: p * D + (gix + 1) * 128],
                                    szc3[:, gix:gix + 1, 0],
                                    zpc[:, p * 32 + gix: p * 32 + gix + 1],
                                    ALU.mult, ALU.add)
                v.wait_ge(s_mx, ncn + 1)
                wmv = qbf[:, p * D:(p + 1) * D]
                fence(v.tensor_tensor(wmv.rearrange("p (a b) -> p a b", b=32),
                                      wbf[:, p * D:(p + 1) * D].rearrange("p (a b) -> p a b", b=32),
                                      psM[:, 0:128].unsqueeze(2).broadcast_to([128, 128, 32]),
                                      ALU.mult))
                if debug and ncn >= 1:
                    v.wait_ge(s_dbg, 16 * ncn)
                v.tensor_tensor(wsum[:], wmv, deq[:], ALU.add).then_inc(s_wsum, 1)
                v.wait_ge(s_mm, ncn + 1)
                if ncn >= 1:
                    v.wait_ge(s_st, 16 * ncn)
                for tb in range(TB):
                    r = v.tensor_copy(osb[:, tb * 512:(tb + 1) * 512], psMM[tb % 4][:])
                r.then_inc(s_ev, 1)

        @block.scalar
        def _(s):
            for ncn in range(32):
                pw = ncn % 2
                for kc in range(32):
                    s.wait_ge(s_tp, ncn * 32 + kc + 1)
                    r = s.copy(wT[:, pw * D + kc * 128: pw * D + (kc + 1) * 128],
                               psT[kc % 2][:, 0:128])
                    r.then_inc(s_tx, 1)

        @block.tensor
        def _(t):
            pass
            for tt in range(32):
                t.wait_ge(s_mrd, tt + 1)
                t.wait_ge(s_eld[tt % 2], 16 * (tt // 2 + 1))
                t.matmul(psM[:, 0:128], esb[:, (tt % 2) * 128:(tt % 2 + 1) * 128],
                         mred[:, (tt % 2) * 128:(tt % 2 + 1) * 128],
                         start=(tt == 0), stop=(tt == 31)).then_inc(s_msum, 1)
            t.wait_ge(s_cld, 16 * 33)
            for ncn in range(32):
                pw = ncn % 2
                t.wait_ge(s_bm, 1)
                t.wait_ge(s_pld, 16 * (ncn + 1))
                if ncn >= 1:
                    t.wait_ge(s_wsum, ncn)
                t.matmul(psM[:, 0:128], psl[:], bm[:], start=True, stop=True).then_inc(s_mx, 1)
                t.wait_ge(s_wsum, ncn + 1)
                for kc in range(32):
                    if ncn * 32 + kc >= 2:
                        t.wait_ge(s_tx, ncn * 32 + kc - 1)
                    t.transpose(psT[kc % 2][:, 0:128],
                                wsum[:, kc * 128:(kc + 1) * 128], idb[:]).then_inc(s_tp, 1)
                t.wait_ge(s_tx, 32 * (ncn + 1))
                if ncn >= 1:
                    t.wait_ge(s_ev, ncn)
                for kc in range(32):
                    for tb in range(TB):
                        r = t.matmul(psMM[tb % 4][:],
                                     wT[:, pw * D + kc * 128: pw * D + (kc + 1) * 128],
                                     xsb[:, kc * T + tb * 512: kc * T + (tb + 1) * 512],
                                     start=(kc == 0), stop=(kc == 31))
                r.then_inc(s_mm, 1)
    return nc



def _t(a):
    return np.ascontiguousarray(a.T)


def _make_l0_consts():
    E = np.zeros((32, 128, 128), dtype=BF16NP)
    P = np.zeros((32, 128, 128), dtype=BF16NP)
    for t in range(32):
        for n in range(128):
            E[t, n, 4 * t + n // 32] = 1
            P[t, 4 * t + n // 32, n] = 1
    Ebm = np.ascontiguousarray(E.transpose(1, 0, 2).reshape(128, 32 * 128))
    Pbm = np.ascontiguousarray(P.transpose(1, 0, 2).reshape(128, 32 * 128))
    identb = np.eye(128, dtype=BF16NP)
    return Ebm, Pbm, identb


def _stage_szr(scales_and_zeros):
    sz = np.asarray(scales_and_zeros, dtype=np.float32)
    out = sz.transpose(1, 0, 2).reshape(32, 128, 32, 2).transpose(1, 0, 2, 3)
    return np.ascontiguousarray(out.reshape(128, 32 * 32 * 2))


def run_ln0(x0_shards):
    nc = build_ln_launch(with_rms=False)
    in_maps = [{"xa": s} for s in x0_shards]
    res = run_bass_kernel_spmd(nc, in_maps, core_ids=list(range(NCORES)))
    return [r["xn"] for r in res.results]


def run_l0(xn0f_shards, Wnk, Qnk, maskA, szr, Ebm, Pbm, identb):
    nc = build_l0_launch()
    in_maps = [{"xn0f": s, "Wnk": Wnk, "Qnk": Qnk, "maskA": maskA, "szr": szr,
                "Ebm": Ebm, "Pbm": Pbm, "identb": identb} for s in xn0f_shards]
    res = run_bass_kernel_spmd(nc, in_maps, core_ids=list(range(NCORES)))
    return [r["o0f"] for r in res.results]


def run_ln1(x0_shards, o0_shards):
    nc = build_ln_launch(with_rms=True)
    in_maps = [{"xa": a, "xb": b} for a, b in zip(x0_shards, o0_shards)]
    res = run_bass_kernel_spmd(nc, in_maps, core_ids=list(range(NCORES)))
    return ([r["xs"] for r in res.results], [r["xn"] for r in res.results],
            [r["h"] for r in res.results])






def build_proj_launch(T: int = TC, TH: int = 2176):
    NB = 145
    nc = bass.Bass("TRN2", target_bir_lowering=False)
    hf = nc.dram_tensor("hf", [D, TH], BF16, kind="ExternalInput").ap()
    WinT = nc.dram_tensor("WinT", [128, NB, 32, 128], F32, kind="ExternalInput").ap()
    nwf = nc.dram_tensor("nwf", [128, 32], F32, kind="ExternalInput").ap()
    convw = nc.dram_tensor("convw", [128, 80 * 4], F32, kind="ExternalInput").ap()
    convb = nc.dram_tensor("convb", [128, 80], F32, kind="ExternalInput").ap()
    dtb = nc.dram_tensor("dtb", [128, 1], F32, kind="ExternalInput").ap()
    sg = nc.dram_tensor("sg", [64, 128, T], BF16, kind="ExternalOutput").ap()
    hid = nc.dram_tensor("hid", [64, 128, T], F32, kind="ExternalOutput").ap()
    Bc = nc.dram_tensor("Bc", [8, 128, T], F32, kind="ExternalOutput").ap()
    Cc = nc.dram_tensor("Cc", [8, 128, T], F32, kind="ExternalOutput").ap()
    dto = nc.dram_tensor("dto", [128, T], F32, kind="ExternalOutput").ap()

    MAIN = [(1, 128, 512), (2, 640, 512), (3, 1152, 512), (4, 1664, 512)]
    HALO = (0, 0, 128)

    with ExitStack() as ctx:
        hsb = ctx.enter_context(nc.sbuf_tensor("hsb", [128, 32 * TH], BF16))
        wsb = ctx.enter_context(nc.sbuf_tensor("wsb", [128, 2 * 32 * 128], BF16))
        tmpf = ctx.enter_context(nc.sbuf_tensor("tmpf", [128, TH], BF16))
        cvin = ctx.enter_context(nc.sbuf_tensor("cvin", [128, TH], F32))
        acc0 = ctx.enter_context(nc.sbuf_tensor("acc0", [128, T], F32))
        acc1 = ctx.enter_context(nc.sbuf_tensor("acc1", [128, T], F32))
        sgb = ctx.enter_context(nc.sbuf_tensor("sgb", [128, T], BF16))
        hcb = ctx.enter_context(nc.sbuf_tensor("hcb", [128, T], F32))
        dts = ctx.enter_context(nc.sbuf_tensor("dts", [128, T], F32))
        nws = ctx.enter_context(nc.sbuf_tensor("nws", [128, 32], F32))
        cws = ctx.enter_context(nc.sbuf_tensor("cws", [128, 80 * 4], F32))
        cbs = ctx.enter_context(nc.sbuf_tensor("cbs", [128, 80], F32))
        dtbs = ctx.enter_context(nc.sbuf_tensor("dtbs", [128, 1], F32))
        scs = ctx.enter_context(nc.sbuf_tensor("scs", [128, 2], F32))
        vcs = ctx.enter_context(nc.sbuf_tensor("vcs", [128, 2], F32))
        psH = ctx.enter_context(nc.psum_tensor("psH", [128, 512], F32))
        ps1 = ctx.enter_context(nc.psum_tensor("ps1", [128, 512], F32))
        ps2 = ctx.enter_context(nc.psum_tensor("ps2", [128, 512], F32))
        ps3 = ctx.enter_context(nc.psum_tensor("ps3", [128, 512], F32))
        ps4 = ctx.enter_context(nc.psum_tensor("ps4", [128, 512], F32))
        s_cld = ctx.enter_context(nc.semaphore("s_cld"))
        s_h = ctx.enter_context(nc.semaphore("s_h"))
        s_fold = ctx.enter_context(nc.semaphore("s_fold"))
        s_w0 = ctx.enter_context(nc.semaphore("s_w0"))
        s_w1 = ctx.enter_context(nc.semaphore("s_w1"))
        s_mm = ctx.enter_context(nc.semaphore("s_mm"))
        s_ev = ctx.enter_context(nc.semaphore("s_ev"))
        s_cv = ctx.enter_context(nc.semaphore("s_cv"))
        s_sp = ctx.enter_context(nc.semaphore("s_sp"))
        s_sl = ctx.enter_context(nc.semaphore("s_sl"))
        s_fen = ctx.enter_context(nc.semaphore("s_fen"))
        s_fs = ctx.enter_context(nc.semaphore("s_fs"))
        s_sg = ctx.enter_context(nc.semaphore("s_sg"))
        s_hc = ctx.enter_context(nc.semaphore("s_hc"))
        s_dt = ctx.enter_context(nc.semaphore("s_dt"))
        s_stg = ctx.enter_context(nc.semaphore("s_stg"))
        s_sth = ctx.enter_context(nc.semaphore("s_sth"))
        s_std = ctx.enter_context(nc.semaphore("s_std"))
        block = ctx.enter_context(nc.Block())
        psMAIN = [None, ps1, ps2, ps3, ps4]
        s_w = [s_w0, s_w1]
        dtv = cvin

        def role(nb):
            if nb < 64:
                return "gate"
            if nb < 144:
                return "hbc"
            return "dt"

        @block.sync
        def _(sync):
            sync.dma_start(out=nws[:], in_=nwf[:]).then_inc(s_cld, 16)
            sync.dma_start(out=cws[:], in_=convw[:]).then_inc(s_cld, 16)
            sync.dma_start(out=cbs[:], in_=convb[:]).then_inc(s_cld, 16)
            sync.dma_start(out=dtbs[:], in_=dtb[:]).then_inc(s_cld, 16)
            for kc in range(32):
                sync.dma_start(out=hsb[:, kc * TH:(kc + 1) * TH],
                               in_=hf[kc * 128:(kc + 1) * 128, :]).then_inc(s_h, 16)
            isg = 0; ihc = 0
            for nb in range(NB):
                r = role(nb)
                if r == "gate":
                    isg += 1
                    sync.wait_ge(s_sg, isg)
                    sync.dma_start(out=sg[nb], in_=sgb[:]).then_inc(s_stg, 16)
                elif r == "hbc":
                    cc = nb - 64
                    ihc += 1
                    sync.wait_ge(s_hc, ihc)
                    dst = hid[cc] if cc < 64 else (Bc[cc - 64] if cc < 72 else Cc[cc - 72])
                    sync.dma_start(out=dst, in_=hcb[:]).then_inc(s_sth, 16)
                else:
                    sync.wait_ge(s_dt, 1)
                    sync.dma_start(out=dto[:], in_=cvin[:, 0:T]).then_inc(s_std, 16)
            sync.wait_ge(s_stg, 16 * 64)
            sync.wait_ge(s_sth, 16 * 80)
            sync.wait_ge(s_std, 16)

        @block.gpsimd
        def _(g):
            for nb in range(NB):
                p = nb % 2
                if nb >= 2:
                    g.wait_ge(s_mm, nb - 1)
                g.dma_start(out=wsb[:, p * 4096:(p + 1) * 4096].rearrange("p (a b) -> p a b", b=128),
                            in_=WinT[:, nb]).then_inc(s_w[p], 16)

        @block.vector
        def _(v):
            nfen = [0]

            def fence(r):
                r.then_inc(s_fen, 1)
                nfen[0] += 1
                v.wait_ge(s_fen, nfen[0])

            def fence_new():
                fence(v.tensor_copy(vcs[:, 0:1], vcs[:, 1:2]))

            v.wait_ge(s_h, 16 * 32)
            v.wait_ge(s_cld, 16 * 4)
            for kc in range(32):
                fence(v.tensor_scalar_mul(tmpf[:], hsb[:, kc * TH:(kc + 1) * TH],
                                          nws[:, kc:kc + 1]))
                fence(v.tensor_copy(hsb[:, kc * TH:(kc + 1) * TH], tmpf[:]))
            for nb in range(NB):
                r = role(nb)
                if r == "gate":
                    continue
                v.wait_ge(s_mm, nb + 1)
                if r == "hbc":
                    cc = nb - 64
                    v.tensor_copy(cvin[:, 0:128], psH[:, 0:128])
                    for (bk, off, sz) in MAIN:
                        rr = v.tensor_copy(cvin[:, off:off + sz], psMAIN[bk][:])
                    rr.then_inc(s_ev, 1)
                    fence_new()
                    if cc >= 1:
                        v.wait_ge(s_hc, cc)
                    fence(v.tensor_scalar_mul(acc0[:], cvin[:, 125:125 + T],
                                              cws[:, cc * 4:cc * 4 + 1]))
                    fence(v.scalar_tensor_tensor(acc1[:], cvin[:, 126:126 + T],
                                                 cws[:, cc * 4 + 1:cc * 4 + 2], acc0[:],
                                                 ALU.mult, ALU.add))
                    fence(v.scalar_tensor_tensor(acc0[:], cvin[:, 127:127 + T],
                                                 cws[:, cc * 4 + 2:cc * 4 + 3], acc1[:],
                                                 ALU.mult, ALU.add))
                    v.scalar_tensor_tensor(acc1[:], cvin[:, 128:128 + T],
                                           cws[:, cc * 4 + 3:cc * 4 + 4], acc0[:],
                                           ALU.mult, ALU.add).then_inc(s_cv, 1)
                else:
                    for (bk, off, sz) in MAIN:
                        j = (off - 128)
                        rr = v.tensor_scalar_add(dtv[:, j:j + sz], psMAIN[bk][:], dtbs[:])
                    rr.then_inc(s_ev, 1)
                    fence_new()
                    v.tensor_copy(vcs[:, 0:1], vcs[:, 1:2]).then_inc(s_cv, 1)
                    v.wait_ge(s_sp, 1)
                    fence(v.tensor_relu(dts[:], dtv[:, 0:T]))
                    v.tensor_tensor(dtv[:, 0:T], dts[:], acc0[:], ALU.add).then_inc(s_dt, 1)

        @block.scalar
        def _(s):
            nfs = [0]

            def fens(r):
                r.then_inc(s_fs, 1)
                nfs[0] += 1
                s.wait_ge(s_fs, nfs[0])

            isg = 0; icv = 0
            for nb in range(NB):
                r = role(nb)
                if r == "gate":
                    s.wait_ge(s_mm, nb + 1)
                    if isg >= 1:
                        s.wait_ge(s_stg, 16 * isg)
                    for (bk, off, sz) in MAIN:
                        j = off - 128
                        rr = s.activation(sgb[:, j:j + sz], psMAIN[bk][:], AF.Silu)
                    rr.then_inc(s_ev, 1)
                    s.copy(scs[:, 0:1], scs[:, 1:2]).then_inc(s_sg, 1)
                    isg += 1
                elif r == "hbc":
                    cc = nb - 64
                    icv += 1
                    s.wait_ge(s_cv, icv)
                    if cc >= 1:
                        s.wait_ge(s_sth, 16 * cc)
                    for j in range(0, T, 512):
                        rr = s.activation(hcb[:, j:j + 512], acc1[:, j:j + 512], AF.Silu,
                                          bias=cbs[:, cc:cc + 1])
                    s.copy(scs[:, 0:1], scs[:, 1:2]).then_inc(s_hc, 1)
                else:
                    s.wait_ge(s_cv, icv + 1)
                    for j in range(0, T, 512):
                        rr = s.activation(dts[:, j:j + 512], cvin[:, j:j + 512], AF.Abs)
                    fens(rr)
                    for j in range(0, T, 512):
                        rr = s.activation(acc1[:, j:j + 512], dts[:, j:j + 512], AF.Exp, scale=-1.0)
                    fens(rr)
                    for j in range(0, T, 512):
                        s.activation(acc0[:, j:j + 512], acc1[:, j:j + 512], AF.Ln, bias=1.0)
                    s.copy(scs[:, 0:1], scs[:, 1:2]).then_inc(s_sp, 1)

        @block.tensor
        def _(t):
            t.wait_ge(s_fen, 64)
            for nb in range(NB):
                p = nb % 2
                r = role(nb)
                t.wait_ge(s_w[p], 16 * (nb // 2 + 1))
                if nb >= 1:
                    t.wait_ge(s_ev, nb)
                blocks = ([HALO] + MAIN) if r == "hbc" else MAIN
                for kc in range(32):
                    lhsT = wsb[:, p * 4096 + kc * 128: p * 4096 + (kc + 1) * 128]
                    for (bk, off, sz) in blocks:
                        ps = psH if bk == 0 else psMAIN[bk]
                        rr = t.matmul(ps[:, 0:sz], lhsT,
                                      hsb[:, kc * TH + off: kc * TH + off + sz],
                                      start=(kc == 0), stop=(kc == 31))
                rr.then_inc(s_mm, 1)
    return nc


def _stage_proj_w(w):
    W4 = np.asarray(w, np.float32).reshape(145, 128, 32, 128)
    return np.ascontiguousarray(W4.transpose(3, 0, 2, 1))


def _stage_conv(conv_w, conv_b, norm_w, dt_bias):
    cw = np.ascontiguousarray(np.asarray(conv_w, np.float32).reshape(80, 128, 4)
                              .transpose(1, 0, 2).reshape(128, 320))
    cb = np.ascontiguousarray(np.asarray(conv_b, np.float32).reshape(80, 128).T)
    nw = np.ascontiguousarray(np.asarray(norm_w, np.float32).reshape(32, 128).T)
    db = np.ascontiguousarray(np.asarray(dt_bias, np.float32).reshape(128, 1))
    return cw, cb, nw, db


def run_proj(hf_shards, WinT, nwf, convw, convb, dtb):
    nc = build_proj_launch()
    in_maps = [{"hf": s, "WinT": WinT, "nwf": nwf, "convw": convw,
                "convb": convb, "dtb": dtb} for s in hf_shards]
    res = run_bass_kernel_spmd(nc, in_maps, core_ids=list(range(len(hf_shards))))
    return res.results



def build_ssd_launch(NU: int = 4, NCH: int = 32):
    SQ = NCH * 128
    nc = bass.Bass("TRN2", target_bir_lowering=False)
    hid4 = nc.dram_tensor("hid4", [NU, 8, 128, SQ], F32, kind="ExternalInput").ap()
    sg4 = nc.dram_tensor("sg4", [NU, 8, 128, SQ], BF16, kind="ExternalInput").ap()
    B4 = nc.dram_tensor("B4", [NU, 128, SQ], F32, kind="ExternalInput").ap()
    C4 = nc.dram_tensor("C4", [NU, 128, SQ], F32, kind="ExternalInput").ap()
    dt4 = nc.dram_tensor("dt4", [NU, 16, SQ], F32, kind="ExternalInput").ap()
    alog = nc.dram_tensor("alog", [NU, 16, 1], F32, kind="ExternalInput").ap()
    dpb = nc.dram_tensor("dpb", [128, NU * 16], F32, kind="ExternalInput").ap()
    idf = nc.dram_tensor("idf", [128, 128], F32, kind="ExternalInput").ap()
    idbm = nc.dram_tensor("idbm", [128, 128], BF16, kind="ExternalInput").ap()
    triu = nc.dram_tensor("triu", [128, 128], F32, kind="ExternalInput").ap()
    ones16 = nc.dram_tensor("ones16", [16, 128], F32, kind="ExternalInput").ap()
    i16 = nc.dram_tensor("i16", [16, 16], F32, kind="ExternalInput").ap()
    idb64 = nc.dram_tensor("idb64", [128, 64], BF16, kind="ExternalInput").ap()
    negrow = nc.dram_tensor("negrow", [2, 2048], F32, kind="ExternalInput").ap()
    y4 = nc.dram_tensor("y4", [NU, NCH, 128, 1024], BF16, kind="ExternalOutput").ap()

    NC_TOT = NU * NCH

    def ne_of(u, c):
        return u * (NCH - 1) + c

    with ExitStack() as ctx:
        hidc = ctx.enter_context(nc.sbuf_tensor("hidc", [128, 2 * 1024], BF16))
        sgc = ctx.enter_context(nc.sbuf_tensor("sgc", [128, 2 * 1024], BF16))
        sgt = ctx.enter_context(nc.sbuf_tensor("sgt", [128, 1024], BF16))
        Bb = ctx.enter_context(nc.sbuf_tensor("Bb", [128, SQ], BF16))
        Cb = ctx.enter_context(nc.sbuf_tensor("Cb", [128, SQ], BF16))
        dtu = ctx.enter_context(nc.sbuf_tensor("dtu", [16, SQ], F32))
        alg = ctx.enter_context(nc.sbuf_tensor("alg", [16, 1], F32))
        nega = ctx.enter_context(nc.sbuf_tensor("nega", [16, 1], F32))
        acv = ctx.enter_context(nc.sbuf_tensor("acv", [16, 128], F32))
        acs = ctx.enter_context(nc.sbuf_tensor("acs", [16, 128], F32))
        ons16 = ctx.enter_context(nc.sbuf_tensor("ons16", [16, 128], F32))
        i16s = ctx.enter_context(nc.sbuf_tensor("i16s", [16, 16], F32))
        diag = ctx.enter_context(nc.sbuf_tensor("diag", [16, 16], F32))
        ell = ctx.enter_context(nc.sbuf_tensor("ell", [16, 2], F32))
        dop = ctx.enter_context(nc.sbuf_tensor("dop", [2, 2 * 2048], F32))
        dtT = ctx.enter_context(nc.sbuf_tensor("dtT", [128, 16], F32))
        acsT = ctx.enter_context(nc.sbuf_tensor("acsT", [128, 16], F32))
        EA = ctx.enter_context(nc.sbuf_tensor("EA", [128, 16], F32))
        Ebc = ctx.enter_context(nc.sbuf_tensor("Ebc", [128, 16], F32))
        Msk = ctx.enter_context(nc.sbuf_tensor("Msk", [128, 128], F32))
        BTs = ctx.enter_context(nc.sbuf_tensor("BTs", [128, 128], BF16))
        LTm = ctx.enter_context(nc.sbuf_tensor("LTm", [128, 2 * 128], F32))
        LTe = ctx.enter_context(nc.sbuf_tensor("LTe", [128, 2 * 128], F32))
        Ylh = ctx.enter_context(nc.sbuf_tensor("Ylh", [128, 2 * 128], BF16))
        ddv = ctx.enter_context(nc.sbuf_tensor("ddv", [128, 2], F32))
        hdtp = ctx.enter_context(nc.sbuf_tensor("hdtp", [128, 1024], BF16))
        hcpp = ctx.enter_context(nc.sbuf_tensor("hcpp", [128, 1024], BF16))
        srhp = ctx.enter_context(nc.sbuf_tensor("srhp", [128, 1024], BF16))
        prevb = ctx.enter_context(nc.sbuf_tensor("prevb", [128, 2 * 1024], F32))
        prevbf = ctx.enter_context(nc.sbuf_tensor("prevbf", [128, 1024], BF16))
        stb = ctx.enter_context(nc.sbuf_tensor("stb", [128, 2 * 1024], F32))
        ysb = ctx.enter_context(nc.sbuf_tensor("ysb", [128, 1024], F32))
        y2b = ctx.enter_context(nc.sbuf_tensor("y2b", [128, 1024], F32))
        ygb = ctx.enter_context(nc.sbuf_tensor("ygb", [128, 1024], F32))
        jnk = ctx.enter_context(nc.sbuf_tensor("jnk", [128, 1024], F32))
        ssq = ctx.enter_context(nc.sbuf_tensor("ssq", [128, 1], F32))
        rgs = ctx.enter_context(nc.sbuf_tensor("rgs", [128, 2], F32))
        ybf = ctx.enter_context(nc.sbuf_tensor("ybf", [128, 2 * 1024], BF16))
        dps = ctx.enter_context(nc.sbuf_tensor("dps", [128, NU * 16], F32))
        idfs = ctx.enter_context(nc.sbuf_tensor("idfs", [128, 128], F32))
        idbs = ctx.enter_context(nc.sbuf_tensor("idbs", [128, 128], BF16))
        trius = ctx.enter_context(nc.sbuf_tensor("trius", [128, 128], F32))
        idb64s = ctx.enter_context(nc.sbuf_tensor("idb64s", [128, 64], BF16))
        epsc = ctx.enter_context(nc.sbuf_tensor("epsc", [128, 1], F32))
        vcs = ctx.enter_context(nc.sbuf_tensor("vcs", [128, 2], F32))
        scs = ctx.enter_context(nc.sbuf_tensor("scs", [128, 2], F32))
        psY0 = ctx.enter_context(nc.psum_tensor("psY0", [128, 512], F32))
        psY1 = ctx.enter_context(nc.psum_tensor("psY1", [128, 512], F32))
        psDa = ctx.enter_context(nc.psum_tensor("psDa", [128, 512], F32))
        psDb = ctx.enter_context(nc.psum_tensor("psDb", [128, 512], F32))
        psTa = ctx.enter_context(nc.psum_tensor("psTa", [128, 1024], BF16))
        psTb = ctx.enter_context(nc.psum_tensor("psTb", [128, 1024], BF16))
        psSa = ctx.enter_context(nc.psum_tensor("psSa", [128, 512], F32))
        psSb = ctx.enter_context(nc.psum_tensor("psSb", [128, 512], F32))
        sems = {}
        for nm in ["cld", "hl0", "hl1", "sl0", "sl1", "ucl", "acs", "el", "dgv",
                   "dtt", "act", "ebm", "ebe", "prv", "bt", "btv", "mt", "msk",
                   "sgte", "dmm", "ltm", "lte", "ylh", "ht", "hde", "yd",
                   "sm", "ste", "yo", "yev", "ya", "yg", "sq", "ybo", "yst0", "yst1",
                   "fen", "fes", "dml"]:
            sems[nm] = ctx.enter_context(nc.semaphore("s_" + nm))
        block = ctx.enter_context(nc.Block())
        S = sems
        s_hl = [S["hl0"], S["hl1"]]
        s_sl = [S["sl0"], S["sl1"]]
        psD = [psDa, psDb]
        psT_ = [psTa, psTb]
        psY = [psY0, psY1]
        psS_ = [psSa, psSb]
        hid3 = hidc[:].rearrange("p (b r t) -> p b r t", b=2, r=8)

        @block.sync
        def _(sync):
            for ap_, sb in [(idf, idfs), (idbm, idbs), (triu, trius), (ones16, ons16),
                            (i16, i16s), (dpb, dps), (idb64, idb64s)]:
                sync.dma_start(out=sb[:], in_=ap_[:]).then_inc(S["cld"], 16)
            for ci in range(NC_TOT):
                u, c = divmod(ci, NCH)
                pb = ci % 2
                sync.wait_ge(S["ybo"], ci + 1)
                sync.dma_start(out=y4[u, c], in_=ybf[:, pb * 1024:(pb + 1) * 1024]).then_inc(S["yst0" if pb == 0 else "yst1"], 16)
            sync.wait_ge(S["yst0"], 16 * ((NC_TOT + 1) // 2))
            sync.wait_ge(S["yst1"], 16 * (NC_TOT // 2))

        @block.gpsimd
        def _(g):
            for u in range(NU):
                if u >= 1:
                    g.wait_ge(S["ybo"], u * NCH)
                if u == 0:
                    g.dma_start(out=dop[0:2, 2048:4096], in_=negrow[:]).then_inc(S["dml"], 16)
                g.dma_start(out=Bb[:], in_=B4[u]).then_inc(S["ucl"], 16)
                g.dma_start(out=Cb[:], in_=C4[u]).then_inc(S["ucl"], 16)
                g.dma_start(out=dtu[:], in_=dt4[u]).then_inc(S["ucl"], 16)
                g.dma_start(out=alg[:], in_=alog[u]).then_inc(S["ucl"], 16)
                for c in range(NCH):
                    ci = u * NCH + c
                    p = ci % 2
                    if ci >= 2:
                        g.wait_ge(S["ht"], 16 * (ci - 1))
                        g.wait_ge(S["sgte"], ci - 1)
                    g.dma_start(out=hid3[:, p], in_=hid4[u, :, :, c * 128:(c + 1) * 128]
                                .rearrange("r p t -> p r t")).then_inc(s_hl[p], 16)
                    g.dma_start(out=sgc[:, p * 1024:(p + 1) * 1024].rearrange("p (r t) -> p r t", r=8),
                                in_=sg4[u, :, :, c * 128:(c + 1) * 128].rearrange("r p t -> p r t")
                                ).then_inc(s_sl[p], 16)
                    g.wait_ge(S["acs"], ci + 1)
                    if ci >= 1:
                        g.wait_ge(S["dmm"], 16 * ci)
                    g.dma_start(out=dop[1:2, 0:2048], in_=acs[:, :]).then_inc(S["dml"], 16)
                    g.dma_start(out=dop[0:1, 2048:4096], in_=acs[:, :]).then_inc(S["dml"], 16)

        @block.vector
        def _(v):
            nfen = [0]

            def fence(r):
                r.then_inc(S["fen"], 1)
                nfen[0] += 1
                v.wait_ge(S["fen"], nfen[0])

            v.wait_ge(S["cld"], 16 * 7)
            v.memset(epsc[:], EPS)
            v.memset(dop[0:1, 0:2048], 1.0)
            fence(v.memset(vcs[:], 0.0))
            for u in range(NU):
                v.wait_ge(S["ucl"], 16 * 4 * (u + 1))
                v.wait_ge(S["el"], u * (NCH + 1) + 1)
                fence(v.tensor_scalar_mul(nega[:], ell[:, 1:2], -1.0))
                v.memset(prevb[:, 0:1024], 0.0)
                fence(v.memset(prevb[:, 1024:2048], 0.0))
                for c in range(NCH):
                    ci = u * NCH + c
                    cp = c % 2
                    ne = ne_of(u, c)
                    if ci >= 1:
                        v.wait_ge(S["dml"], 16 + 32 * ci)
                        v.wait_ge(S["bt"], 2 * (ci - 1) + 1)
                        v.wait_ge(S["el"], u * (NCH + 1) + c + 1)
                    fence(v.tensor_scalar_mul(acv[:], dtu[:, c * 128:(c + 1) * 128], nega[:]))
                    v.tensor_tensor_scan(acs[:], ons16[:], acv[:], 0.0,
                                         ALU.mult, ALU.add).then_inc(S["acs"], 1)
                    v.wait_ge(S["mt"], ci + 1)
                    v.tensor_tensor(Msk[:], psSa[:, 0:128], trius[:], ALU.mult).then_inc(S["msk"], 1)
                    v.wait_ge(S["bt"], 2 * (ci + 1))
                    v.tensor_copy(BTs[:], psTa[:, 0:128]).then_inc(S["btv"], 1)
                    v.tensor_copy(sgt[:], psTb[:]).then_inc(S["sgte"], 1)
                    if c >= 1:
                        v.tensor_scalar_mul(diag[:], i16s[:], ell[:, 0:1]).then_inc(S["dgv"], 1)
                        v.wait_ge(S["ebe"], ne)
                        for h in range(16):
                            o = h * 64
                            r = v.scalar_tensor_tensor(
                                prevb[:, cp * 1024 + o: cp * 1024 + o + 64],
                                prevb[:, (1 - cp) * 1024 + o: (1 - cp) * 1024 + o + 64],
                                Ebc[:, h:h + 1],
                                stb[:, (1 - cp) * 1024 + o: (1 - cp) * 1024 + o + 64],
                                ALU.mult, ALU.add)
                        fence(r)
                        v.tensor_copy(prevbf[:], prevb[:, cp * 1024:(cp + 1) * 1024]).then_inc(S["prv"], 1)
                    v.wait_ge(S["dtt"], ci + 1)
                    for h in range(16):
                        hp = h % 2
                        v.wait_ge(S["dmm"], ci * 16 + h + 1)
                        v.tensor_scalar_min(LTm[:, hp * 128:(hp + 1) * 128],
                                            psD[hp][:, 0:128], 0.0).then_inc(S["ltm"], 1)
                        v.wait_ge(S["lte"], ci * 16 + h + 1)
                        fence(v.tensor_tensor(ddv[:, hp:hp + 1],
                                              LTe[:, hp * 128 + 127: hp * 128 + 128],
                                              dtT[:, h:h + 1], ALU.mult))
                        v.tensor_tensor(Ylh[:, hp * 128:(hp + 1) * 128], Msk[:],
                                        LTe[:, hp * 128:(hp + 1) * 128], ALU.mult)
                        v.wait_ge(S["ht"], 16 * ci + h + 1)
                        r = v.tensor_scalar_mul(srhp[:, h * 64:(h + 1) * 64],
                                                psT_[hp][:, 0:64], ddv[:, hp:hp + 1])
                        fence(r)
                        v.tensor_copy(vcs[:, 0:1], vcs[:, 1:2]).then_inc(S["ylh"], 1)
                    v.wait_ge(S["sm"], 2 * (ci + 1))
                    v.tensor_copy(stb[:, cp * 1024: cp * 1024 + 512], psSa[:])
                    fence(v.tensor_copy(stb[:, cp * 1024 + 512: cp * 1024 + 1024], psSb[:]))
                    v.tensor_copy(vcs[:, 0:1], vcs[:, 1:2]).then_inc(S["ste"], 1)
                    v.wait_ge(S["yd"], ci + 1)
                    v.tensor_copy(ysb[:, 0:512], psY0[:])
                    fence(v.tensor_copy(ysb[:, 512:1024], psY1[:]))
                    v.tensor_copy(vcs[:, 0:1], vcs[:, 1:2]).then_inc(S["yev"], 1)
                    if c >= 1:
                        v.wait_ge(S["yo"], 2 * ne)
                        v.wait_ge(S["act"], ci + 1)
                        for h in range(16):
                            o = h * 64
                            ps = psS_[h // 8]
                            r = v.scalar_tensor_tensor(ysb[:, o:o + 64],
                                                       ps[:, (h % 8) * 64:(h % 8) * 64 + 64],
                                                       EA[:, h:h + 1], ysb[:, o:o + 64],
                                                       ALU.mult, ALU.add)
                        fence(r)
                        v.tensor_copy(vcs[:, 0:1], vcs[:, 1:2]).then_inc(S["ya"], 1)
                    v.wait_ge(S["hde"], 16 * (ci + 1))
                    for h in range(16):
                        o = h * 64
                        r = v.scalar_tensor_tensor(y2b[:, o:o + 64], hcpp[:, o:o + 64],
                                                   dps[:, u * 16 + h: u * 16 + h + 1],
                                                   ysb[:, o:o + 64], ALU.mult, ALU.add)
                    fence(r)
                    v.wait_ge(S["sgte"], ci + 1)
                    fence(v.tensor_tensor(ygb[:], y2b[:], sgt[:], ALU.mult))
                    v.tensor_copy(vcs[:, 0:1], vcs[:, 1:2]).then_inc(S["yg"], 1)
                    v.wait_ge(S["sq"], ci + 1)
                    pb = ci % 2
                    if ci >= 2:
                        v.wait_ge(S["yst0" if pb == 0 else "yst1"], 16 * (ci // 2))
                    v.tensor_scalar_mul(ybf[:, pb * 1024:(pb + 1) * 1024], ygb[:],
                                        rgs[:, 0:1]).then_inc(S["ybo"], 1)

        @block.scalar
        def _(s):
            nfes = [0]

            def fens(r):
                r.then_inc(S["fes"], 1)
                nfes[0] += 1
                s.wait_ge(S["fes"], nfes[0])

            for u in range(NU):
                s.wait_ge(S["ucl"], 16 * 4 * (u + 1))
                s.activation(ell[:, 1:2], alg[:], AF.Exp).then_inc(S["el"], 1)
                for c in range(NCH):
                    ci = u * NCH + c
                    ne = ne_of(u, c)
                    s.wait_ge(S["bt"], 2 * ci + 1)
                    s.copy(dtT[:], psDa[:, 256:272])
                    fens(s.copy(acsT[:], psDa[:, 272:288]))
                    s.copy(scs[:, 0:1], scs[:, 1:2]).then_inc(S["dtt"], 1)
                    s.activation(EA[:], acsT[:], AF.Exp).then_inc(S["act"], 1)
                    if c >= 1:
                        s.wait_ge(S["dgv"], ne)
                    s.activation(ell[:, 0:1], acs[:, 127:128], AF.Exp).then_inc(S["el"], 1)
                    if c >= 1:
                        s.wait_ge(S["ebm"], ne)
                        s.copy(Ebc[:], psDa[:, 288:304]).then_inc(S["ebe"], 1)
                    for h in range(16):
                        hp = h % 2
                        s.wait_ge(S["ltm"], ci * 16 + h + 1)
                        s.activation(LTe[:, hp * 128:(hp + 1) * 128],
                                     LTm[:, hp * 128:(hp + 1) * 128], AF.Exp).then_inc(S["lte"], 1)
                        s.wait_ge(S["ht"], 16 * ci + h + 1)
                        s.wait_ge(S["dtt"], ci + 1)
                        s.activation(hdtp[:, h * 64:(h + 1) * 64], psT_[hp][:, 0:64],
                                     AF.Identity, scale=dtT[:, h:h + 1])
                        s.copy(hcpp[:, h * 64:(h + 1) * 64], psT_[hp][:, 0:64]).then_inc(S["hde"], 1)
                    s.wait_ge(S["yg"], ci + 1)
                    s.activation(jnk[:], ygb[:], AF.Square, accum_out=ssq[:])
                    fens(s.activation(rgs[:, 1:2], ssq[:], AF.Ln, scale=1.0 / 1024.0, bias=epsc[:]))
                    s.activation(rgs[:, 0:1], rgs[:, 1:2], AF.Exp, scale=-0.5).then_inc(S["sq"], 1)

        @block.tensor
        def _(t):
            t.wait_ge(S["cld"], 16 * 7)
            for u in range(NU):
                t.wait_ge(S["ucl"], 16 * 4 * (u + 1))
                for c in range(NCH):
                    ci = u * NCH + c
                    p = ci % 2
                    cp = c % 2
                    ne = ne_of(u, c)
                    t.wait_ge(S["acs"], ci + 1)
                    if ci >= 1:
                        t.wait_ge(S["ltm"], 16 * ci)
                        t.wait_ge(S["dtt"], ci)
                        if ci >= NCH or (ci % NCH) >= 1 or True:
                            pass
                    t.transpose(psDa[:, 256:272], dtu[:, c * 128:(c + 1) * 128], idfs[0:16, 0:16])
                    t.transpose(psDa[:, 272:288], acs[:], idfs[0:16, 0:16]).then_inc(S["bt"], 1)
                    if ci >= 1:
                        t.wait_ge(S["ste"], ci)
                        t.wait_ge(S["msk"], ci)
                        if (ci % NCH) != 0:
                            t.wait_ge(S["ya"], ne_of(*divmod(ci - 1, NCH)))
                        elif u >= 1:
                            t.wait_ge(S["ya"], ne_of(u - 1, NCH - 1))
                    t.matmul(psSa[:, 0:128], Bb[:, c * 128:(c + 1) * 128],
                             Cb[:, c * 128:(c + 1) * 128], start=True, stop=True).then_inc(S["mt"], 1)
                    t.wait_ge(s_hl[p], 16 * (ci // 2 + 1))
                    t.wait_ge(s_sl[p], 16 * (ci // 2 + 1))
                    if ci >= 1:
                        t.wait_ge(S["hde"], 16 * ci)
                        t.wait_ge(S["ylh"], 16 * ci)
                        t.wait_ge(S["btv"], ci)
                        t.wait_ge(S["sgte"], ci)
                    t.transpose(psTa[:, 0:128], Bb[:, c * 128:(c + 1) * 128], idbs[:])
                    for r8 in range(8):
                        rr = t.transpose(psTb[:, r8 * 128:(r8 + 1) * 128],
                                         sgc[:, p * 1024 + r8 * 128: p * 1024 + (r8 + 1) * 128],
                                         idbs[:])
                    rr.then_inc(S["bt"], 1)
                    if c >= 1:
                        t.wait_ge(S["dgv"], ne)
                        if ne >= 2:
                            t.wait_ge(S["ebe"], ne - 1)
                            t.wait_ge(S["prv"], ne - 1)
                        t.matmul(psDa[:, 288:304], ons16[:], diag[:],
                                 start=True, stop=True).then_inc(S["ebm"], 1)
                    t.wait_ge(S["dml"], 16 + 32 * (ci + 1))
                    t.wait_ge(S["dtt"], ci + 1)
                    if c >= 1:
                        t.wait_ge(S["ebe"], ne)
                    for h in range(16):
                        hp = h % 2
                        if ci * 16 + h >= 2:
                            t.wait_ge(S["ltm"], ci * 16 + h - 1)
                        t.matmul(psD[hp][:, 0:128], dop[:, h * 128:(h + 1) * 128],
                                 dop[:, 2048 + h * 128: 2048 + (h + 1) * 128],
                                 start=True, stop=True).then_inc(S["dmm"], 1)
                        if ci * 16 + h >= 2:
                            t.wait_ge(S["hde"], ci * 16 + h - 1)
                            t.wait_ge(S["ylh"], ci * 16 + h - 1)
                        if h == 0:
                            t.wait_ge(S["btv"], ci + 1)
                        if h == 1:
                            t.wait_ge(S["sgte"], ci + 1)
                        t.transpose(psT_[hp][:, 0:64],
                                    hid3[(h % 2) * 64:(h % 2) * 64 + 64, p, h // 2, :],
                                    idb64s[(h % 2) * 64:(h % 2) * 64 + 64, :]).then_inc(S["ht"], 1)
                        t.wait_ge(S["ylh"], ci * 16 + h + 1)
                        t.wait_ge(S["hde"], ci * 16 + h + 1)
                        if h == 0:
                            t.wait_ge(S["yev"], ci)
                        r = t.matmul(psY[h // 8][:, (h % 8) * 64:(h % 8) * 64 + 64],
                                     Ylh[:, hp * 128:(hp + 1) * 128],
                                     hdtp[:, h * 64:(h + 1) * 64],
                                     start=True, stop=True)
                        if h == 15:
                            r.then_inc(S["yd"], 1)
                    t.wait_ge(S["ylh"], 16 * (ci + 1))
                    t.wait_ge(S["btv"], ci + 1)
                    t.wait_ge(S["msk"], ci + 1)
                    t.matmul(psSa[:], BTs[:], srhp[:, 0:512], start=True, stop=True)
                    t.matmul(psSb[:], BTs[:], srhp[:, 512:1024], start=True, stop=True).then_inc(S["sm"], 2)
                    if c >= 1:
                        t.wait_ge(S["ste"], ci + 1)
                        t.wait_ge(S["prv"], ne)
                        t.matmul(psSa[:], Cb[:, c * 128:(c + 1) * 128],
                                 prevbf[:, 0:512], start=True, stop=True)
                        t.matmul(psSb[:], Cb[:, c * 128:(c + 1) * 128],
                                 prevbf[:, 512:1024],
                                 start=True, stop=True).then_inc(S["yo"], 2)
    return nc



def build_final_launch(T: int = TC):
    TH = T // 2
    nc = bass.Bass("TRN2", target_bir_lowering=False)
    ygf = nc.dram_tensor("ygf", [8192, T], BF16, kind="ExternalInput").ap()
    xn1f = nc.dram_tensor("xn1f", [D, T], BF16, kind="ExternalInput").ap()
    x1f = nc.dram_tensor("x1f", [D, T], F32, kind="ExternalInput").ap()
    OutWT = nc.dram_tensor("OutWT", [128, 32, 64, 128], F32, kind="ExternalInput").ap()
    A1 = nc.dram_tensor("A1", [D, D], I32, kind="ExternalInput").ap()
    ws1 = nc.dram_tensor("ws1", [128, 32], F32, kind="ExternalInput").ap()
    wz1 = nc.dram_tensor("wz1", [128, 32], F32, kind="ExternalInput").ap()
    bias1f = nc.dram_tensor("bias1f", [128, 32], F32, kind="ExternalInput").ap()
    gnwf = nc.dram_tensor("gnwf", [128, 64], F32, kind="ExternalInput").ap()
    identb = nc.dram_tensor("identb", [128, 128], BF16, kind="ExternalInput").ap()
    outf = nc.dram_tensor("outf", [D, T], F32, kind="ExternalOutput").ap()
    w1scr = nc.dram_tensor("w1scr", [128, 32, 32, 128], BF16).ap()
    oacc = nc.dram_tensor("oacc", [D, T], F32).ap()

    with ExitStack() as ctx:
        bigb = ctx.enter_context(nc.sbuf_tensor("bigb", [128, 64 * TH], BF16))
        ygh = bigb[:]
        xnh = bigb[:, 0:32 * TH]
        qvb = bigb[:].bitcast(I32)
        wsb = ctx.enter_context(nc.sbuf_tensor("wsb", [128, 2 * 64 * 128], BF16))
        w1sb = ctx.enter_context(nc.sbuf_tensor("w1sb", [128, 2 * 32 * 128], BF16))
        ws1s = ctx.enter_context(nc.sbuf_tensor("ws1s", [128, 32], F32))
        wz1s = ctx.enter_context(nc.sbuf_tensor("wz1s", [128, 32], F32))
        zs1 = ctx.enter_context(nc.sbuf_tensor("zs1", [128, 32], F32))
        b1s = ctx.enter_context(nc.sbuf_tensor("b1s", [128, 32], F32))
        gnws = ctx.enter_context(nc.sbuf_tensor("gnws", [128, 64], F32))
        idb = ctx.enter_context(nc.sbuf_tensor("idb", [128, 128], BF16))
        tmpb = ctx.enter_context(nc.sbuf_tensor("tmpb", [128, TH], BF16))
        oa = ctx.enter_context(nc.sbuf_tensor("oa", [128, 2 * TH], F32))
        ob = ctx.enter_context(nc.sbuf_tensor("ob", [128, 2 * TH], F32))
        dq = oa[:].bitcast(BF16)[:, 0:D]
        w1T = ob[:].bitcast(BF16)[:, 0:D]
        xs = ctx.enter_context(nc.sbuf_tensor("xs", [128, 2 * TH], F32))
        vcs = ctx.enter_context(nc.sbuf_tensor("vcs", [128, 2], F32))
        psA0 = ctx.enter_context(nc.psum_tensor("psA0", [128, 512], F32))
        psA1 = ctx.enter_context(nc.psum_tensor("psA1", [128, 512], F32))
        psB0 = ctx.enter_context(nc.psum_tensor("psB0", [128, 512], F32))
        psB1 = ctx.enter_context(nc.psum_tensor("psB1", [128, 512], F32))
        psT0 = ctx.enter_context(nc.psum_tensor("psT0", [128, 1024], BF16))
        psT1 = ctx.enter_context(nc.psum_tensor("psT1", [128, 1024], BF16))
        sems = {}
        for nm in ["cld", "ql0", "ql1", "dqv", "tp", "tx", "w1st", "ygl", "gnf",
                   "wl0", "wl1", "mma", "eva", "oast", "xnl", "w1l0", "w1l1",
                   "x1l0", "x1l1", "oal0", "oal1", "mmf", "evf", "ost", "fen"]:
            sems[nm] = ctx.enter_context(nc.semaphore("s_" + nm))
        block = ctx.enter_context(nc.Block())
        S = sems
        psT = [psT0, psT1]
        s_ql = [S["ql0"], S["ql1"]]
        s_wl = [S["wl0"], S["wl1"]]
        s_w1l = [S["w1l0"], S["w1l1"]]
        s_x1l = [S["x1l0"], S["x1l1"]]
        s_oal = [S["oal0"], S["oal1"]]

        @block.sync
        def _(sync):
            for ap_, sb in [(ws1, ws1s), (wz1, wz1s), (bias1f, b1s), (gnwf, gnws),
                            (identb, idb)]:
                sync.dma_start(out=sb[:], in_=ap_[:]).then_inc(S["cld"], 16)
            for ncn in range(32):
                sync.wait_ge(S["tx"], 32 * (ncn + 1))
                sync.dma_start(out=w1scr[:, ncn], in_=w1T[:].rearrange("p (a b) -> p a b", b=128)
                               ).then_inc(S["w1st"], 16)
            for th in range(2):
                if th == 1:
                    sync.wait_ge(S["mma"], 32)
                for kc in range(64):
                    sync.dma_start(out=ygh[:, kc * TH:(kc + 1) * TH],
                                   in_=ygf[kc * 128:(kc + 1) * 128, th * TH:(th + 1) * TH]
                                   ).then_inc(S["ygl"], 16)
                for nb in range(32):
                    i = th * 32 + nb
                    sync.wait_ge(S["eva"], i + 1)
                    sync.dma_start(out=oacc[nb * 128:(nb + 1) * 128, th * TH:(th + 1) * TH],
                                   in_=oa[:, (i % 2) * TH:(i % 2 + 1) * TH]).then_inc(S["oast"], 16)
            for th in range(2):
                if th == 0:
                    sync.wait_ge(S["mma"], 64)
                else:
                    sync.wait_ge(S["mmf"], 32)
                for kc in range(32):
                    sync.dma_start(out=xnh[:, kc * TH:(kc + 1) * TH],
                                   in_=xn1f[kc * 128:(kc + 1) * 128, th * TH:(th + 1) * TH]
                                   ).then_inc(S["xnl"], 16)
                for nb in range(32):
                    i = th * 32 + nb
                    p = i % 2
                    if i >= 2:
                        sync.wait_ge(S["evf"], i - 1)
                    sync.dma_start(out=xs[:, p * TH:(p + 1) * TH],
                                   in_=x1f[nb * 128:(nb + 1) * 128, th * TH:(th + 1) * TH]
                                   ).then_inc(s_x1l[p], 16)
                    sync.wait_ge(S["oast"], 16 * (th * 32 + nb + 1) if th == 0 else 16 * 64)
                    sync.dma_start(out=ob[:, p * TH:(p + 1) * TH],
                                   in_=oacc[nb * 128:(nb + 1) * 128, th * TH:(th + 1) * TH]
                                   ).then_inc(s_oal[p], 16)
                    sync.wait_ge(S["evf"], i + 1)
                    sync.dma_start(out=outf[nb * 128:(nb + 1) * 128, th * TH:(th + 1) * TH],
                                   in_=oa[:, (i % 2) * TH:(i % 2 + 1) * TH]).then_inc(S["ost"], 16)
            sync.wait_ge(S["ost"], 16 * 64)

        @block.gpsimd
        def _(g):
            for ncn in range(32):
                p = ncn % 2
                if ncn >= 2:
                    g.wait_ge(S["dqv"], ncn - 1)
                g.dma_start(out=qvb[:, p * D:(p + 1) * D],
                            in_=A1[ncn * 128:(ncn + 1) * 128, :]).then_inc(s_ql[p], 16)
            for th in range(2):
                for nb in range(32):
                    i = th * 32 + nb
                    p = i % 2
                    if i >= 2:
                        g.wait_ge(S["mma"], i - 1)
                    g.dma_start(out=wsb[:, p * 8192:(p + 1) * 8192].rearrange("p (a b) -> p a b", b=128),
                                in_=OutWT[:, nb]).then_inc(s_wl[p], 16)
            for th in range(2):
                for nb in range(32):
                    i = th * 32 + nb
                    p = i % 2
                    if i == 0:
                        g.wait_ge(S["w1st"], 16 * 32)
                    if i >= 2:
                        g.wait_ge(S["mmf"], i - 1)
                    g.dma_start(out=w1sb[:, p * 4096:(p + 1) * 4096].rearrange("p (a b) -> p a b", b=128),
                                in_=w1scr[:, nb]).then_inc(s_w1l[p], 16)

        @block.vector
        def _(v):
            nfen = [0]

            def fence(r):
                r.then_inc(S["fen"], 1)
                nfen[0] += 1
                v.wait_ge(S["fen"], nfen[0])

            v.wait_ge(S["cld"], 16 * 5)
            fence(v.tensor_tensor(zs1[:], wz1s[:], ws1s[:], ALU.mult))
            fence(v.tensor_scalar_mul(zs1[:], zs1[:], -1.0))
            for ncn in range(32):
                p = ncn % 2
                v.wait_ge(s_ql[p], 16 * (ncn // 2 + 1))
                if ncn >= 1:
                    v.wait_ge(S["tp"], 32 * ncn)
                q3 = qvb[:, p * D:(p + 1) * D].rearrange("p (a b) -> p a b", b=4)
                fence(v.tensor_single_scalar(q3[:, :, 1], q3[:, :, 1], 6,
                                             ALU.logical_shift_right))
                fence(v.tensor_single_scalar(qvb[:, p * D:(p + 1) * D],
                                             qvb[:, p * D:(p + 1) * D], 63, ALU.bitwise_and))
                for gx in range(4):
                    r = v.tensor_scalar(dq[:, gx * 1024:(gx + 1) * 1024],
                                        qvb[:, p * D + gx * 1024: p * D + (gx + 1) * 1024],
                                        ws1s[:, ncn:ncn + 1], zs1[:, ncn:ncn + 1],
                                        ALU.mult, ALU.add)
                r.then_inc(S["dqv"], 1)
            for th in range(2):
                v.wait_ge(S["ygl"], 16 * 64 * (th + 1))
                for kc in range(64):
                    fence(v.tensor_scalar_mul(tmpb[:], ygh[:, kc * TH:(kc + 1) * TH],
                                              gnws[:, kc:kc + 1]))
                    fence(v.tensor_copy(ygh[:, kc * TH:(kc + 1) * TH], tmpb[:]))
                v.tensor_copy(vcs[:, 0:1], vcs[:, 1:2]).then_inc(S["gnf"], 1)
                for nb in range(32):
                    i = th * 32 + nb
                    v.wait_ge(S["mma"], i + 1)
                    if i >= 2:
                        v.wait_ge(S["oast"], 16 * (i - 1))
                    v.tensor_copy(oa[:, (i % 2) * TH: (i % 2) * TH + 512], psA0[:])
                    r = v.tensor_copy(oa[:, (i % 2) * TH + 512:(i % 2) * TH + 1024], psA1[:])
                    r.then_inc(S["eva"], 1)
            for th in range(2):
                for nb in range(32):
                    i = th * 32 + nb
                    p = i % 2
                    v.wait_ge(S["mmf"], i + 1)
                    v.wait_ge(s_oal[p], 16 * (i // 2 + 1))
                    v.wait_ge(s_x1l[p], 16 * (i // 2 + 1))
                    if i < 2:
                        v.wait_ge(S["oast"], 16 * (63 + i))
                    if i >= 2:
                        v.wait_ge(S["ost"], 16 * (i - 1))
                    for bk, ps in [(0, psB0), (1, psB1)]:
                        o0 = p * TH + bk * 512
                        fence(v.tensor_scalar_add(oa[:, o0:o0 + 512], ps[:],
                                                  b1s[:, nb:nb + 1]))
                    fence(v.tensor_tensor(oa[:, p * TH:(p + 1) * TH], oa[:, p * TH:(p + 1) * TH],
                                          ob[:, p * TH:(p + 1) * TH], ALU.add))
                    fence(v.tensor_tensor(oa[:, p * TH:(p + 1) * TH], oa[:, p * TH:(p + 1) * TH],
                                          xnh[:, nb * TH:(nb + 1) * TH], ALU.add))
                    r = v.tensor_tensor(oa[:, p * TH:(p + 1) * TH], oa[:, p * TH:(p + 1) * TH],
                                        xs[:, p * TH:(p + 1) * TH], ALU.add)
                    r.then_inc(S["evf"], 1)

        @block.scalar
        def _(s):
            for ncn in range(32):
                for kc in range(32):
                    s.wait_ge(S["tp"], ncn * 32 + kc + 1)
                    s.copy(w1T[:, kc * 128:(kc + 1) * 128],
                           psT[kc % 2][:, 0:128]).then_inc(S["tx"], 1)

        @block.tensor
        def _(t):
            t.wait_ge(S["cld"], 16 * 5)
            for ncn in range(32):
                t.wait_ge(S["dqv"], ncn + 1)
                if ncn >= 1:
                    t.wait_ge(S["w1st"], 16 * ncn)
                for kc in range(32):
                    if ncn * 32 + kc >= 2:
                        t.wait_ge(S["tx"], ncn * 32 + kc - 1)
                    t.transpose(psT[kc % 2][:, 0:128], dq[:, kc * 128:(kc + 1) * 128],
                                idb[:]).then_inc(S["tp"], 1)
            for th in range(2):
                t.wait_ge(S["gnf"], th + 1)
                for nb in range(32):
                    i = th * 32 + nb
                    p = i % 2
                    t.wait_ge(s_wl[p], 16 * (i // 2 + 1))
                    if i >= 1:
                        t.wait_ge(S["eva"], i)
                    for kc in range(64):
                        lhsT = wsb[:, p * 8192 + kc * 128: p * 8192 + (kc + 1) * 128]
                        t.matmul(psA0[:], lhsT, ygh[:, kc * TH: kc * TH + 512],
                                 start=(kc == 0), stop=(kc == 63))
                        r = t.matmul(psA1[:], lhsT, ygh[:, kc * TH + 512: kc * TH + 1024],
                                     start=(kc == 0), stop=(kc == 63))
                    r.then_inc(S["mma"], 1)
            for th in range(2):
                t.wait_ge(S["xnl"], 16 * 32 * (th + 1))
                for nb in range(32):
                    i = th * 32 + nb
                    p = i % 2
                    t.wait_ge(s_w1l[p], 16 * (i // 2 + 1))
                    if i >= 1:
                        t.wait_ge(S["evf"], i)
                    for kc in range(32):
                        lhsT = w1sb[:, p * 4096 + kc * 128: p * 4096 + (kc + 1) * 128]
                        t.matmul(psB0[:], lhsT, xnh[:, kc * TH: kc * TH + 512],
                                 start=(kc == 0), stop=(kc == 31))
                        r = t.matmul(psB1[:], lhsT, xnh[:, kc * TH + 512: kc * TH + 1024],
                                     start=(kc == 0), stop=(kc == 31))
                    r.then_inc(S["mmf"], 1)
    return nc


def _stage_final(out_proj_w, qweight, weight_scale, weight_zero_point, bias, gnorm_w):
    W4 = np.asarray(out_proj_w, np.float32).T.reshape(64, 128, 32, 128)
    OutWT = np.ascontiguousarray(W4.transpose(1, 2, 0, 3))
    q = np.asarray(qweight)
    cols = np.empty(4096, np.int64)
    for k in range(4096):
        g, j = divmod(k, 4)
        cols[k] = 3 * g + (0 if j == 0 else (0 if j == 1 else (1 if j == 2 else 2)))
    A1 = np.ascontiguousarray(q[:, cols]).astype(np.int32)
    ws1 = np.ascontiguousarray(np.asarray(weight_scale, np.float32).reshape(32, 128).T)
    wz1 = np.ascontiguousarray(np.asarray(weight_zero_point, np.float32).reshape(32, 128).T)
    b1 = np.ascontiguousarray(np.asarray(bias, np.float32).reshape(32, 128).T)
    gnw = np.ascontiguousarray(np.asarray(gnorm_w, np.float32).reshape(64, 128).T)
    return OutWT, A1, ws1, wz1, b1, gnw


def run_final(yg_shards, xn1_shards, x1_shards, OutWT, A1, ws1, wz1, b1, gnw, identb):
    nc = build_final_launch()
    in_maps = [{"ygf": a, "xn1f": b, "x1f": c, "OutWT": OutWT, "A1": A1,
                "ws1": ws1, "wz1": wz1, "bias1f": b1, "gnwf": gnw, "identb": identb}
               for a, b, c in zip(yg_shards, xn1_shards, x1_shards)]
    res = run_bass_kernel_spmd(nc, in_maps, core_ids=list(range(len(in_maps))))
    return [r["outf"] for r in res.results]



def kernel(**inputs):
    return kernel_fused(**inputs)


def kernel_unfused(L0_attn_data, L0_attn_weight, L0_attn_mask, L0_ffn_weight_q,
           L0_ffn_scales_and_zeros, L1_attn_norm_w, L1_attn_in_proj_w,
           L1_attn_conv_w, L1_attn_conv_b, L1_attn_dt_bias, L1_attn_A_log,
           L1_attn_Dp, L1_attn_gnorm_w, L1_attn_out_proj_w, L1_ffn_qweight,
           L1_ffn_weight_scale, L1_ffn_weight_zero_point, L1_ffn_bias):
    x0 = np.ascontiguousarray(np.asarray(L0_attn_data, np.float32).reshape(16384, D))
    x0_shards = [np.ascontiguousarray(x0[c * TC:(c + 1) * TC]) for c in range(NCORES)]

    xn0_tm = run_ln0(x0_shards)

    Ebm, Pbm, identb = _make_l0_consts()
    szr = _stage_szr(L0_ffn_scales_and_zeros)
    o0 = run_l0([_t(s) for s in xn0_tm], np.asarray(L0_attn_weight, np.float32),
                np.asarray(L0_ffn_weight_q, np.int32),
                np.asarray(L0_attn_mask, np.int32), szr, Ebm, Pbm, identb)

    x1_tm, xn1_tm, h_tm = run_ln1(x0_shards, [_t(s) for s in o0])

    h_full = np.concatenate(h_tm, axis=0)
    hf_shards = []
    for c in range(NCORES):
        b, hb = divmod(c, 2)
        base = b * S + hb * TC
        if hb == 0:
            halo = np.zeros((128, D), dtype=BF16NP)
        else:
            halo = h_full[base - 128: base]
        hf_shards.append(_t(np.concatenate([halo, h_full[base: base + TC]], axis=0)))
    WinT = _stage_proj_w(L1_attn_in_proj_w)
    cw, cb, nw, db = _stage_conv(L1_attn_conv_w, L1_attn_conv_b, L1_attn_norm_w,
                                 L1_attn_dt_bias)
    dres = run_proj(hf_shards, WinT, nw, cw, cb, db)

    A_log = np.asarray(L1_attn_A_log, np.float32)
    Dp = np.asarray(L1_attn_Dp, np.float32)
    hid_b, sg_b, B_b, C_b, dt_b = [], [], [], [], []
    for b in range(B_SZ):
        hid_b.append(np.concatenate([dres[2 * b]["hid"], dres[2 * b + 1]["hid"]], axis=2))
        sg_b.append(np.concatenate([dres[2 * b]["sg"], dres[2 * b + 1]["sg"]], axis=2))
        B_b.append(np.concatenate([dres[2 * b]["Bc"], dres[2 * b + 1]["Bc"]], axis=2))
        C_b.append(np.concatenate([dres[2 * b]["Cc"], dres[2 * b + 1]["Cc"]], axis=2))
        dt_b.append(np.concatenate([dres[2 * b]["dto"], dres[2 * b + 1]["dto"]], axis=1))
    units = [(b, g) for b in range(B_SZ) for g in range(8)]
    idf = np.eye(128, dtype=np.float32)
    idb128 = np.eye(128, dtype=BF16NP)
    triu = np.triu(np.ones((128, 128), np.float32))
    ones16 = np.ones((16, 128), np.float32)
    i16 = np.eye(16, dtype=np.float32)
    idb64 = np.vstack([np.eye(64, dtype=BF16NP)] * 2)
    negrow = np.vstack([np.zeros((1, 2048), np.float32), -np.ones((1, 2048), np.float32)])
    in_maps = []
    for c in range(NCORES):
        us = units[4 * c: 4 * c + 4]
        ins = {
            "hid4": np.stack([hid_b[b][8 * g: 8 * g + 8] for b, g in us]),
            "sg4": np.stack([sg_b[b][8 * g: 8 * g + 8] for b, g in us]),
            "B4": np.stack([B_b[b][g] for b, g in us]),
            "C4": np.stack([C_b[b][g] for b, g in us]),
            "dt4": np.stack([dt_b[b][16 * g: 16 * g + 16] for b, g in us]),
            "alog": np.stack([A_log[16 * g: 16 * g + 16][:, None] for b, g in us]),
            "dpb": np.concatenate([np.tile(Dp[16 * g: 16 * g + 16], (128, 1))
                                   for b, g in us], axis=1),
            "idf": idf, "idbm": idb128, "triu": triu, "ones16": ones16,
            "i16": i16, "idb64": idb64, "negrow": negrow,
        }
        in_maps.append(ins)
    nc_e = build_ssd_launch()
    res_e = run_bass_kernel_spmd(nc_e, in_maps, core_ids=list(range(NCORES)))
    y_full = np.empty((16384, 8192), dtype=BF16NP)
    for c in range(NCORES):
        y4 = res_e.results[c]["y4"]
        for j, (b, g) in enumerate(units[4 * c: 4 * c + 4]):
            y_full[b * S:(b + 1) * S, g * 1024:(g + 1) * 1024] = y4[j].reshape(S, 1024)

    OutWT, A1, ws1, wz1, b1f, gnwf = _stage_final(
        L1_attn_out_proj_w, L1_ffn_qweight, L1_ffn_weight_scale,
        L1_ffn_weight_zero_point, L1_ffn_bias, L1_attn_gnorm_w)
    outs = run_final([_t(y_full[c * TC:(c + 1) * TC]) for c in range(NCORES)],
                     [_t(s) for s in xn1_tm], [_t(s) for s in x1_tm],
                     OutWT, A1, ws1, wz1, b1f, gnwf, idb128)
    out = np.concatenate([o.T for o in outs], axis=0)
    return np.ascontiguousarray(out.reshape(B_SZ, S, D))



TPAD = 2176


class _Arena:
    def __init__(self, nc, ctx, name, nbytes):
        assert nbytes % 4 == 0
        self.t = ctx.enter_context(nc.sbuf_tensor(name, [128, nbytes // 4], I32))
        self.off = 0
        self.nbytes = nbytes

    def reset(self):
        self.off = 0

    def take(self, nelem, dtype):
        esz = {F32: 4, I32: 4, BF16: 2}[dtype]
        nb = nelem * esz
        nb_al = (nb + 63) & ~63
        assert self.off + nb_al <= self.nbytes, f"arena overflow {self.off}+{nb_al}>{self.nbytes}"
        if dtype == I32:
            v = self.t[:, self.off // 4: self.off // 4 + nelem]
        elif dtype == F32:
            v = self.t[:].bitcast(F32)[:, self.off // 4: self.off // 4 + nelem]
        else:
            v = self.t[:].bitcast(BF16)[:, self.off // 2: self.off // 2 + nelem]
        self.off += nb_al
        return v


def build_fused(dbg_stage=0):
    nc = bass.Bass("TRN2", target_bir_lowering=False)
    D_ = D
    x0s = nc.dram_tensor("x0s", [TPAD, D_], F32, kind="ExternalInput").ap()
    Wnk = nc.dram_tensor("Wnk", [D_, D_], F32, kind="ExternalInput").ap()
    Qnk = nc.dram_tensor("Qnk", [D_, D_], I32, kind="ExternalInput").ap()
    maskA = nc.dram_tensor("maskA", [D_, D_], I32, kind="ExternalInput").ap()
    szr = nc.dram_tensor("szr", [128, 2048], F32, kind="ExternalInput").ap()
    Ebm = nc.dram_tensor("Ebm", [128, 4096], BF16, kind="ExternalInput").ap()
    Pbm = nc.dram_tensor("Pbm", [128, 4096], BF16, kind="ExternalInput").ap()
    WinT = nc.dram_tensor("WinT", [128, 145, 32, 128], F32, kind="ExternalInput").ap()
    nwf = nc.dram_tensor("nwf", [128, 32], F32, kind="ExternalInput").ap()
    convw = nc.dram_tensor("convw", [128, 320], F32, kind="ExternalInput").ap()
    convb = nc.dram_tensor("convb", [128, 80], F32, kind="ExternalInput").ap()
    dtb = nc.dram_tensor("dtb", [128, 1], F32, kind="ExternalInput").ap()
    alog8 = nc.dram_tensor("alog8", [16, 8], F32, kind="ExternalInput").ap()
    dpb = nc.dram_tensor("dpb", [128, 128], F32, kind="ExternalInput").ap()
    OutWT = nc.dram_tensor("OutWT", [128, 32, 64, 128], F32, kind="ExternalInput").ap()
    A1 = nc.dram_tensor("A1", [D_, D_], I32, kind="ExternalInput").ap()
    ws1 = nc.dram_tensor("ws1", [128, 32], F32, kind="ExternalInput").ap()
    wz1 = nc.dram_tensor("wz1", [128, 32], F32, kind="ExternalInput").ap()
    bias1f = nc.dram_tensor("bias1f", [128, 32], F32, kind="ExternalInput").ap()
    gnwf = nc.dram_tensor("gnwf", [128, 64], F32, kind="ExternalInput").ap()
    identb = nc.dram_tensor("identb", [128, 128], BF16, kind="ExternalInput").ap()
    idf = nc.dram_tensor("idf", [128, 128], F32, kind="ExternalInput").ap()
    triu = nc.dram_tensor("triu", [128, 128], F32, kind="ExternalInput").ap()
    ones16 = nc.dram_tensor("ones16", [16, 128], F32, kind="ExternalInput").ap()
    i16 = nc.dram_tensor("i16", [16, 16], F32, kind="ExternalInput").ap()
    idb64 = nc.dram_tensor("idb64", [128, 64], BF16, kind="ExternalInput").ap()
    negrow = nc.dram_tensor("negrow", [2, 2048], F32, kind="ExternalInput").ap()
    smask = nc.dram_tensor("smask", [128, 1], F32, kind="ExternalInput").ap()
    sgidx = nc.dram_tensor("sgidx", [128, 8], I32, kind="ExternalInput").ap()
    out = nc.dram_tensor("out", [TC, D_], F32, kind="ExternalOutput").ap()
    o0f = nc.dram_tensor("o0f", [D_, TPAD], F32).ap()
    x1blk = nc.dram_tensor("x1blk", [17, 32, 128, 128], F32).ap()
    hblk = nc.dram_tensor("hblk", [17, 32, 128, 128], BF16).ap()
    xn1blk = nc.dram_tensor("xn1blk", [17, 32, 128, 128], BF16).ap()
    sgd = nc.dram_tensor("sgd", [64, 128, TC], BF16).ap()
    hidd = nc.dram_tensor("hidd", [64, 128, TPAD], F32).ap()
    Bcd = nc.dram_tensor("Bcd", [8, 128, TPAD], F32).ap()
    Ccd = nc.dram_tensor("Ccd", [8, 128, TPAD], F32).ap()
    dtod = nc.dram_tensor("dtod", [128, TPAD], F32).ap()
    ygfd = nc.dram_tensor("ygfd", [8192, TC], BF16).ap()
    w1scr = nc.dram_tensor("w1scr", [128, 32, 32, 128], BF16).ap()
    oaccd = nc.dram_tensor("oaccd", [D_, TC], F32).ap()
    if dbg_stage:
        dbgo = nc.dram_tensor("dbgo", [8192, 2048], F32, kind="ExternalOutput").ap()

    with ExitStack() as ctx:
        ar = _Arena(nc, ctx, "AR", 201600)
        idbs = ctx.enter_context(nc.sbuf_tensor("idbs", [128, 128], BF16))
        idfs = ctx.enter_context(nc.sbuf_tensor("idfs", [128, 128], F32))
        trius = ctx.enter_context(nc.sbuf_tensor("trius", [128, 128], F32))
        ons16 = ctx.enter_context(nc.sbuf_tensor("ons16", [16, 128], F32))
        i16s = ctx.enter_context(nc.sbuf_tensor("i16s", [16, 16], F32))
        idb64s = ctx.enter_context(nc.sbuf_tensor("idb64s", [128, 64], BF16))
        dps = ctx.enter_context(nc.sbuf_tensor("dps", [128, 128], F32))
        cst = ctx.enter_context(nc.sbuf_tensor("cst", [128, 4], F32))
        vcs = ctx.enter_context(nc.sbuf_tensor("vcs", [128, 2], F32))
        scs = ctx.enter_context(nc.sbuf_tensor("scs", [128, 2], F32))
        smsk = ctx.enter_context(nc.sbuf_tensor("smsk", [128, 1], F32))
        nwsp = ctx.enter_context(nc.sbuf_tensor("nwsp", [128, 32], F32))
        P0 = ctx.enter_context(nc.psum_tensor("P0", [128, 512], F32))
        P1 = ctx.enter_context(nc.psum_tensor("P1", [128, 512], F32))
        P2 = ctx.enter_context(nc.psum_tensor("P2", [128, 512], F32))
        P3 = ctx.enter_context(nc.psum_tensor("P3", [128, 512], F32))
        P4 = ctx.enter_context(nc.psum_tensor("P4", [128, 512], F32))
        P5 = ctx.enter_context(nc.psum_tensor("P5", [128, 512], F32))
        PT0 = ctx.enter_context(nc.psum_tensor("PT0", [128, 1024], BF16))
        PT1 = ctx.enter_context(nc.psum_tensor("PT1", [128, 1024], BF16))
        PF = [P0, P1, P2, P3, P4, P5]
        PT = [PT0, PT1]

        SM = {}

        def sem(name):
            SM[name] = ctx.enter_context(nc.semaphore("s_" + name))
            return SM[name]

        big = ar.take(69632, BF16)
        xsb = big
        hfm = big
        ar_mark = ar.off

        block = ctx.enter_context(nc.Block())

        for nm in ["cldA", "ldA0", "ldA1", "bnA", "rsA", "afA", "tpA", "txA",
                   "mldB0", "mldB1", "mrdB", "msumB", "bmB", "pldB", "eldB0",
                   "eldB1", "szlB0", "szlB1", "wldB0", "wldB1", "mxB", "wsB",
                   "tpB", "txB", "mmB", "evB", "stB", "fenV", "fesS",
                   "o0C0", "o0C1", "ldC0", "ldC1", "bnC", "rsC", "toC", "adC",
                   "afC", "tpC", "txC", "stC",
                   "wD0", "wD1", "mmD", "evD", "cvD", "sgD", "hcD", "spD",
                   "dtD", "stgD", "sthD", "stdD",
                   "uclE", "hlE0", "hlE1", "slE0", "slE1", "acsE", "elE",
                   "dmlE", "btE", "btvE", "dttE", "mtE", "mskE", "sgtE",
                   "dmmE", "ltmE", "lteE", "ylhE", "htE", "hdeE", "ydE",
                   "smE", "steE", "dgvE", "ebmE", "ebeE",
                   "yoE", "yevE", "yaE", "ygE", "sqE",
                   "ystE0", "actE", "pvE", "ugE", "corE",
                   "ybfE", "ytE", "ytxE", "ttC",
                   "qlF0", "qlF1", "yglF",
                   "pg"]:
            sem(nm)
        for fn, pn in [("dqF", "bnA"),
                       ("tpF", "tpA"), ("txF", "txA"), ("w1stF", "rsA"),
                       ("gnfF", "mrdB"), ("wlF0", "wldB0"),
                       ("wlF1", "wldB1"), ("mmaF", "mmB"), ("evaF", "evB"),
                       ("oastF", "stB"), ("xnlF", "msumB"), ("w1lF0", "szlB0"),
                       ("w1lF1", "szlB1"), ("x1lF0", "ldC0"), ("x1lF1", "ldC1"),
                       ("oalF0", "o0C0"), ("oalF1", "o0C1"), ("mmfF", "mxB"),
                       ("evfF", "wsB"), ("ostF", "stC"), ("otF", "toC"),
                       ("otxF", "adC")]:
            SM[fn] = SM[pn]
        _emit_fused(nc, block, SM, locals())
    return nc


def _emit_fused(nc, block, S, env):
    g = lambda k: env[k]
    ar = g("ar"); ar_mark = g("ar_mark")
    xsb, hfm = g("xsb"), g("hfm")
    PF, PT = g("PF"), g("PT")
    P0, P1, P2, P3, P4, P5 = PF
    PT0, PT1 = PT
    idbs, idfs, trius, ons16, i16s, idb64s, dps, cst, vcs, scs, smsk = (
        g("idbs"), g("idfs"), g("trius"), g("ons16"), g("i16s"), g("idb64s"),
        g("dps"), g("cst"), g("vcs"), g("scs"), g("smsk"))
    nwsp = g("nwsp")
    (x0s, Wnk, Qnk, maskA, szr, Ebm, Pbm, WinT, nwf, convw, convb, dtb, alog8,
     dpb, OutWT, A1, ws1, wz1, bias1f, gnwf, identb, idf, triu, ones16, i16,
     idb64, negrow, smask, sgidx, out) = (
        g("x0s"), g("Wnk"), g("Qnk"), g("maskA"), g("szr"), g("Ebm"), g("Pbm"),
        g("WinT"), g("nwf"), g("convw"), g("convb"), g("dtb"), g("alog8"),
        g("dpb"), g("OutWT"), g("A1"), g("ws1"), g("wz1"), g("bias1f"),
        g("gnwf"), g("identb"), g("idf"), g("triu"), g("ones16"), g("i16"),
        g("idb64"), g("negrow"), g("smask"), g("sgidx"), g("out"))
    (o0f, x1blk, xn1blk, sgd, hidd, Bcd, Ccd, dtod,
     ygfd, w1scr, oaccd) = (
        g("o0f"), g("x1blk"), g("xn1blk"), g("sgd"), g("hidd"), g("Bcd"),
        g("Ccd"), g("dtod"), g("ygfd"), g("w1scr"), g("oaccd"))
    hblk = g("hblk")

    T = TPAD
    NT = 17
    NG, NCH2 = 8, 17

    ar.off = ar_mark
    A_xin = ar.take(2 * D, F32)
    A_xnt = ar.take(2 * D, BF16)
    A_stats = ar.take(2 * 48, F32)
    A_mv = ar.take(4, F32)
    A_tiny = ar.take(16, F32)
    assert ar.off <= ar.nbytes

    ar.off = ar_mark
    B_wbf = ar.take(2 * D, BF16)
    B_qbf = ar.take(2 * D, BF16)
    B_wsum = ar.take(D, BF16)
    B_wT = ar.take(D, BF16)
    B_szc = ar.take(2 * 64, F32)
    B_zpc = ar.take(2 * 32, F32)
    B_esb = ar.take(2 * 128, BF16)
    B_mred = ar.take(2 * 128, BF16)
    B_bm = ar.take(128, BF16)
    B_psl = ar.take(128, BF16)
    B_osb = ar.take(T, F32)
    assert ar.off <= ar.nbytes

    ar.off = 0
    C_xa = ar.take(2 * D, F32)
    C_o0c = ar.take(2 * D, F32)
    C_x1 = ar.take(D, F32)
    C_xnt = ar.take(D, BF16)
    C_ht = ar.take(D, BF16)
    C_stats = ar.take(48, F32)
    C_mv = ar.take(4, F32)
    C_tiny = ar.take(16, F32)
    C_stg = ar.take(D, BF16)
    C_stg1 = ar.take(D, F32)
    C_stgH = ar.take(D, BF16)
    assert ar.off <= ar.nbytes

    ar.off = ar_mark
    D_wsb = ar.take(2 * 4096, BF16)
    D_cvin = ar.take(T + 3, F32)
    D_acc0 = ar.take(T, F32)
    D_acc1 = ar.take(T, F32)
    D_dtsgb = ar.take(T, F32)
    D_sgb = D_dtsgb.bitcast(BF16)[:, 0:TC]
    D_hcb = ar.take(T, F32)
    D_dts = D_dtsgb
    D_cws = ar.take(320, F32)
    D_cbs = ar.take(80, F32)
    D_dtbs = ar.take(1, F32)
    D_dtv = D_cvin
    assert ar.off <= ar.nbytes

    ar.off = 0
    E_Bb = ar.take(T, BF16)
    E_Cb = ar.take(T, BF16)
    E_dtu = ar.take(T, F32)
    E_acs2 = ar.take(128, F32)
    E_acv = ar.take(128, F32)
    E_hidc = ar.take(2 * 1024, BF16)
    E_sgc = ar.take(2 * 1024, BF16)
    E_sgt = ar.take(1024, BF16)
    E_hdtp = ar.take(1024, BF16)
    E_hcpp = ar.take(1024, BF16)
    E_srhp = ar.take(1024, BF16)
    E_prevb = ar.take(2 * 1024, F32)
    E_stb = ar.take(2 * 1024, F32)
    E_prevbf = ar.take(1024, BF16)
    E_LTm = ar.take(2 * 128, F32)
    E_LTe = ar.take(2 * 128, F32)
    E_Ylh = ar.take(2 * 128, BF16)
    E_Msk = ar.take(128, F32)
    E_BTs = ar.take(128, BF16)
    E_ddv = ar.take(2, F32)
    E_dtT = ar.take(16, F32)
    E_acsT = ar.take(16, F32)
    E_EA = ar.take(16, F32)
    E_Ebc = ar.take(16, F32)
    E_ell = ar.take(2, F32)
    E_alg = ar.take(16, F32)
    E_nega = ar.take(1, F32)
    E_diag = ar.take(16, F32)
    E_dop = ar.take(2 * 2048, F32)
    E_ysb = ar.take(1024, F32)
    E_y2b = ar.take(1024, F32)
    E_ygb = ar.take(1024, F32)
    E_ssq = ar.take(1, F32)
    E_rgs = ar.take(2, F32)
    E_ybf = ar.take(1024, BF16)
    E_ygT = ar.take(1024, BF16)
    assert ar.off <= ar.nbytes, f"E arena {ar.off}"

    ar.off = ar_mark
    F_un = ar.take(8192, I32)
    F_wsb = F_un.bitcast(BF16)
    F_w1sb = F_un.bitcast(BF16)[:, 0:2 * 4096]
    F_ob = F_un.bitcast(F32)[:, 4096:6144]
    F_xs = F_un.bitcast(F32)[:, 6144:8192]
    F_oa = ar.take(2 * 1024, F32)
    F_tmp = ar.take(1024, F32)
    F_otb = F_tmp
    F_ws1s = ar.take(32, F32)
    F_wz1s = ar.take(32, F32)
    F_zs1 = ar.take(32, F32)
    F_b1s = ar.take(32, F32)
    F_gnws = ar.take(64, F32)
    assert ar.off <= ar.nbytes, f"F arena {ar.off}"
    F_ygh = xsb[:, 0:64 * 1024]
    F_qv = g("big")[:, 0:16384].bitcast(I32)
    F_dq = xsb[:, 32 * 1024:32 * 1024 + D]
    F_w1T = xsb[:, 40 * 1024:40 * 1024 + D]
    F_xnh = xsb[:, 0:32 * 1024]

    FB = {"qlF0": 0, "qlF1": 0, "dqF": 17, "tpF": 544, "txF": 1568,
          "w1stF": 17, "yglF": 0, "gnfF": 32, "wlF0": 512, "wlF1": 512,
          "mmaF": 32, "evaF": 32, "oastF": 512, "xnlF": 32, "w1lF0": 256,
          "w1lF1": 256, "x1lF0": 144, "x1lF1": 128, "oalF0": 576,
          "oalF1": 512, "mmfF": 32, "evfF": 32, "ostF": 3264, "otF": 544,
          "otxF": 544}
    MAINB = [(0, 128, 512), (1, 640, 512), (2, 1152, 512), (3, 1664, 512)]
    HALOB = (4, 0, 128)
    psDm = [P2, P3]

    def role_d(nb):
        return "gate" if nb < 64 else ("hbc" if nb < 144 else "dt")

    NPH = 6

    hid3E = E_hidc.rearrange("p (b r t) -> p b r t", b=2, r=8)

    @block.sync
    def _(sync):
        for ap_, sb in [(identb, idbs), (idf, idfs), (triu, trius),
                        (ones16, ons16), (i16, i16s), (idb64, idb64s),
                        (dpb, dps), (smask, smsk), (nwf, nwsp)]:
            sync.dma_start(out=sb[:], in_=ap_[:]).then_inc(S["cldA"], 16)
        for i in range(NT):
            p = i % 2
            if i >= 2:
                sync.wait_ge(S["afA"], i - 1)
            sync.dma_start(out=A_xin[:, p * D:(p + 1) * D],
                           in_=x0s[i * 128:(i + 1) * 128, :]).then_inc(
                S["ldA0" if p == 0 else "ldA1"], 16)
        sync.nop().then_inc(S["pg"], 1)
        sync.wait_ge(S["pg"], 5)
        for t in range(32):
            pe = t % 2
            if t >= 2:
                sync.wait_ge(S["msumB"], t - 1)
            sync.dma_start(out=B_esb[:, pe * 128:(pe + 1) * 128],
                           in_=Ebm[:, t * 128:(t + 1) * 128]).then_inc(
                S["eldB0" if pe == 0 else "eldB1"], 16)
        for ncn in range(32):
            sync.wait_ge(S["evB"], ncn + 1)
            sync.dma_start(out=o0f[ncn * 128:(ncn + 1) * 128, :],
                           in_=B_osb[:]).then_inc(S["stB"], 16)
        sync.wait_ge(S["stB"], 16 * 32)
        sync.nop().then_inc(S["pg"], 1)
        sync.wait_ge(S["pg"], 10)
        for i in range(NT):
            p = i % 2
            if i >= 2:
                sync.wait_ge(S["adC"], 32 * (i - 1))
            sync.dma_start(out=C_xa[:, p * D:(p + 1) * D],
                           in_=x0s[i * 128:(i + 1) * 128, :]).then_inc(
                S["ldC0" if p == 0 else "ldC1"], 16)
            for q4 in range(4):
                sync.dma_start(
                    out=C_o0c[:, p * D + q4 * 1024: p * D + (q4 + 1) * 1024]
                    .rearrange("p (a t) -> p a t", t=128),
                    in_=o0f[q4 * 1024:(q4 + 1) * 1024, i * 128:(i + 1) * 128]
                    .rearrange("(a p) t -> p a t", p=128)).then_inc(
                    S["o0C0" if p == 0 else "o0C1"], 16)
            sync.wait_ge(S["tpC"], 96 * (i + 1))
            for q4 in range(4):
                sync.dma_start(
                    out=xn1blk[i, q4 * 8:(q4 + 1) * 8].rearrange("a p t -> p a t"),
                    in_=C_stg[:, q4 * 1024:(q4 + 1) * 1024]
                    .rearrange("p (a t) -> p a t", t=128)).then_inc(S["stC"], 16)
                sync.dma_start(
                    out=x1blk[i, q4 * 8:(q4 + 1) * 8].rearrange("a p t -> p a t"),
                    in_=C_stg1[:, q4 * 1024:(q4 + 1) * 1024]
                    .rearrange("p (a t) -> p a t", t=128)).then_inc(S["stC"], 16)
                sync.dma_start(
                    out=hblk[i, q4 * 8:(q4 + 1) * 8].rearrange("a p t -> p a t"),
                    in_=C_stgH[:, q4 * 1024:(q4 + 1) * 1024]
                    .rearrange("p (a t) -> p a t", t=128)).then_inc(S["stC"], 16)
        sync.wait_ge(S["stC"], 16 * 12 * NT)
        sync.nop().then_inc(S["pg"], 1)
        sync.wait_ge(S["pg"], 15)
        for kc in range(32):
            sync.dma_start(
                out=hfm[:, kc * T: kc * T + 1024].rearrange("p (a t) -> p a t", t=128),
                in_=hblk[0:8, kc].rearrange("a p t -> p a t")).then_inc(S["txA"], 16)
            sync.dma_start(
                out=hfm[:, kc * T + 1024: kc * T + 2176].rearrange("p (a t) -> p a t", t=128),
                in_=hblk[8:17, kc].rearrange("a p t -> p a t")).then_inc(S["txA"], 16)
        sync.dma_start(out=D_cws[:], in_=convw[:]).then_inc(S["cldA"], 16)
        sync.dma_start(out=D_cbs[:], in_=convb[:]).then_inc(S["cldA"], 16)
        sync.dma_start(out=D_dtbs[:], in_=dtb[:]).then_inc(S["cldA"], 16)
        isg = 0; ihc = 0
        for nb in range(145):
            r = role_d(nb)
            if r == "gate":
                isg += 1
                sync.wait_ge(S["sgD"], isg)
                sync.dma_start(out=sgd[nb], in_=D_sgb[:]).then_inc(S["stgD"], 16)
            elif r == "hbc":
                cc = nb - 64
                ihc += 1
                sync.wait_ge(S["hcD"], ihc)
                dst = hidd[cc] if cc < 64 else (Bcd[cc - 64] if cc < 72 else Ccd[cc - 72])
                sync.dma_start(out=dst, in_=D_hcb[:]).then_inc(S["sthD"], 16)
            else:
                sync.wait_ge(S["dtD"], 1)
                sync.dma_start(out=dtod[:], in_=D_cvin[:, 0:T]).then_inc(S["stdD"], 16)
        sync.wait_ge(S["stgD"], 16 * 64)
        sync.wait_ge(S["sthD"], 16 * 80)
        sync.wait_ge(S["stdD"], 16)
        sync.nop().then_inc(S["pg"], 1)
        sync.wait_ge(S["pg"], 20)
        for gix in range(NG):
            for c in range(1, NCH2):
                ny = gix * 16 + c
                sync.wait_ge(S["ytxE"], 8 * ny)
                sync.dma_start(out=ygfd[gix * 1024:(gix + 1) * 1024,
                                        (c - 1) * 128: c * 128]
                               .rearrange("(r p) t -> p r t", p=128),
                               in_=E_ygT[:].rearrange("p (r t) -> p r t", t=128)
                               ).then_inc(S["ystE0"], 16)
        sync.wait_ge(S["ystE0"], 16 * NG * 16)
        sync.nop().then_inc(S["pg"], 1)
        sync.wait_ge(S["pg"], 25)
        for ap_, sb in [(ws1, F_ws1s), (wz1, F_wz1s), (bias1f, F_b1s), (gnwf, F_gnws)]:
            sync.dma_start(out=sb[:], in_=ap_[:]).then_inc(S["cldA"], 16)
        for ncn in range(32):
            sync.wait_ge(S["txF"], FB["txF"] + 32 * (ncn + 1))
            sync.dma_start(out=w1scr[:, ncn], in_=F_w1T.rearrange("p (a b) -> p a b", b=128)
                           ).then_inc(S["w1stF"], 16)
        for th in range(2):
            if th == 1:
                sync.wait_ge(S["mmaF"], FB["mmaF"] + 32)
            for kc in range(64):
                sync.dma_start(out=F_ygh[:, kc * 1024:(kc + 1) * 1024],
                               in_=ygfd[kc * 128:(kc + 1) * 128, th * 1024:(th + 1) * 1024]
                               ).then_inc(S["yglF"], 16)
            for nb in range(32):
                i = th * 32 + nb
                sync.wait_ge(S["evaF"], FB["evaF"] + i + 1)
                sync.dma_start(out=oaccd[nb * 128:(nb + 1) * 128, th * 1024:(th + 1) * 1024],
                               in_=F_oa[:, (i % 2) * 1024:(i % 2 + 1) * 1024]).then_inc(S["oastF"], 16)
        for th in range(2):
            if th == 0:
                sync.wait_ge(S["mmaF"], FB["mmaF"] + 64)
            else:
                sync.wait_ge(S["mmfF"], FB["mmfF"] + 32)
            for kc in range(32):
                sync.dma_start(out=F_xnh[:, kc * 1024:(kc + 1) * 1024]
                               .rearrange("p (a t) -> p a t", t=128),
                               in_=xn1blk[1 + th * 8: 9 + th * 8, kc].rearrange("a p t -> p a t")
                               ).then_inc(S["xnlF"], 16)
            for nb in range(32):
                i = th * 32 + nb
                p = i % 2
                if i >= 2:
                    sync.wait_ge(S["evfF"], FB["evfF"] + i - 1)
                sync.dma_start(out=F_xs[:, p * 1024:(p + 1) * 1024]
                               .rearrange("p (a t) -> p a t", t=128),
                               in_=x1blk[1 + th * 8: 9 + th * 8, nb].rearrange("a p t -> p a t")
                               ).then_inc(S["x1lF0" if p == 0 else "x1lF1"], 16)
                sync.wait_ge(S["oastF"], FB["oastF"] + 16 * (th * 32 + nb + 1) if th == 0 else 16 * 64)
                sync.dma_start(out=F_ob[:, p * 1024:(p + 1) * 1024],
                               in_=oaccd[nb * 128:(nb + 1) * 128, th * 1024:(th + 1) * 1024]
                               ).then_inc(S["oalF0" if p == 0 else "oalF1"], 16)
                sync.wait_ge(S["otxF"], FB["otxF"] + 8 * (i + 1))
                sync.dma_start(out=out[th * 1024:(th + 1) * 1024, nb * 128:(nb + 1) * 128]
                               .rearrange("(a p) n -> p a n", p=128),
                               in_=F_otb[:].rearrange("p (a n) -> p a n", n=128)
                               ).then_inc(S["ostF"], 16)
        sync.wait_ge(S["ostF"], FB["ostF"] + 16 * 64)
        sync.nop().then_inc(S["pg"], 1)

    @block.gpsimd
    def _(gp):
        gp.nop().then_inc(S["pg"], 1)
        gp.wait_ge(S["pg"], 5)
        for t in range(32):
            p = t % 2
            if t >= 2:
                gp.wait_ge(S["mrdB"], t - 1)
            gp.dma_start(out=B_wbf[:, p * D:(p + 1) * D],
                         in_=maskA[t * 128:(t + 1) * 128, :]).then_inc(
                S["mldB0" if p == 0 else "mldB1"], 16)
        for ncn in range(32):
            p = ncn % 2
            if ncn >= 1:
                gp.wait_ge(S["mxB"], ncn)
            gp.dma_start(out=B_psl[:], in_=Pbm[:, ncn * 128:(ncn + 1) * 128]).then_inc(S["pldB"], 16)
            if ncn >= 2:
                gp.wait_ge(S["wsB"], ncn - 1)
            gp.dma_start(out=B_szc[:, p * 64:(p + 1) * 64],
                         in_=szr[:, ncn * 64:(ncn + 1) * 64]).then_inc(
                S["szlB0" if p == 0 else "szlB1"], 16)
            if ncn == 0:
                gp.wait_ge(S["bmB"], 1)
            gp.dma_start(out=B_wbf[:, p * D:(p + 1) * D],
                         in_=Wnk[ncn * 128:(ncn + 1) * 128, :]).then_inc(
                S["wldB0" if p == 0 else "wldB1"], 16)
            gp.dma_start(out=B_qbf[:, p * D:(p + 1) * D],
                         in_=Qnk[ncn * 128:(ncn + 1) * 128, :]).then_inc(
                S["wldB0" if p == 0 else "wldB1"], 16)
        gp.nop().then_inc(S["pg"], 1)
        gp.wait_ge(S["pg"], 10)
        gp.nop().then_inc(S["pg"], 1)
        gp.wait_ge(S["pg"], 15)
        for nb in range(145):
            p = nb % 2
            if nb >= 2:
                gp.wait_ge(S["mmD"], nb - 1)
            gp.dma_start(out=D_wsb[:, p * 4096:(p + 1) * 4096]
                         .rearrange("p (a b) -> p a b", b=128),
                         in_=WinT[:, nb]).then_inc(S["wD0" if p == 0 else "wD1"], 16)
        gp.nop().then_inc(S["pg"], 1)
        gp.wait_ge(S["pg"], 20)
        gp.dma_start(out=E_dop[0:2, 2048:4096], in_=negrow[:]).then_inc(S["dmlE"], 16)
        for gix in range(NG):
            if gix >= 1:
                gp.wait_ge(S["ugE"], gix)
            gp.dma_start(out=E_Bb[:], in_=Bcd[gix]).then_inc(S["uclE"], 16)
            gp.dma_start(out=E_Cb[:], in_=Ccd[gix]).then_inc(S["uclE"], 16)
            gp.dma_start(out=E_dtu[0:16, :], in_=dtod[gix * 16:(gix + 1) * 16, :]).then_inc(S["uclE"], 16)
            gp.dma_start(out=E_alg[0:16, 0:8], in_=alog8[:, :]).then_inc(S["uclE"], 16)
            for c in range(NCH2):
                ci = gix * NCH2 + c
                p = ci % 2
                if ci >= 2:
                    gp.wait_ge(S["htE"], 16 * (ci - 1))
                gp.dma_start(out=hid3E[:, p],
                             in_=hidd[gix * 8:(gix + 1) * 8, :, c * 128:(c + 1) * 128]
                             .rearrange("r p t -> p r t")).then_inc(
                    S["hlE0" if p == 0 else "hlE1"], 16)
                if c >= 1:
                    ny = gix * 16 + c
                    if ny >= 3:
                        gp.wait_ge(S["sgtE"], ny - 2)
                    gp.dma_start(out=E_sgc[:, (ny % 2) * 1024:(ny % 2 + 1) * 1024]
                                 .rearrange("p (r t) -> p r t", r=8),
                                 in_=sgd[gix * 8:(gix + 1) * 8, :, (c - 1) * 128: c * 128]
                                 .rearrange("r p t -> p r t")).then_inc(
                        S["slE0" if ny % 2 == 0 else "slE1"], 16)
                gp.wait_ge(S["acsE"], ci + 1)
                if ci >= 1:
                    gp.wait_ge(S["dmmE"], 16 * ci)
                gp.dma_start(out=E_dop[1:2, 0:2048], in_=E_acs2[0:16, :]).then_inc(S["dmlE"], 16)
                gp.dma_start(out=E_dop[0:1, 2048:4096], in_=E_acs2[0:16, :]).then_inc(S["dmlE"], 16)
        gp.nop().then_inc(S["pg"], 1)
        gp.wait_ge(S["pg"], 25)
        for ncn in range(32):
            p = ncn % 2
            if ncn >= 2:
                gp.wait_ge(S["dqF"], FB["dqF"] + ncn - 1)
            gp.dma_start(out=F_qv[:, p * D:(p + 1) * D],
                         in_=A1[ncn * 128:(ncn + 1) * 128, :]).then_inc(
                S["qlF0" if p == 0 else "qlF1"], 16)
        for th in range(2):
            for nb in range(32):
                i = th * 32 + nb
                p = i % 2
                if i >= 2:
                    gp.wait_ge(S["mmaF"], FB["mmaF"] + i - 1)
                gp.dma_start(out=F_wsb[:, p * 8192:(p + 1) * 8192]
                             .rearrange("p (a b) -> p a b", b=128),
                             in_=OutWT[:, nb]).then_inc(S["wlF0" if p == 0 else "wlF1"], 16)
        for th in range(2):
            for nb in range(32):
                i = th * 32 + nb
                p = i % 2
                if i == 0:
                    gp.wait_ge(S["w1stF"], FB["w1stF"] + 16 * 32)
                    gp.wait_ge(S["mmaF"], FB["mmaF"] + 64)
                if i >= 2:
                    gp.wait_ge(S["mmfF"], FB["mmfF"] + i - 1)
                gp.dma_start(out=F_w1sb[:, p * 4096:(p + 1) * 4096]
                             .rearrange("p (a b) -> p a b", b=128),
                             in_=w1scr[:, nb]).then_inc(S["w1lF0" if p == 0 else "w1lF1"], 16)
        gp.nop().then_inc(S["pg"], 1)

    @block.vector
    def _(v):
        nfen = [0]

        def fence(r):
            r.then_inc(S["fenV"], 1)
            nfen[0] += 1
            v.wait_ge(S["fenV"], nfen[0])

        def fence_new():
            fence(v.tensor_copy(vcs[:, 0:1], vcs[:, 1:2]))

        v.memset(vcs[:], 0.0)
        v.memset(cst[:, 0:1], EPS)
        fence(v.memset(cst[:, 1:2], EPS * EPS))
        for i in range(NT):
            p = i % 2
            q = p * 8
            v.wait_ge(S["ldA0" if p == 0 else "ldA1"], 16 * (i // 2 + 1))
            if i >= 2:
                v.wait_ge(S["rsA"], i - 1)
            for j in range(8):
                r = v.bn_stats(A_stats[:, p * 48 + j * 6: p * 48 + (j + 1) * 6],
                               A_xin[:, p * D + j * 512: p * D + (j + 1) * 512])
            fence(r)
            v.bn_aggr(A_mv[:, p * 2:(p + 1) * 2], A_stats[:, p * 48:(p + 1) * 48]).then_inc(S["bnA"], 1)
            v.wait_ge(S["rsA"], i + 1)
            if i >= 2:
                v.wait_ge(S["tpA"], 32 * (i - 1))
            v.tensor_scalar(A_xnt[:, p * D:(p + 1) * D], A_xin[:, p * D:(p + 1) * D],
                            A_tiny[:, q + 2:q + 3], A_tiny[:, q + 0:q + 1],
                            ALU.add, ALU.mult).then_inc(S["afA"], 1)
        v.nop().then_inc(S["pg"], 1)
        v.wait_ge(S["pg"], 5)
        for t in range(32):
            p = t % 2
            v.wait_ge(S["mldB0" if p == 0 else "mldB1"], 16 * (t // 2 + 1))
            if t >= 2:
                v.wait_ge(S["msumB"], t - 1)
            with nc.allow_low_precision(reason="mask 0/1 sums<=32 exact in bf16"):
                v.reduce_sum(B_mred[:, p * 128:(p + 1) * 128],
                             B_wbf[:, p * D:(p + 1) * D].rearrange("p (a b) -> p a b", b=32),
                             axis=mybir.AxisListType.X).then_inc(S["mrdB"], 1)
        v.wait_ge(S["msumB"], 32)
        v.tensor_scalar(B_bm[:], P5[:, 0:128], 0.0, None, ALU.is_gt).then_inc(S["bmB"], 1)
        for ncn in range(32):
            p = ncn % 2
            v.wait_ge(S["wldB0" if p == 0 else "wldB1"], 32 * (ncn // 2 + 1))
            v.wait_ge(S["szlB0" if p == 0 else "szlB1"], 16 * (ncn // 2 + 1))
            szc3 = B_szc[:, p * 64:(p + 1) * 64].rearrange("p (g c) -> p g c", c=2)
            fence(v.scalar_tensor_tensor(B_zpc[:, p * 32:(p + 1) * 32], szc3[:, :, 0], -8.0,
                                         szc3[:, :, 1], ALU.mult, ALU.add))
            if ncn >= 1:
                v.wait_ge(S["tpB"], 32 * ncn)
            for gx in range(32):
                v.tensor_scalar(B_wsum[:, gx * 128:(gx + 1) * 128],
                                B_qbf[:, p * D + gx * 128: p * D + (gx + 1) * 128],
                                szc3[:, gx:gx + 1, 0], B_zpc[:, p * 32 + gx: p * 32 + gx + 1],
                                ALU.mult, ALU.add)
            v.wait_ge(S["mxB"], ncn + 1)
            wmv = B_qbf[:, p * D:(p + 1) * D]
            fence(v.tensor_tensor(wmv.rearrange("p (a b) -> p a b", b=32),
                                  B_wbf[:, p * D:(p + 1) * D].rearrange("p (a b) -> p a b", b=32),
                                  P5[:, 0:128].unsqueeze(2).broadcast_to([128, 128, 32]),
                                  ALU.mult))
            v.tensor_tensor(B_wsum[:], B_wsum[:], wmv, ALU.add).then_inc(S["wsB"], 1)
            v.wait_ge(S["mmB"], ncn + 1)
            if ncn >= 1:
                v.wait_ge(S["stB"], 16 * ncn)
            for tb, (off, sz) in enumerate([(0, 512), (512, 512), (1024, 512), (1536, 512), (2048, 128)]):
                r = v.tensor_copy(B_osb[:, off:off + sz], PF[tb][:, 0:sz])
            r.then_inc(S["evB"], 1)
        v.nop().then_inc(S["pg"], 1)
        v.wait_ge(S["pg"], 10)
        for i in range(NT):
            p = i % 2
            v.wait_ge(S["ldC0" if p == 0 else "ldC1"], 16 * (i // 2 + 1))
            if i >= 1:
                v.wait_ge(S["tpC"], 96 * i)
            for ncx in range(32):
                v.wait_ge(S["toC"], i * 32 + ncx + 1)
                r = v.tensor_tensor(C_x1[:, ncx * 128:(ncx + 1) * 128],
                                    PF[ncx % 2][:, 0:128],
                                    C_xa[:, p * D + ncx * 128: p * D + (ncx + 1) * 128],
                                    ALU.add)
                r.then_inc(S["adC"], 1)
            fence_new()
            for j in range(8):
                r = v.bn_stats(C_stats[:, j * 6:(j + 1) * 6], C_x1[:, j * 512:(j + 1) * 512])
            fence(r)
            v.bn_aggr(C_mv[:, 0:2], C_stats[:]).then_inc(S["bnC"], 1)
            v.wait_ge(S["rsC"], i + 1)
            v.tensor_scalar(C_xnt[:], C_x1[:], C_tiny[:, 2:3], C_tiny[:, 0:1],
                            ALU.add, ALU.mult)
            r = v.tensor_scalar(C_ht[:], C_x1[:], C_tiny[:, 3:4], C_tiny[:, 1:2],
                                ALU.add, ALU.mult)
            r.then_inc(S["afC"], 1)
        v.nop().then_inc(S["pg"], 1)
        v.wait_ge(S["pg"], 15)
        v.wait_ge(S["cldA"], 16 * 12)
        for nb in range(145):
            r = role_d(nb)
            if r == "gate":
                continue
            v.wait_ge(S["mmD"], nb + 1)
            if r == "hbc":
                cc = nb - 64
                v.memset(D_cvin[:, 0:3], 0.0)
                v.tensor_copy(D_cvin[:, 3:131], P4[:, 0:128])
                for (bk, off, sz) in MAINB:
                    rr = v.tensor_copy(D_cvin[:, 3 + off:3 + off + sz], PF[bk][:])
                rr.then_inc(S["evD"], 1)
                fence_new()
                if cc >= 1:
                    v.wait_ge(S["hcD"], cc)
                fence(v.tensor_scalar_mul(D_acc0[:], D_cvin[:, 0:T],
                                          D_cws[:, cc * 4:cc * 4 + 1]))
                fence(v.scalar_tensor_tensor(D_acc1[:], D_cvin[:, 1:1 + T],
                                             D_cws[:, cc * 4 + 1:cc * 4 + 2], D_acc0[:],
                                             ALU.mult, ALU.add))
                fence(v.scalar_tensor_tensor(D_acc0[:], D_cvin[:, 2:2 + T],
                                             D_cws[:, cc * 4 + 2:cc * 4 + 3], D_acc1[:],
                                             ALU.mult, ALU.add))
                v.scalar_tensor_tensor(D_acc1[:], D_cvin[:, 3:3 + T],
                                       D_cws[:, cc * 4 + 3:cc * 4 + 4], D_acc0[:],
                                       ALU.mult, ALU.add).then_inc(S["cvD"], 1)
            else:
                rr = v.tensor_scalar_add(D_dtv[:, 0:128], P4[:, 0:128], D_dtbs[:])
                for (bk, off, sz) in MAINB:
                    rr = v.tensor_scalar_add(D_dtv[:, off:off + sz], PF[bk][:], D_dtbs[:])
                rr.then_inc(S["evD"], 1)
                fence_new()
                v.tensor_copy(vcs[:, 0:1], vcs[:, 1:2]).then_inc(S["cvD"], 1)
                v.wait_ge(S["spD"], 1)
                fence(v.tensor_relu(D_dts[:], D_dtv[:, 0:T]))
                v.tensor_tensor(D_dtv[:, 0:T], D_dts[:], D_acc0[:], ALU.add).then_inc(S["dtD"], 1)
        v.nop().then_inc(S["pg"], 1)
        v.wait_ge(S["pg"], 20)
        v.memset(E_dop[0:1, 0:2048], 1.0)
        fence(v.memset(vcs[:], 0.0))
        for gix in range(NG):
            v.wait_ge(S["uclE"], 16 * 4 * (gix + 1))
            v.wait_ge(S["elE"], gix * 17 + 1)
            fence(v.tensor_scalar_mul(E_nega[0:16, :], E_ell[0:16, 1:2], -1.0))
            v.memset(E_prevb[:, 0:1024], 0.0)
            fence(v.memset(E_prevb[:, 1024:2048], 0.0))
            for c in range(NCH2):
                ci = gix * NCH2 + c
                cp = c % 2
                ny = gix * 16 + c
                if ci >= 1:
                    v.wait_ge(S["dmlE"], 16 + 32 * ci)
                    v.wait_ge(S["btE"], 2 * (ci - 1) + 1)
                    v.wait_ge(S["elE"], gix * 17 + c + 1)
                fence(v.tensor_scalar_mul(E_acv[0:16, :], E_dtu[0:16, c * 128:(c + 1) * 128],
                                          E_nega[0:16, :]))
                v.tensor_tensor_scan(E_acs2[0:16, :], ons16[:], E_acv[0:16, :], 0.0,
                                     ALU.mult, ALU.add).then_inc(S["acsE"], 1)
                v.wait_ge(S["btE"], 2 * (ci + 1))
                v.tensor_copy(E_BTs[:], PT0[:, 0:128]).then_inc(S["btvE"], 1)
                if c >= 1:
                    v.wait_ge(S["mtE"], ny)
                    v.tensor_tensor(E_Msk[:], P4[:, 0:128], trius[:], ALU.mult).then_inc(S["mskE"], 1)
                    v.wait_ge(S["slE0" if ny % 2 == 0 else "slE1"], 16 * ((ny - 1) // 2 + 1))
                    v.tensor_copy(E_sgt[:], PT1[:]).then_inc(S["sgtE"], 1)
                    v.wait_ge(S["elE"], gix * 17 + c + 1)
                    v.tensor_scalar_mul(E_diag[0:16, :], i16s[:],
                                        E_ell[0:16, 0:1]).then_inc(S["dgvE"], 1)
                    v.wait_ge(S["ebeE"], gix * 16 + c)
                    for h in range(16):
                        o = h * 64
                        r = v.scalar_tensor_tensor(
                            E_prevb[:, cp * 1024 + o: cp * 1024 + o + 64],
                            E_prevb[:, (1 - cp) * 1024 + o: (1 - cp) * 1024 + o + 64],
                            E_Ebc[:, h:h + 1],
                            E_stb[:, (1 - cp) * 1024 + o: (1 - cp) * 1024 + o + 64],
                            ALU.mult, ALU.add)
                    fence(r)
                    r = v.tensor_copy(E_prevbf[:], E_prevb[:, cp * 1024:(cp + 1) * 1024])
                    r.then_inc(S["corE"], 1)
                v.tensor_copy(vcs[:, 0:1], vcs[:, 1:2]).then_inc(S["pvE"], 1)
                v.wait_ge(S["dttE"], ci + 1)
                for h in range(16):
                    hp = h % 2
                    v.wait_ge(S["dmmE"], ci * 16 + h + 1)
                    v.tensor_scalar_min(E_LTm[:, hp * 128:(hp + 1) * 128],
                                        psDm[hp][:, 0:128], 0.0).then_inc(S["ltmE"], 1)
                    v.wait_ge(S["lteE"], ci * 16 + h + 1)
                    fence(v.tensor_tensor(E_ddv[:, hp:hp + 1],
                                          E_LTe[:, hp * 128 + 127: hp * 128 + 128],
                                          E_dtT[:, h:h + 1], ALU.mult))
                    if c >= 1:
                        v.tensor_tensor(E_Ylh[:, hp * 128:(hp + 1) * 128], E_Msk[:],
                                        E_LTe[:, hp * 128:(hp + 1) * 128], ALU.mult)
                    v.wait_ge(S["htE"], 16 * ci + h + 1)
                    r = v.tensor_scalar_mul(E_srhp[:, h * 64:(h + 1) * 64],
                                            PT[hp][:, 0:64], E_ddv[:, hp:hp + 1])
                    fence(r)
                    v.tensor_copy(vcs[:, 0:1], vcs[:, 1:2]).then_inc(S["ylhE"], 1)
                v.wait_ge(S["smE"], 2 * (ci + 1))
                v.tensor_copy(E_stb[:, cp * 1024: cp * 1024 + 512], P0[:])
                fence(v.tensor_copy(E_stb[:, cp * 1024 + 512: cp * 1024 + 1024], P1[:]))
                v.tensor_copy(vcs[:, 0:1], vcs[:, 1:2]).then_inc(S["steE"], 1)
                if c >= 1:
                    v.wait_ge(S["ydE"], ny)
                    v.tensor_copy(E_ysb[:, 0:512], P4[:])
                    fence(v.tensor_copy(E_ysb[:, 512:1024], P5[:]))
                    v.tensor_copy(vcs[:, 0:1], vcs[:, 1:2]).then_inc(S["yevE"], 1)
                    v.wait_ge(S["yoE"], 2 * ny)
                    v.wait_ge(S["actE"], ci + 1)
                    for h in range(16):
                        o = h * 64
                        psx = P0 if h < 8 else P1
                        r = v.scalar_tensor_tensor(E_ysb[:, o:o + 64],
                                                   psx[:, (h % 8) * 64:(h % 8) * 64 + 64],
                                                   E_EA[:, h:h + 1], E_ysb[:, o:o + 64],
                                                   ALU.mult, ALU.add)
                    fence(r)
                    v.tensor_copy(vcs[:, 0:1], vcs[:, 1:2]).then_inc(S["yaE"], 1)
                    v.wait_ge(S["hdeE"], 16 * (ci + 1))
                    for h in range(16):
                        o = h * 64
                        r = v.scalar_tensor_tensor(E_y2b[:, o:o + 64], E_hcpp[:, o:o + 64],
                                                   dps[:, gix * 16 + h: gix * 16 + h + 1],
                                                   E_ysb[:, o:o + 64], ALU.mult, ALU.add)
                    fence(r)
                    v.wait_ge(S["sgtE"], ny)
                    fence(v.tensor_tensor(E_ygb[:], E_y2b[:], E_sgt[:], ALU.mult))
                    v.tensor_copy(vcs[:, 0:1], vcs[:, 1:2]).then_inc(S["ygE"], 1)
                    v.wait_ge(S["sqE"], ny)
                    if ny >= 2:
                        v.wait_ge(S["ytE"], 8 * (ny - 1))
                    v.tensor_scalar_mul(E_ybf[:], E_ygb[:], E_rgs[:, 0:1]).then_inc(S["ybfE"], 1)
            v.tensor_copy(vcs[:, 0:1], vcs[:, 1:2]).then_inc(S["ugE"], 1)
        v.nop().then_inc(S["pg"], 1)
        v.wait_ge(S["pg"], 25)
        v.wait_ge(S["cldA"], 16 * 16)
        fence(v.tensor_tensor(F_zs1[:], F_wz1s[:], F_ws1s[:], ALU.mult))
        fence(v.tensor_scalar_mul(F_zs1[:], F_zs1[:], -1.0))
        for ncn in range(32):
            p = ncn % 2
            v.wait_ge(S["qlF0" if p == 0 else "qlF1"], FB["qlF0" if p == 0 else "qlF1"] + 16 * (ncn // 2 + 1))
            if ncn >= 1:
                v.wait_ge(S["tpF"], FB["tpF"] + 32 * ncn)
            q3 = F_qv[:, p * D:(p + 1) * D].rearrange("p (a b) -> p a b", b=4)
            fence(v.tensor_single_scalar(q3[:, :, 1], q3[:, :, 1], 6, ALU.logical_shift_right))
            fence(v.tensor_single_scalar(F_qv[:, p * D:(p + 1) * D],
                                         F_qv[:, p * D:(p + 1) * D], 63, ALU.bitwise_and))
            for gx in range(4):
                r = v.tensor_scalar(F_dq[gx * 1024:(gx + 1) * 1024] if False else F_dq[:, gx * 1024:(gx + 1) * 1024],
                                    F_qv[:, p * D + gx * 1024: p * D + (gx + 1) * 1024],
                                    F_ws1s[:, ncn:ncn + 1], F_zs1[:, ncn:ncn + 1],
                                    ALU.mult, ALU.add)
            r.then_inc(S["dqF"], 1)
        for th in range(2):
            v.wait_ge(S["yglF"], FB["yglF"] + 16 * 64 * (th + 1))
            for kc in range(64):
                fence(v.tensor_scalar_mul(F_tmp[:].bitcast(BF16)[:, 0:1024],
                                          F_ygh[:, kc * 1024:(kc + 1) * 1024],
                                          F_gnws[:, kc:kc + 1]))
                fence(v.tensor_copy(F_ygh[:, kc * 1024:(kc + 1) * 1024],
                                    F_tmp[:].bitcast(BF16)[:, 0:1024]))
            v.tensor_copy(vcs[:, 0:1], vcs[:, 1:2]).then_inc(S["gnfF"], 1)
            for nb in range(32):
                i = th * 32 + nb
                v.wait_ge(S["mmaF"], FB["mmaF"] + i + 1)
                if i >= 2:
                    v.wait_ge(S["oastF"], FB["oastF"] + 16 * (i - 1))
                v.tensor_copy(F_oa[:, (i % 2) * 1024: (i % 2) * 1024 + 512], P0[:])
                r = v.tensor_copy(F_oa[:, (i % 2) * 1024 + 512:(i % 2) * 1024 + 1024], P1[:])
                r.then_inc(S["evaF"], 1)
        for th in range(2):
            for nb in range(32):
                i = th * 32 + nb
                p = i % 2
                v.wait_ge(S["mmfF"], FB["mmfF"] + i + 1)
                v.wait_ge(S["oalF0" if p == 0 else "oalF1"], FB["oalF0" if p == 0 else "oalF1"] + 16 * (i // 2 + 1))
                v.wait_ge(S["x1lF0" if p == 0 else "x1lF1"], FB["x1lF0" if p == 0 else "x1lF1"] + 16 * (i // 2 + 1))
                if i < 2:
                    v.wait_ge(S["oastF"], FB["oastF"] + 16 * (63 + i))
                if i >= 2:
                    v.wait_ge(S["otF"], FB["otF"] + 8 * (i - 1))
                for bk, psx in [(0, P2), (1, P3)]:
                    o0_ = p * 1024 + bk * 512
                    fence(v.tensor_scalar_add(F_oa[:, o0_:o0_ + 512], psx[:], F_b1s[:, nb:nb + 1]))
                fence(v.tensor_tensor(F_oa[:, p * 1024:(p + 1) * 1024], F_oa[:, p * 1024:(p + 1) * 1024],
                                      F_ob[:, p * 1024:(p + 1) * 1024], ALU.add))
                fence(v.tensor_tensor(F_oa[:, p * 1024:(p + 1) * 1024], F_oa[:, p * 1024:(p + 1) * 1024],
                                      F_xnh[:, nb * 1024:(nb + 1) * 1024], ALU.add))
                r = v.tensor_tensor(F_oa[:, p * 1024:(p + 1) * 1024], F_oa[:, p * 1024:(p + 1) * 1024],
                                    F_xs[:, p * 1024:(p + 1) * 1024], ALU.add)
                r.then_inc(S["evfF"], 1)
        v.nop().then_inc(S["pg"], 1)

    @block.scalar
    def _(s):
        nfes = [0]

        def fens(r):
            r.then_inc(S["fesS"], 1)
            nfes[0] += 1
            s.wait_ge(S["fesS"], nfes[0])

        for i in range(NT):
            p = i % 2
            q = p * 8
            s.wait_ge(S["bnA"], i + 1)
            s.activation(A_tiny[:, q + 5:q + 6], A_mv[:, p * 2 + 1:p * 2 + 2], AF.Ln, bias=cst[:, 0:1])
            s.activation(A_tiny[:, q + 0:q + 1], A_tiny[:, q + 5:q + 6], AF.Exp, scale=-0.5)
            s.activation(A_tiny[:, q + 2:q + 3], A_mv[:, p * 2:p * 2 + 1], AF.Identity, scale=-1.0)
            s.copy(scs[:, 0:1], A_tiny[:, q + 0:q + 1]).then_inc(S["rsA"], 1)
            for kc in range(32):
                s.wait_ge(S["tpA"], i * 32 + kc + 1)
                s.copy(xsb[:, kc * T + i * 128: kc * T + (i + 1) * 128],
                       PT[kc % 2][:, 0:128]).then_inc(S["txA"], 1)
        s.nop().then_inc(S["pg"], 1)
        s.wait_ge(S["pg"], 5)
        for ncn in range(32):
            if ncn >= 1:
                s.wait_ge(S["mmB"], ncn)
            for kc in range(32):
                s.wait_ge(S["tpB"], ncn * 32 + kc + 1)
                s.copy(B_wT[:, kc * 128:(kc + 1) * 128],
                       PT[kc % 2][:, 0:128]).then_inc(S["txB"], 1)
        s.nop().then_inc(S["pg"], 1)
        s.wait_ge(S["pg"], 10)
        for i in range(NT):
            s.wait_ge(S["bnC"], i + 1)
            s.activation(C_tiny[:, 5:6], C_mv[:, 1:2], AF.Ln, bias=cst[:, 0:1])
            s.activation(C_tiny[:, 0:1], C_tiny[:, 5:6], AF.Exp, scale=-0.5)
            s.activation(C_tiny[:, 2:3], C_mv[:, 0:1], AF.Identity, scale=-1.0)
            s.activation(C_tiny[:, 5:6], C_mv[:, 1:2], AF.Ln, scale=1.0 + EPS, bias=cst[:, 1:2])
            s.activation(C_tiny[:, 1:2], C_tiny[:, 5:6], AF.Exp, scale=-0.5)
            s.activation(C_tiny[:, 3:4], C_mv[:, 0:1], AF.Identity, scale=-1.0)
            s.copy(scs[:, 0:1], C_tiny[:, 0:1]).then_inc(S["rsC"], 1)
            if i >= 1:
                s.wait_ge(S["stC"], 192 * i)
            for kc in range(32):
                s.wait_ge(S["ttC"], i * 96 + 3 * kc + 1)
                s.activation(C_stgH[:, kc * 128:(kc + 1) * 128],
                             PT[kc % 2][:, 0:128], AF.Identity, scale=nwsp[:, kc:kc + 1])
                s.wait_ge(S["ttC"], i * 96 + 3 * kc + 2)
                s.copy(C_stg[:, kc * 128:(kc + 1) * 128], PT[kc % 2][:, 128:256])
                s.wait_ge(S["ttC"], i * 96 + 3 * kc + 3)
                s.copy(C_stg1[:, kc * 128:(kc + 1) * 128],
                       PF[2 + kc % 2][:, 0:128]).then_inc(S["tpC"], 3)
        s.nop().then_inc(S["pg"], 1)
        s.wait_ge(S["pg"], 15)
        isg = 0; icv = 0
        for nb in range(145):
            r = role_d(nb)
            if r == "gate":
                s.wait_ge(S["mmD"], nb + 1)
                if isg >= 1:
                    s.wait_ge(S["stgD"], 16 * isg)
                for (bk, off, sz) in MAINB:
                    j = off - 128
                    rr = s.activation(D_sgb[:, j:j + sz], PF[bk][:], AF.Silu)
                rr.then_inc(S["evD"], 1)
                s.copy(scs[:, 1:2], scs[:, 0:1]).then_inc(S["sgD"], 1)
                isg += 1
            elif r == "hbc":
                cc = nb - 64
                icv += 1
                s.wait_ge(S["cvD"], icv)
                if cc >= 1:
                    s.wait_ge(S["sthD"], 16 * cc)
                for j in range(0, T, 512):
                    sz_ = min(512, T - j)
                    rr = s.activation(D_hcb[:, j:j + sz_], D_acc1[:, j:j + sz_], AF.Silu,
                                      bias=D_cbs[:, cc:cc + 1])
                s.copy(scs[:, 1:2], scs[:, 0:1]).then_inc(S["hcD"], 1)
            else:
                s.wait_ge(S["cvD"], icv + 1)
                s.wait_ge(S["stgD"], 16 * 64)
                for j in range(0, T, 512):
                    sz_ = min(512, T - j)
                    rr = s.activation(D_dts[:, j:j + sz_], D_cvin[:, j:j + sz_], AF.Abs)
                fens(rr)
                for j in range(0, T, 512):
                    sz_ = min(512, T - j)
                    rr = s.activation(D_acc1[:, j:j + sz_], D_dts[:, j:j + sz_], AF.Exp, scale=-1.0)
                fens(rr)
                for j in range(0, T, 512):
                    sz_ = min(512, T - j)
                    s.activation(D_acc0[:, j:j + sz_], D_acc1[:, j:j + sz_], AF.Ln, bias=1.0)
                s.copy(scs[:, 1:2], scs[:, 0:1]).then_inc(S["spD"], 1)
        s.nop().then_inc(S["pg"], 1)
        s.wait_ge(S["pg"], 20)
        for gix in range(NG):
            s.wait_ge(S["uclE"], 16 * 4 * (gix + 1))
            s.activation(E_ell[0:16, 1:2], E_alg[0:16, gix:gix + 1], AF.Exp).then_inc(S["elE"], 1)
            for c in range(NCH2):
                ci = gix * NCH2 + c
                ny = gix * 16 + c
                s.wait_ge(S["btE"], 2 * ci + 1)
                if ci >= 1:
                    s.wait_ge(S["ylhE"], 16 * ci)
                s.copy(E_dtT[:], P5[:, 256:272])
                fens(s.copy(E_acsT[:], P5[:, 272:288]))
                s.copy(scs[:, 1:2], scs[:, 0:1]).then_inc(S["dttE"], 1)
                s.wait_ge(S["yaE"], gix * 16 + c - 1 if c >= 1 else gix * 16)
                s.activation(E_EA[:], E_acsT[:], AF.Exp).then_inc(S["actE"], 1)
                if c <= NCH2 - 2:
                    s.wait_ge(S["dgvE"], gix * 16 + c)
                    s.activation(E_ell[0:16, 0:1], E_acs2[0:16, 127:128],
                                 AF.Exp).then_inc(S["elE"], 1)
                if c >= 1:
                    s.wait_ge(S["ebmE"], gix * 16 + c)
                    s.wait_ge(S["pvE"], ci)
                    s.copy(E_Ebc[:], P5[:, 288:304]).then_inc(S["ebeE"], 1)
                for h in range(16):
                    hp = h % 2
                    s.wait_ge(S["ltmE"], ci * 16 + h + 1)
                    s.activation(E_LTe[:, hp * 128:(hp + 1) * 128],
                                 E_LTm[:, hp * 128:(hp + 1) * 128], AF.Exp).then_inc(S["lteE"], 1)
                    s.wait_ge(S["htE"], 16 * ci + h + 1)
                    s.wait_ge(S["dttE"], ci + 1)
                    s.activation(E_hdtp[:, h * 64:(h + 1) * 64], PT[hp][:, 0:64],
                                 AF.Identity, scale=E_dtT[:, h:h + 1])
                    s.copy(E_hcpp[:, h * 64:(h + 1) * 64], PT[hp][:, 0:64]).then_inc(S["hdeE"], 1)
                if c >= 1:
                    s.wait_ge(S["ygE"], ny)
                    s.activation(E_y2b[:], E_ygb[:], AF.Square, accum_out=E_ssq[:])
                    fens(s.activation(E_rgs[:, 1:2], E_ssq[:], AF.Ln, scale=1.0 / 1024.0,
                                      bias=cst[:, 0:1]))
                    s.activation(E_rgs[:, 0:1], E_rgs[:, 1:2], AF.Exp, scale=-0.5).then_inc(S["sqE"], 1)
                    if ny >= 2:
                        s.wait_ge(S["ystE0"], 16 * (ny - 1))
                    for r8 in range(8):
                        s.wait_ge(S["ytE"], 8 * (ny - 1) + r8 + 1)
                        s.copy(E_ygT[:, r8 * 128:(r8 + 1) * 128],
                               PT[r8 % 2][:, 128:256]).then_inc(S["ytxE"], 1)
        s.nop().then_inc(S["pg"], 1)
        s.wait_ge(S["pg"], 25)
        for ncn in range(32):
            for kc in range(32):
                s.wait_ge(S["tpF"], FB["tpF"] + ncn * 32 + kc + 1)
                s.copy(F_w1T[:, kc * 128:(kc + 1) * 128], PT[kc % 2][:, 0:128]).then_inc(S["txF"], 1)
        for th in range(2):
            for nb in range(32):
                i = th * 32 + nb
                for r8 in range(8):
                    s.wait_ge(S["otF"], FB["otF"] + 8 * i + r8 + 1)
                    s.copy(F_otb[:, r8 * 128:(r8 + 1) * 128],
                           PF[4 + r8 % 2][:, 0:128]).then_inc(S["otxF"], 1)
        s.nop().then_inc(S["pg"], 1)

    @block.tensor
    def _(t):
        t.wait_ge(S["cldA"], 16 * 9)
        for i in range(NT):
            p = i % 2
            t.wait_ge(S["afA"], i + 1)
            for kc in range(32):
                if i * 32 + kc >= 2:
                    t.wait_ge(S["txA"], i * 32 + kc - 1)
                t.transpose(PT[kc % 2][:, 0:128],
                            A_xnt[:, p * D + kc * 128: p * D + (kc + 1) * 128],
                            idbs[:]).then_inc(S["tpA"], 1)
        t.nop().then_inc(S["pg"], 1)
        t.wait_ge(S["pg"], 5)
        for tt in range(32):
            t.wait_ge(S["mrdB"], tt + 1)
            t.wait_ge(S["eldB0" if tt % 2 == 0 else "eldB1"], 16 * (tt // 2 + 1))
            t.matmul(P5[:, 0:128], B_esb[:, (tt % 2) * 128:(tt % 2 + 1) * 128],
                     B_mred[:, (tt % 2) * 128:(tt % 2 + 1) * 128],
                     start=(tt == 0), stop=(tt == 31)).then_inc(S["msumB"], 1)
        for ncn in range(32):
            pw = ncn % 2
            t.wait_ge(S["bmB"], 1)
            t.wait_ge(S["pldB"], 16 * (ncn + 1))
            if ncn >= 1:
                t.wait_ge(S["wsB"], ncn)
            t.matmul(P5[:, 0:128], B_psl[:], B_bm[:], start=True, stop=True).then_inc(S["mxB"], 1)
            t.wait_ge(S["wsB"], ncn + 1)
            for kc in range(32):
                if ncn * 32 + kc >= 2:
                    t.wait_ge(S["txB"], ncn * 32 + kc - 1)
                t.transpose(PT[kc % 2][:, 0:128], B_wsum[:, kc * 128:(kc + 1) * 128],
                            idbs[:]).then_inc(S["tpB"], 1)
            t.wait_ge(S["txB"], 32 * (ncn + 1))
            if ncn >= 1:
                t.wait_ge(S["evB"], ncn)
            for kc in range(32):
                for tb, (off, sz) in enumerate([(0, 512), (512, 512), (1024, 512), (1536, 512), (2048, 128)]):
                    r = t.matmul(PF[tb][:, 0:sz],
                                 B_wT[:, kc * 128:(kc + 1) * 128],
                                 xsb[:, kc * T + off: kc * T + off + sz],
                                 start=(kc == 0), stop=(kc == 31))
            r.then_inc(S["mmB"], 1)
        t.nop().then_inc(S["pg"], 1)
        t.wait_ge(S["pg"], 10)
        for i in range(NT):
            p = i % 2
            t.wait_ge(S["o0C0" if p == 0 else "o0C1"], 16 * 4 * (i // 2 + 1))
            for ncx in range(32):
                if i * 32 + ncx >= 2:
                    t.wait_ge(S["adC"], i * 32 + ncx - 1)
                t.transpose(PF[ncx % 2][:, 0:128],
                            C_o0c[:, p * D + ncx * 128: p * D + (ncx + 1) * 128]
                            .bitcast(F32) if False else C_o0c[:, p * D + ncx * 128: p * D + (ncx + 1) * 128],
                            idfs[:]).then_inc(S["toC"], 1)
            t.wait_ge(S["afC"], i + 1)
            for kc in range(32):
                base = i * 96 + 3 * kc
                if base >= 4:
                    t.wait_ge(S["tpC"], base - 3)
                t.transpose(PT[kc % 2][:, 0:128], C_ht[:, kc * 128:(kc + 1) * 128],
                            idbs[:]).then_inc(S["ttC"], 1)
                t.transpose(PT[kc % 2][:, 128:256], C_xnt[:, kc * 128:(kc + 1) * 128],
                            idbs[:]).then_inc(S["ttC"], 1)
                t.transpose(PF[2 + kc % 2][:, 0:128], C_x1[:, kc * 128:(kc + 1) * 128],
                            idfs[:]).then_inc(S["ttC"], 1)
        t.nop().then_inc(S["pg"], 1)
        t.wait_ge(S["pg"], 15)
        t.wait_ge(S["txA"], 544 + 16 * 64)
        for nb in range(145):
            p = nb % 2
            r_ = role_d(nb)
            t.wait_ge(S["wD0" if p == 0 else "wD1"], 16 * (nb // 2 + 1))
            if nb >= 1:
                t.wait_ge(S["evD"], nb)
            blocks = ([HALOB] + MAINB) if r_ in ("hbc", "dt") else MAINB
            for kc in range(32):
                lhsT = D_wsb[:, p * 4096 + kc * 128: p * 4096 + (kc + 1) * 128]
                for (bk, off, sz) in blocks:
                    psx = P4 if bk == 4 else PF[bk]
                    r = t.matmul(psx[:, 0:sz], lhsT,
                                 hfm[:, kc * T + off: kc * T + off + sz],
                                 start=(kc == 0), stop=(kc == 31))
            r.then_inc(S["mmD"], 1)
        t.nop().then_inc(S["pg"], 1)
        t.wait_ge(S["pg"], 20)
        for gix in range(NG):
            t.wait_ge(S["uclE"], 16 * 4 * (gix + 1))
            for c in range(NCH2):
                ci = gix * NCH2 + c
                cp = c % 2
                p = ci % 2
                ny = gix * 16 + c
                t.wait_ge(S["acsE"], ci + 1)
                if ci >= 1:
                    t.wait_ge(S["ltmE"], 16 * ci)
                    t.wait_ge(S["dttE"], ci)
                    t.wait_ge(S["ebeE"], gix * 16 + c - 1 if c >= 1 else gix * 16)
                    if ny >= 1 and c >= 1:
                        t.wait_ge(S["yevE"], ny - 1)
                    elif gix >= 1:
                        t.wait_ge(S["yevE"], gix * 16)
                t.transpose(P5[:, 256:272], E_dtu[0:16, c * 128:(c + 1) * 128], idfs[0:16, 0:16])
                t.transpose(P5[:, 272:288], E_acs2[0:16, :], idfs[0:16, 0:16]).then_inc(S["btE"], 1)
                t.wait_ge(S["hlE0" if p == 0 else "hlE1"], 16 * (ci // 2 + 1))
                if ci >= 1:
                    t.wait_ge(S["hdeE"], 16 * ci)
                    t.wait_ge(S["ylhE"], 16 * ci)
                    t.wait_ge(S["btvE"], ci)
                    if ny >= 1:
                        t.wait_ge(S["ytxE"], 8 * (ny - 1 if c >= 1 else ny))
                t.transpose(PT0[:, 0:128], E_Bb[:, c * 128:(c + 1) * 128], idbs[:])
                if c >= 1:
                    if ny >= 2:
                        t.wait_ge(S["mskE"], ny - 1)
                        t.wait_ge(S["yaE"], ny - 1)
                    t.wait_ge(S["steE"], ci)
                    t.matmul(P4[:, 0:128], E_Bb[:, c * 128:(c + 1) * 128],
                             E_Cb[:, c * 128:(c + 1) * 128], start=True, stop=True).then_inc(S["mtE"], 1)
                    if ny >= 2:
                        t.wait_ge(S["sgtE"], ny - 1)
                    t.wait_ge(S["slE0" if ny % 2 == 0 else "slE1"],
                              16 * ((ny - 1) // 2 + 1))
                    for r8 in range(8):
                        rr = t.transpose(PT1[:, r8 * 128:(r8 + 1) * 128],
                                         E_sgc[:, (ny % 2) * 1024 + r8 * 128:
                                               (ny % 2) * 1024 + (r8 + 1) * 128],
                                         idbs[:])
                    rr.then_inc(S["btE"], 1)
                else:
                    t.nop().then_inc(S["btE"], 1)
                if c >= 1:
                    t.wait_ge(S["dgvE"], gix * 16 + c)
                    if gix * 16 + c >= 2:
                        t.wait_ge(S["ebeE"], gix * 16 + c - 1)
                        t.wait_ge(S["pvE"], ci)
                    t.matmul(P5[:, 288:304], ons16[:], E_diag[0:16, :],
                             start=True, stop=True).then_inc(S["ebmE"], 1)
                t.wait_ge(S["dmlE"], 16 + 32 * (ci + 1))
                t.wait_ge(S["dttE"], ci + 1)
                for h in range(16):
                    hp = h % 2
                    if ci * 16 + h >= 2:
                        t.wait_ge(S["ltmE"], ci * 16 + h - 1)
                    t.matmul(psDm[hp][:, 0:128], E_dop[0:2, h * 128:(h + 1) * 128],
                             E_dop[0:2, 2048 + h * 128: 2048 + (h + 1) * 128],
                             start=True, stop=True).then_inc(S["dmmE"], 1)
                    if ci * 16 + h >= 2:
                        t.wait_ge(S["hdeE"], ci * 16 + h - 1)
                        t.wait_ge(S["ylhE"], ci * 16 + h - 1)
                    if h == 0:
                        t.wait_ge(S["btvE"], ci + 1)
                    if h == 1 and c >= 1:
                        t.wait_ge(S["sgtE"], ny)
                    t.transpose(PT[hp][:, 0:64],
                                hid3E[(h % 2) * 64:(h % 2) * 64 + 64, p, h // 2, :],
                                idb64s[(h % 2) * 64:(h % 2) * 64 + 64, :]).then_inc(S["htE"], 1)
                    if c >= 1:
                        t.wait_ge(S["ylhE"], ci * 16 + h + 1)
                        t.wait_ge(S["hdeE"], ci * 16 + h + 1)
                        if h == 0:
                            t.wait_ge(S["yevE"], ny - 1)
                            t.wait_ge(S["mskE"], ny)
                        if h == 8:
                            t.wait_ge(S["ebeE"], gix * 16 + c)
                        rydm = t.matmul((P4 if h < 8 else P5)[:, (h % 8) * 64:(h % 8) * 64 + 64],
                                        E_Ylh[:, hp * 128:(hp + 1) * 128],
                                        E_hdtp[:, h * 64:(h + 1) * 64],
                                        start=True, stop=True)
                        if h == 15:
                            rydm.then_inc(S["ydE"], 1)
                t.wait_ge(S["ylhE"], 16 * (ci + 1))
                t.wait_ge(S["btvE"], ci + 1)
                if ci >= 1:
                    t.wait_ge(S["steE"], ci)
                if ny >= 1 and c >= 1:
                    t.wait_ge(S["yaE"], ny - 1)
                elif gix >= 1:
                    t.wait_ge(S["yaE"], gix * 16)
                t.matmul(P0[:], E_BTs[:], E_srhp[:, 0:512], start=True, stop=True)
                t.matmul(P1[:], E_BTs[:], E_srhp[:, 512:1024], start=True, stop=True).then_inc(S["smE"], 2)
                if c >= 1:
                    t.wait_ge(S["corE"], ny)
                    t.wait_ge(S["steE"], ci + 1)
                    t.matmul(P0[:], E_Cb[:, c * 128:(c + 1) * 128], E_prevbf[:, 0:512],
                             start=True, stop=True)
                    t.matmul(P1[:], E_Cb[:, c * 128:(c + 1) * 128], E_prevbf[:, 512:1024],
                             start=True, stop=True).then_inc(S["yoE"], 2)
                    t.wait_ge(S["ybfE"], ny)
                    if ny >= 2:
                        t.wait_ge(S["ytxE"], 8 * (ny - 1))
                    for r8 in range(8):
                        if r8 >= 2:
                            t.wait_ge(S["ytxE"], 8 * (ny - 1) + r8 - 1)
                        t.transpose(PT[r8 % 2][:, 128:256],
                                    E_ybf[:, r8 * 128:(r8 + 1) * 128],
                                    idbs[:]).then_inc(S["ytE"], 1)
        t.nop().then_inc(S["pg"], 1)
        t.wait_ge(S["pg"], 25)
        for ncn in range(32):
            t.wait_ge(S["dqF"], FB["dqF"] + ncn + 1)
            if ncn >= 1:
                t.wait_ge(S["w1stF"], FB["w1stF"] + 16 * ncn)
            for kc in range(32):
                if ncn * 32 + kc >= 2:
                    t.wait_ge(S["txF"], FB["txF"] + ncn * 32 + kc - 1)
                t.transpose(PT[kc % 2][:, 0:128], F_dq[:, kc * 128:(kc + 1) * 128],
                            idbs[:]).then_inc(S["tpF"], 1)
        for th in range(2):
            t.wait_ge(S["gnfF"], FB["gnfF"] + th + 1)
            for nb in range(32):
                i = th * 32 + nb
                p = i % 2
                t.wait_ge(S["wlF0" if p == 0 else "wlF1"], FB["wlF0" if p == 0 else "wlF1"] + 16 * (i // 2 + 1))
                if i >= 1:
                    t.wait_ge(S["evaF"], FB["evaF"] + i)
                for kc in range(64):
                    lhsT = F_wsb[:, p * 8192 + kc * 128: p * 8192 + (kc + 1) * 128]
                    t.matmul(P0[:], lhsT, F_ygh[:, kc * 1024: kc * 1024 + 512],
                             start=(kc == 0), stop=(kc == 63))
                    r = t.matmul(P1[:], lhsT, F_ygh[:, kc * 1024 + 512: kc * 1024 + 1024],
                                 start=(kc == 0), stop=(kc == 63))
                r.then_inc(S["mmaF"], 1)
        for th in range(2):
            t.wait_ge(S["xnlF"], FB["xnlF"] + 16 * 32 * (th + 1))
            for nb in range(32):
                i = th * 32 + nb
                p = i % 2
                t.wait_ge(S["w1lF0" if p == 0 else "w1lF1"], FB["w1lF0" if p == 0 else "w1lF1"] + 16 * (i // 2 + 1))
                if i >= 1:
                    t.wait_ge(S["evfF"], FB["evfF"] + i)
                for kc in range(32):
                    lhsT = F_w1sb[:, p * 4096 + kc * 128: p * 4096 + (kc + 1) * 128]
                    t.matmul(P2[:], lhsT, F_xnh[:, kc * 1024: kc * 1024 + 512],
                             start=(kc == 0), stop=(kc == 31))
                    r = t.matmul(P3[:], lhsT, F_xnh[:, kc * 1024 + 512: kc * 1024 + 1024],
                                 start=(kc == 0), stop=(kc == 31))
                r.then_inc(S["mmfF"], 1)
                t.wait_ge(S["evfF"], FB["evfF"] + i + 1)
                if i >= 1:
                    t.wait_ge(S["otxF"], FB["otxF"] + 8 * i)
                for r8 in range(8):
                    if r8 >= 2:
                        t.wait_ge(S["otxF"], FB["otxF"] + 8 * i + r8 - 1)
                    t.transpose(PF[4 + r8 % 2][:, 0:128],
                                F_oa[:, p * 1024 + r8 * 128: p * 1024 + (r8 + 1) * 128],
                                idfs[:]).then_inc(S["otF"], 1)
        t.nop().then_inc(S["pg"], 1)


def kernel_fused(L0_attn_data, L0_attn_weight, L0_attn_mask, L0_ffn_weight_q,
                 L0_ffn_scales_and_zeros, L1_attn_norm_w, L1_attn_in_proj_w,
                 L1_attn_conv_w, L1_attn_conv_b, L1_attn_dt_bias, L1_attn_A_log,
                 L1_attn_Dp, L1_attn_gnorm_w, L1_attn_out_proj_w, L1_ffn_qweight,
                 L1_ffn_weight_scale, L1_ffn_weight_zero_point, L1_ffn_bias):
    x0 = np.ascontiguousarray(np.asarray(L0_attn_data, np.float32).reshape(16384, D))
    Ebm, Pbm, identb = _make_l0_consts()
    szr = _stage_szr(L0_ffn_scales_and_zeros)
    WinT = _stage_proj_w(L1_attn_in_proj_w)
    cw, cb, nw, db = _stage_conv(L1_attn_conv_w, L1_attn_conv_b, L1_attn_norm_w,
                                 L1_attn_dt_bias)
    OutWT, A1, ws1, wz1, b1f, gnwf = _stage_final(
        L1_attn_out_proj_w, L1_ffn_qweight, L1_ffn_weight_scale,
        L1_ffn_weight_zero_point, L1_ffn_bias, L1_attn_gnorm_w)
    A_log = np.asarray(L1_attn_A_log, np.float32)
    Dp = np.asarray(L1_attn_Dp, np.float32)
    alog8 = np.ascontiguousarray(A_log.reshape(8, 16).T)
    dpb = np.tile(Dp, (128, 1)).astype(np.float32)
    idf = np.eye(128, dtype=np.float32)
    triu = np.triu(np.ones((128, 128), np.float32))
    ones16 = np.ones((16, 128), np.float32)
    i16 = np.eye(16, dtype=np.float32)
    idb64 = np.vstack([np.eye(64, dtype=BF16NP)] * 2)
    negrow = np.vstack([np.zeros((1, 2048), np.float32), -np.ones((1, 2048), np.float32)])
    Wnk = np.asarray(L0_attn_weight, np.float32)
    Qnk = np.asarray(L0_ffn_weight_q, np.int32)
    maskA = np.asarray(L0_attn_mask, np.int32)

    common = dict(Wnk=Wnk, Qnk=Qnk, maskA=maskA, szr=szr, Ebm=Ebm, Pbm=Pbm,
                  WinT=WinT, nwf=nw, convw=cw, convb=cb, dtb=db, alog8=alog8,
                  dpb=dpb, OutWT=OutWT, A1=A1, ws1=ws1, wz1=wz1, bias1f=b1f,
                  gnwf=gnwf, identb=identb, idf=idf, triu=triu, ones16=ones16,
                  i16=i16, idb64=idb64, negrow=negrow)
    in_maps = []
    for c in range(NCORES):
        b, hb = divmod(c, 2)
        base = b * S + hb * TC
        if hb == 0:
            halo = np.zeros((128, D), np.float32)
        else:
            halo = x0[base - 128: base]
        x0s = np.ascontiguousarray(np.concatenate([halo, x0[base: base + TC]], axis=0))
        smask = np.full((128, 1), 1.0 if hb == 1 else 0.0, np.float32)
        partner = (c - 1) % 8
        sgidx = (partner * 1024 + np.arange(8)[None, :] * 128
                 + np.arange(128)[:, None]).astype(np.int32)
        ins = dict(common)
        ins.update(x0s=x0s, smask=smask, sgidx=np.ascontiguousarray(sgidx))
        in_maps.append(ins)
    nc = build_fused()
    res = run_bass_kernel_spmd(nc, in_maps, core_ids=list(range(NCORES)))
    out = np.concatenate([res.results[c]["out"] for c in range(NCORES)], axis=0)
    return np.ascontiguousarray(out.reshape(B_SZ, S, D))


# version 06 (final)
# speedup vs baseline: 1.1206x; 1.1206x over previous
import numpy as np
import ml_dtypes
from contextlib import ExitStack
import concourse.bass as bass
import concourse.mybir as mybir
from concourse.bass_utils import run_bass_kernel_spmd
from concourse.bass_core_rust import IndirectOffsetOnAxis

F32 = mybir.dt.float32
BF16 = mybir.dt.bfloat16
I32 = mybir.dt.int32
AF = mybir.ActivationFunctionType
ALU = mybir.AluOpType
BF16NP = ml_dtypes.bfloat16

NCORES = 8
D = 4096
B_SZ = 4
S = 4096
TC = 2048
EPS = 1e-5


def build_ln_launch(with_rms: bool, ntiles: int = TC // 128):
    nc = bass.Bass("TRN2", target_bir_lowering=False)
    xa = nc.dram_tensor("xa", [ntiles * 128, D], F32, kind="ExternalInput").ap()
    if with_rms:
        xb = nc.dram_tensor("xb", [ntiles * 128, D], F32, kind="ExternalInput").ap()
        xs = nc.dram_tensor("xs", [ntiles * 128, D], F32, kind="ExternalOutput").ap()
        h = nc.dram_tensor("h", [ntiles * 128, D], BF16, kind="ExternalOutput").ap()
    xn = nc.dram_tensor("xn", [ntiles * 128, D], BF16, kind="ExternalOutput").ap()

    nst = 3 if with_rms else 1
    nld = 2 if with_rms else 1
    naff = 2 if with_rms else 1

    with (
        nc.sbuf_tensor("xin", [128, 2 * D], F32) as xin,
        nc.sbuf_tensor("xbin", [128, (2 * D) if with_rms else 1], F32) as xbin,
        nc.sbuf_tensor("xnt", [128, 2 * D], BF16) as xnt,
        nc.sbuf_tensor("ht", [128, (2 * D) if with_rms else 1], BF16) as ht,
        nc.sbuf_tensor("stats", [128, 2 * 48], F32) as stats,
        nc.sbuf_tensor("mv", [128, 2 * 2], F32) as mv,
        nc.sbuf_tensor("tiny", [128, 16], F32) as tiny,
        nc.sbuf_tensor("cst", [128, 2], F32) as cst,
        nc.semaphore("s_fen") as s_fen,
        nc.semaphore("s_ld0") as s_ld0,
        nc.semaphore("s_ld1") as s_ld1,
        nc.semaphore("s_bn") as s_bn,
        nc.semaphore("s_rs") as s_rs,
        nc.semaphore("s_aff") as s_aff,
        nc.semaphore("s_st0") as s_st0,
        nc.semaphore("s_st1") as s_st1,
        nc.Block() as block,
    ):
        s_ld = [s_ld0, s_ld1]
        s_st = [s_st0, s_st1]

        @block.sync
        def _(sync):
            for i in range(ntiles):
                p = i % 2
                if i >= 2:
                    sync.wait_ge(s_aff, naff * (i - 1))
                    if with_rms:
                        sync.wait_ge(s_st[p], 16 * nst * (i // 2))
                sync.dma_start(out=xin[:, p * D:(p + 1) * D],
                               in_=xa[i * 128:(i + 1) * 128, :]).then_inc(s_ld[p], 16)
                if with_rms:
                    sync.dma_start(out=xbin[:, p * D:(p + 1) * D],
                                   in_=xb[i * 128:(i + 1) * 128, :]).then_inc(s_ld[p], 16)

        @block.vector
        def _(v):
            nfen = [0]
            v.memset(cst[:, 0:1], EPS)
            v.memset(cst[:, 1:2], EPS * EPS).then_inc(s_fen, 1)
            nfen[0] += 1
            v.wait_ge(s_fen, nfen[0])
            for i in range(ntiles):
                p = i % 2
                q = p * 8
                v.wait_ge(s_ld[p], 16 * nld * (i // 2 + 1))
                if i >= 2:
                    v.wait_ge(s_rs, i - 1)
                if with_rms:
                    v.tensor_add(xin[:, p * D:(p + 1) * D], xin[:, p * D:(p + 1) * D],
                                 xbin[:, p * D:(p + 1) * D]).then_inc(s_fen, 1)
                    nfen[0] += 1
                    v.wait_ge(s_fen, nfen[0])
                for j in range(8):
                    r = v.bn_stats(stats[:, p * 48 + j * 6: p * 48 + (j + 1) * 6],
                                   xin[:, p * D + j * 512: p * D + (j + 1) * 512])
                r.then_inc(s_fen, 1)
                nfen[0] += 1
                v.wait_ge(s_fen, nfen[0])
                v.bn_aggr(mv[:, p * 2:(p + 1) * 2], stats[:, p * 48:(p + 1) * 48]).then_inc(s_bn, 1)
                v.wait_ge(s_rs, i + 1)
                if i >= 2:
                    v.wait_ge(s_st[p], 16 * nst * (i // 2))
                r = v.tensor_scalar(xnt[:, p * D:(p + 1) * D], xin[:, p * D:(p + 1) * D],
                                    tiny[:, q + 2:q + 3], tiny[:, q + 0:q + 1],
                                    ALU.add, ALU.mult)
                if with_rms:
                    r = v.tensor_scalar(ht[:, p * D:(p + 1) * D], xin[:, p * D:(p + 1) * D],
                                        tiny[:, q + 2:q + 3], tiny[:, q + 1:q + 2],
                                        ALU.add, ALU.mult)
                r.then_inc(s_aff, naff)

        @block.scalar
        def _(s):
            for i in range(ntiles):
                p = i % 2
                q = p * 8
                s.wait_ge(s_bn, i + 1)
                s.activation(tiny[:, q + 5:q + 6], mv[:, p * 2 + 1: p * 2 + 2], AF.Ln, bias=cst[:, 0:1])
                s.activation(tiny[:, q + 0:q + 1], tiny[:, q + 5:q + 6], AF.Exp, scale=-0.5)
                s.activation(tiny[:, q + 2:q + 3], mv[:, p * 2: p * 2 + 1], AF.Identity, scale=-1.0)
                if with_rms:
                    s.activation(tiny[:, q + 5:q + 6], mv[:, p * 2 + 1: p * 2 + 2],
                                 AF.Ln, scale=1.0 + EPS, bias=cst[:, 1:2])
                    s.activation(tiny[:, q + 1:q + 2], tiny[:, q + 5:q + 6], AF.Exp, scale=-0.5)
                s.copy(tiny[:, q + 7:q + 8], tiny[:, q + 0:q + 1]).then_inc(s_rs, 1)

        @block.gpsimd
        def _(g):
            for i in range(ntiles):
                p = i % 2
                g.wait_ge(s_aff, naff * (i + 1))
                g.dma_start(out=xn[i * 128:(i + 1) * 128, :],
                            in_=xnt[:, p * D:(p + 1) * D]).then_inc(s_st[p], 16)
                if with_rms:
                    g.dma_start(out=h[i * 128:(i + 1) * 128, :],
                                in_=ht[:, p * D:(p + 1) * D]).then_inc(s_st[p], 16)
                    g.dma_start(out=xs[i * 128:(i + 1) * 128, :],
                                in_=xin[:, p * D:(p + 1) * D]).then_inc(s_st[p], 16)
            g.wait_ge(s_st0, 16 * nst * ((ntiles + 1) // 2))
            g.wait_ge(s_st1, 16 * nst * (ntiles // 2))
    return nc



def build_l0_launch(T: int = TC, debug=False):
    assert T % 512 == 0
    TB = T // 512
    nc = bass.Bass("TRN2", target_bir_lowering=False)
    if debug:
        dbm = nc.dram_tensor("dbm", [128, 128], F32, kind="ExternalOutput").ap()
        dws = nc.dram_tensor("dws", [32, 128, D], BF16, kind="ExternalOutput").ap()
    xn0f = nc.dram_tensor("xn0f", [D, T], BF16, kind="ExternalInput").ap()
    Wnk = nc.dram_tensor("Wnk", [D, D], F32, kind="ExternalInput").ap()
    Qnk = nc.dram_tensor("Qnk", [D, D], I32, kind="ExternalInput").ap()
    maskA = nc.dram_tensor("maskA", [D, D], I32, kind="ExternalInput").ap()
    szr = nc.dram_tensor("szr", [128, 32 * 32 * 2], F32, kind="ExternalInput").ap()
    Ebm = nc.dram_tensor("Ebm", [128, 32 * 128], BF16, kind="ExternalInput").ap()
    Pbm = nc.dram_tensor("Pbm", [128, 32 * 128], BF16, kind="ExternalInput").ap()
    identb = nc.dram_tensor("identb", [128, 128], BF16, kind="ExternalInput").ap()
    o0f = nc.dram_tensor("o0f", [D, T], F32, kind="ExternalOutput").ap()

    with ExitStack() as ctx:
        xsb = ctx.enter_context(nc.sbuf_tensor("xsb", [128, 32 * T], BF16))
        wbf = ctx.enter_context(nc.sbuf_tensor("wbf", [128, 2 * D], BF16))
        qbf = ctx.enter_context(nc.sbuf_tensor("qbf", [128, 2 * D], BF16))
        deq = ctx.enter_context(nc.sbuf_tensor("deq", [128, D], BF16))
        wsum = ctx.enter_context(nc.sbuf_tensor("wsum", [128, D], BF16))
        wT = ctx.enter_context(nc.sbuf_tensor("wT", [128, 2 * D], BF16))
        szc = ctx.enter_context(nc.sbuf_tensor("szc", [128, 2 * 64], F32))
        zpc = ctx.enter_context(nc.sbuf_tensor("zpc", [128, 2 * 32], F32))
        esb = ctx.enter_context(nc.sbuf_tensor("esb", [128, 2 * 128], BF16))
        idb = ctx.enter_context(nc.sbuf_tensor("idb", [128, 128], BF16))
        mred = ctx.enter_context(nc.sbuf_tensor("mred", [128, 2 * 128], BF16))
        bm = ctx.enter_context(nc.sbuf_tensor("bm", [128, 128], BF16))
        psl = ctx.enter_context(nc.sbuf_tensor("psl", [128, 128], BF16))
        osb = ctx.enter_context(nc.sbuf_tensor("osb", [128, T], F32))
        psA = ctx.enter_context(nc.psum_tensor("psA", [128, 512], F32))
        psB = ctx.enter_context(nc.psum_tensor("psB", [128, 512], F32))
        psC = ctx.enter_context(nc.psum_tensor("psC", [128, 512], F32))
        psD = ctx.enter_context(nc.psum_tensor("psD", [128, 512], F32))
        psM = ctx.enter_context(nc.psum_tensor("psM", [128, 512], F32))
        psT0 = ctx.enter_context(nc.psum_tensor("psT0", [128, 1024], BF16))
        psT1 = ctx.enter_context(nc.psum_tensor("psT1", [128, 1024], BF16))
        s_cld = ctx.enter_context(nc.semaphore("s_cld"))
        s_fen = ctx.enter_context(nc.semaphore("s_fen"))
        s_mld0 = ctx.enter_context(nc.semaphore("s_mld0"))
        s_mld1 = ctx.enter_context(nc.semaphore("s_mld1"))
        s_mrd = ctx.enter_context(nc.semaphore("s_mrd"))
        s_msum = ctx.enter_context(nc.semaphore("s_msum"))
        s_bm = ctx.enter_context(nc.semaphore("s_bm"))
        s_pld = ctx.enter_context(nc.semaphore("s_pld"))
        s_eld0 = ctx.enter_context(nc.semaphore("s_eld0"))
        s_eld1 = ctx.enter_context(nc.semaphore("s_eld1"))
        s_szl0 = ctx.enter_context(nc.semaphore("s_szl0"))
        s_szl1 = ctx.enter_context(nc.semaphore("s_szl1"))
        s_wld0 = ctx.enter_context(nc.semaphore("s_wld0"))
        s_wld1 = ctx.enter_context(nc.semaphore("s_wld1"))
        s_mx = ctx.enter_context(nc.semaphore("s_mx"))
        s_wsum = ctx.enter_context(nc.semaphore("s_wsum"))
        s_tp = ctx.enter_context(nc.semaphore("s_tp"))
        s_tx = ctx.enter_context(nc.semaphore("s_tx"))
        s_mm = ctx.enter_context(nc.semaphore("s_mm"))
        s_ev = ctx.enter_context(nc.semaphore("s_ev"))
        s_st = ctx.enter_context(nc.semaphore("s_st"))
        if debug:
            dbmb = ctx.enter_context(nc.sbuf_tensor("dbmb", [128, 128], F32))
            s_dbw = ctx.enter_context(nc.semaphore("s_dbw"))
            s_dbg = ctx.enter_context(nc.semaphore("s_dbg"))
        block = ctx.enter_context(nc.Block())
        psT = [psT0, psT1]
        psMM = [psA, psB, psC, psD]
        s_mld = [s_mld0, s_mld1]
        s_eld = [s_eld0, s_eld1]
        s_szl = [s_szl0, s_szl1]
        s_wld = [s_wld0, s_wld1]

        @block.sync
        def _(sync):
            sync.dma_start(out=idb[:], in_=identb[:]).then_inc(s_cld, 16)
            for kc in range(32):
                sync.dma_start(out=xsb[:, kc * T:(kc + 1) * T],
                               in_=xn0f[kc * 128:(kc + 1) * 128, :]).then_inc(s_cld, 16)
            for t in range(32):
                pe = t % 2
                if t >= 2:
                    sync.wait_ge(s_msum, t - 1)
                sync.dma_start(out=esb[:, pe * 128:(pe + 1) * 128],
                               in_=Ebm[:, t * 128:(t + 1) * 128]).then_inc(s_eld[pe], 16)
            if debug:
                sync.wait_ge(s_dbw, 1)
                sync.dma_start(out=dbm[:], in_=dbmb[:]).then_inc(s_st, 16)
            for ncn in range(32):
                if debug:
                    sync.wait_ge(s_wsum, ncn + 1)
                    sync.dma_start(out=dws[ncn], in_=wsum[:]).then_inc(s_dbg, 16)
                sync.wait_ge(s_ev, ncn + 1)
                sync.dma_start(out=o0f[ncn * 128:(ncn + 1) * 128, :],
                               in_=osb[:]).then_inc(s_st, 16)
            sync.wait_ge(s_st, 16 * (32 + 1 if debug else 32))

        @block.gpsimd
        def _(g):
            for t in range(32):
                p = t % 2
                if t >= 2:
                    g.wait_ge(s_mrd, t - 1)
                g.dma_start(out=wbf[:, p * D:(p + 1) * D],
                            in_=maskA[t * 128:(t + 1) * 128, :]).then_inc(s_mld[p], 16)
            for ncn in range(32):
                p = ncn % 2
                if ncn >= 1:
                    g.wait_ge(s_mx, ncn)
                g.dma_start(out=psl[:], in_=Pbm[:, ncn * 128:(ncn + 1) * 128]).then_inc(s_pld, 16)
                if ncn >= 2:
                    g.wait_ge(s_wsum, ncn - 1)
                g.dma_start(out=szc[:, p * 64:(p + 1) * 64],
                            in_=szr[:, ncn * 64:(ncn + 1) * 64]).then_inc(s_szl[p], 16)
                if ncn == 0:
                    g.wait_ge(s_bm, 1)
                if ncn >= 2:
                    g.wait_ge(s_wsum, ncn - 1)
                g.dma_start(out=wbf[:, p * D:(p + 1) * D],
                            in_=Wnk[ncn * 128:(ncn + 1) * 128, :]).then_inc(s_wld[p], 16)
                g.dma_start(out=qbf[:, p * D:(p + 1) * D],
                            in_=Qnk[ncn * 128:(ncn + 1) * 128, :]).then_inc(s_wld[p], 16)

        @block.vector
        def _(v):
            nfen = [0]

            def fence(r):
                r.then_inc(s_fen, 1)
                nfen[0] += 1
                v.wait_ge(s_fen, nfen[0])
            for t in range(32):
                p = t % 2
                v.wait_ge(s_mld[p], 16 * (t // 2 + 1))
                if t >= 2:
                    v.wait_ge(s_msum, t - 1)
                with nc.allow_low_precision(reason="mask 0/1 sums<=32 exact in bf16"):
                    v.reduce_sum(mred[:, p * 128:(p + 1) * 128],
                                 wbf[:, p * D:(p + 1) * D].rearrange("p (a b) -> p a b", b=32),
                                 axis=mybir.AxisListType.X).then_inc(s_mrd, 1)
            v.wait_ge(s_msum, 32)
            v.tensor_scalar(bm[:], psM[:, 0:128], 0.0, None, ALU.is_gt).then_inc(s_bm, 1)
            if debug:
                v.tensor_copy(dbmb[:], bm[:]).then_inc(s_dbw, 1)
            for ncn in range(32):
                p = ncn % 2
                v.wait_ge(s_wld[p], 32 * (ncn // 2 + 1))
                v.wait_ge(s_szl[p], 16 * (ncn // 2 + 1))
                szc3 = szc[:, p * 64:(p + 1) * 64].rearrange("p (g c) -> p g c", c=2)
                fence(v.scalar_tensor_tensor(zpc[:, p * 32:(p + 1) * 32], szc3[:, :, 0], -8.0,
                                             szc3[:, :, 1], ALU.mult, ALU.add))
                for gix in range(32):
                    v.tensor_scalar(deq[:, gix * 128:(gix + 1) * 128],
                                    qbf[:, p * D + gix * 128: p * D + (gix + 1) * 128],
                                    szc3[:, gix:gix + 1, 0],
                                    zpc[:, p * 32 + gix: p * 32 + gix + 1],
                                    ALU.mult, ALU.add)
                v.wait_ge(s_mx, ncn + 1)
                wmv = qbf[:, p * D:(p + 1) * D]
                fence(v.tensor_tensor(wmv.rearrange("p (a b) -> p a b", b=32),
                                      wbf[:, p * D:(p + 1) * D].rearrange("p (a b) -> p a b", b=32),
                                      psM[:, 0:128].unsqueeze(2).broadcast_to([128, 128, 32]),
                                      ALU.mult))
                if debug and ncn >= 1:
                    v.wait_ge(s_dbg, 16 * ncn)
                v.tensor_tensor(wsum[:], wmv, deq[:], ALU.add).then_inc(s_wsum, 1)
                v.wait_ge(s_mm, ncn + 1)
                if ncn >= 1:
                    v.wait_ge(s_st, 16 * ncn)
                for tb in range(TB):
                    r = v.tensor_copy(osb[:, tb * 512:(tb + 1) * 512], psMM[tb % 4][:])
                r.then_inc(s_ev, 1)

        @block.scalar
        def _(s):
            for ncn in range(32):
                pw = ncn % 2
                for kc in range(32):
                    s.wait_ge(s_tp, ncn * 32 + kc + 1)
                    r = s.copy(wT[:, pw * D + kc * 128: pw * D + (kc + 1) * 128],
                               psT[kc % 2][:, 0:128])
                    r.then_inc(s_tx, 1)

        @block.tensor
        def _(t):
            pass
            for tt in range(32):
                t.wait_ge(s_mrd, tt + 1)
                t.wait_ge(s_eld[tt % 2], 16 * (tt // 2 + 1))
                t.matmul(psM[:, 0:128], esb[:, (tt % 2) * 128:(tt % 2 + 1) * 128],
                         mred[:, (tt % 2) * 128:(tt % 2 + 1) * 128],
                         start=(tt == 0), stop=(tt == 31)).then_inc(s_msum, 1)
            t.wait_ge(s_cld, 16 * 33)
            for ncn in range(32):
                pw = ncn % 2
                t.wait_ge(s_bm, 1)
                t.wait_ge(s_pld, 16 * (ncn + 1))
                if ncn >= 1:
                    t.wait_ge(s_wsum, ncn)
                t.matmul(psM[:, 0:128], psl[:], bm[:], start=True, stop=True).then_inc(s_mx, 1)
                t.wait_ge(s_wsum, ncn + 1)
                for kc in range(32):
                    if ncn * 32 + kc >= 2:
                        t.wait_ge(s_tx, ncn * 32 + kc - 1)
                    t.transpose(psT[kc % 2][:, 0:128],
                                wsum[:, kc * 128:(kc + 1) * 128], idb[:]).then_inc(s_tp, 1)
                t.wait_ge(s_tx, 32 * (ncn + 1))
                if ncn >= 1:
                    t.wait_ge(s_ev, ncn)
                for kc in range(32):
                    for tb in range(TB):
                        r = t.matmul(psMM[tb % 4][:],
                                     wT[:, pw * D + kc * 128: pw * D + (kc + 1) * 128],
                                     xsb[:, kc * T + tb * 512: kc * T + (tb + 1) * 512],
                                     start=(kc == 0), stop=(kc == 31))
                r.then_inc(s_mm, 1)
    return nc



def _t(a):
    return np.ascontiguousarray(a.T)


def _make_l0_consts():
    E = np.zeros((32, 128, 128), dtype=BF16NP)
    P = np.zeros((32, 128, 128), dtype=BF16NP)
    for t in range(32):
        for n in range(128):
            E[t, n, 4 * t + n // 32] = 1
            P[t, 4 * t + n // 32, n] = 1
    Ebm = np.ascontiguousarray(E.transpose(1, 0, 2).reshape(128, 32 * 128))
    Pbm = np.ascontiguousarray(P.transpose(1, 0, 2).reshape(128, 32 * 128))
    identb = np.eye(128, dtype=BF16NP)
    return Ebm, Pbm, identb


def _stage_szr(scales_and_zeros):
    sz = np.asarray(scales_and_zeros, dtype=np.float32)
    out = sz.transpose(1, 0, 2).reshape(32, 128, 32, 2).transpose(1, 0, 2, 3)
    return np.ascontiguousarray(out.reshape(128, 32 * 32 * 2))


def run_ln0(x0_shards):
    nc = build_ln_launch(with_rms=False)
    in_maps = [{"xa": s} for s in x0_shards]
    res = run_bass_kernel_spmd(nc, in_maps, core_ids=list(range(NCORES)))
    return [r["xn"] for r in res.results]


def run_l0(xn0f_shards, Wnk, Qnk, maskA, szr, Ebm, Pbm, identb):
    nc = build_l0_launch()
    in_maps = [{"xn0f": s, "Wnk": Wnk, "Qnk": Qnk, "maskA": maskA, "szr": szr,
                "Ebm": Ebm, "Pbm": Pbm, "identb": identb} for s in xn0f_shards]
    res = run_bass_kernel_spmd(nc, in_maps, core_ids=list(range(NCORES)))
    return [r["o0f"] for r in res.results]


def run_ln1(x0_shards, o0_shards):
    nc = build_ln_launch(with_rms=True)
    in_maps = [{"xa": a, "xb": b} for a, b in zip(x0_shards, o0_shards)]
    res = run_bass_kernel_spmd(nc, in_maps, core_ids=list(range(NCORES)))
    return ([r["xs"] for r in res.results], [r["xn"] for r in res.results],
            [r["h"] for r in res.results])






def build_proj_launch(T: int = TC, TH: int = 2176):
    NB = 145
    nc = bass.Bass("TRN2", target_bir_lowering=False)
    hf = nc.dram_tensor("hf", [D, TH], BF16, kind="ExternalInput").ap()
    WinT = nc.dram_tensor("WinT", [128, NB, 32, 128], F32, kind="ExternalInput").ap()
    nwf = nc.dram_tensor("nwf", [128, 32], F32, kind="ExternalInput").ap()
    convw = nc.dram_tensor("convw", [128, 80 * 4], F32, kind="ExternalInput").ap()
    convb = nc.dram_tensor("convb", [128, 80], F32, kind="ExternalInput").ap()
    dtb = nc.dram_tensor("dtb", [128, 1], F32, kind="ExternalInput").ap()
    sg = nc.dram_tensor("sg", [64, 128, T], BF16, kind="ExternalOutput").ap()
    hid = nc.dram_tensor("hid", [64, 128, T], F32, kind="ExternalOutput").ap()
    Bc = nc.dram_tensor("Bc", [8, 128, T], F32, kind="ExternalOutput").ap()
    Cc = nc.dram_tensor("Cc", [8, 128, T], F32, kind="ExternalOutput").ap()
    dto = nc.dram_tensor("dto", [128, T], F32, kind="ExternalOutput").ap()

    MAIN = [(1, 128, 512), (2, 640, 512), (3, 1152, 512), (4, 1664, 512)]
    HALO = (0, 0, 128)

    with ExitStack() as ctx:
        hsb = ctx.enter_context(nc.sbuf_tensor("hsb", [128, 32 * TH], BF16))
        wsb = ctx.enter_context(nc.sbuf_tensor("wsb", [128, 2 * 32 * 128], BF16))
        tmpf = ctx.enter_context(nc.sbuf_tensor("tmpf", [128, TH], BF16))
        cvin = ctx.enter_context(nc.sbuf_tensor("cvin", [128, TH], F32))
        acc0 = ctx.enter_context(nc.sbuf_tensor("acc0", [128, T], F32))
        acc1 = ctx.enter_context(nc.sbuf_tensor("acc1", [128, T], F32))
        sgb = ctx.enter_context(nc.sbuf_tensor("sgb", [128, T], BF16))
        hcb = ctx.enter_context(nc.sbuf_tensor("hcb", [128, T], F32))
        dts = ctx.enter_context(nc.sbuf_tensor("dts", [128, T], F32))
        nws = ctx.enter_context(nc.sbuf_tensor("nws", [128, 32], F32))
        cws = ctx.enter_context(nc.sbuf_tensor("cws", [128, 80 * 4], F32))
        cbs = ctx.enter_context(nc.sbuf_tensor("cbs", [128, 80], F32))
        dtbs = ctx.enter_context(nc.sbuf_tensor("dtbs", [128, 1], F32))
        scs = ctx.enter_context(nc.sbuf_tensor("scs", [128, 2], F32))
        vcs = ctx.enter_context(nc.sbuf_tensor("vcs", [128, 2], F32))
        psH = ctx.enter_context(nc.psum_tensor("psH", [128, 512], F32))
        ps1 = ctx.enter_context(nc.psum_tensor("ps1", [128, 512], F32))
        ps2 = ctx.enter_context(nc.psum_tensor("ps2", [128, 512], F32))
        ps3 = ctx.enter_context(nc.psum_tensor("ps3", [128, 512], F32))
        ps4 = ctx.enter_context(nc.psum_tensor("ps4", [128, 512], F32))
        s_cld = ctx.enter_context(nc.semaphore("s_cld"))
        s_h = ctx.enter_context(nc.semaphore("s_h"))
        s_fold = ctx.enter_context(nc.semaphore("s_fold"))
        s_w0 = ctx.enter_context(nc.semaphore("s_w0"))
        s_w1 = ctx.enter_context(nc.semaphore("s_w1"))
        s_mm = ctx.enter_context(nc.semaphore("s_mm"))
        s_ev = ctx.enter_context(nc.semaphore("s_ev"))
        s_cv = ctx.enter_context(nc.semaphore("s_cv"))
        s_sp = ctx.enter_context(nc.semaphore("s_sp"))
        s_sl = ctx.enter_context(nc.semaphore("s_sl"))
        s_fen = ctx.enter_context(nc.semaphore("s_fen"))
        s_fs = ctx.enter_context(nc.semaphore("s_fs"))
        s_sg = ctx.enter_context(nc.semaphore("s_sg"))
        s_hc = ctx.enter_context(nc.semaphore("s_hc"))
        s_dt = ctx.enter_context(nc.semaphore("s_dt"))
        s_stg = ctx.enter_context(nc.semaphore("s_stg"))
        s_sth = ctx.enter_context(nc.semaphore("s_sth"))
        s_std = ctx.enter_context(nc.semaphore("s_std"))
        block = ctx.enter_context(nc.Block())
        psMAIN = [None, ps1, ps2, ps3, ps4]
        s_w = [s_w0, s_w1]
        dtv = cvin

        def role(nb):
            if nb < 64:
                return "gate"
            if nb < 144:
                return "hbc"
            return "dt"

        @block.sync
        def _(sync):
            sync.dma_start(out=nws[:], in_=nwf[:]).then_inc(s_cld, 16)
            sync.dma_start(out=cws[:], in_=convw[:]).then_inc(s_cld, 16)
            sync.dma_start(out=cbs[:], in_=convb[:]).then_inc(s_cld, 16)
            sync.dma_start(out=dtbs[:], in_=dtb[:]).then_inc(s_cld, 16)
            for kc in range(32):
                sync.dma_start(out=hsb[:, kc * TH:(kc + 1) * TH],
                               in_=hf[kc * 128:(kc + 1) * 128, :]).then_inc(s_h, 16)
            isg = 0; ihc = 0
            for nb in range(NB):
                r = role(nb)
                if r == "gate":
                    isg += 1
                    sync.wait_ge(s_sg, isg)
                    sync.dma_start(out=sg[nb], in_=sgb[:]).then_inc(s_stg, 16)
                elif r == "hbc":
                    cc = nb - 64
                    ihc += 1
                    sync.wait_ge(s_hc, ihc)
                    dst = hid[cc] if cc < 64 else (Bc[cc - 64] if cc < 72 else Cc[cc - 72])
                    sync.dma_start(out=dst, in_=hcb[:]).then_inc(s_sth, 16)
                else:
                    sync.wait_ge(s_dt, 1)
                    sync.dma_start(out=dto[:], in_=cvin[:, 0:T]).then_inc(s_std, 16)
            sync.wait_ge(s_stg, 16 * 64)
            sync.wait_ge(s_sth, 16 * 80)
            sync.wait_ge(s_std, 16)

        @block.gpsimd
        def _(g):
            for nb in range(NB):
                p = nb % 2
                if nb >= 2:
                    g.wait_ge(s_mm, nb - 1)
                g.dma_start(out=wsb[:, p * 4096:(p + 1) * 4096].rearrange("p (a b) -> p a b", b=128),
                            in_=WinT[:, nb]).then_inc(s_w[p], 16)

        @block.vector
        def _(v):
            nfen = [0]

            def fence(r):
                r.then_inc(s_fen, 1)
                nfen[0] += 1
                v.wait_ge(s_fen, nfen[0])

            def fence_new():
                fence(v.tensor_copy(vcs[:, 0:1], vcs[:, 1:2]))

            v.wait_ge(s_h, 16 * 32)
            v.wait_ge(s_cld, 16 * 4)
            for kc in range(32):
                fence(v.tensor_scalar_mul(tmpf[:], hsb[:, kc * TH:(kc + 1) * TH],
                                          nws[:, kc:kc + 1]))
                fence(v.tensor_copy(hsb[:, kc * TH:(kc + 1) * TH], tmpf[:]))
            for nb in range(NB):
                r = role(nb)
                if r == "gate":
                    continue
                v.wait_ge(s_mm, nb + 1)
                if r == "hbc":
                    cc = nb - 64
                    v.tensor_copy(cvin[:, 0:128], psH[:, 0:128])
                    for (bk, off, sz) in MAIN:
                        rr = v.tensor_copy(cvin[:, off:off + sz], psMAIN[bk][:])
                    rr.then_inc(s_ev, 1)
                    fence_new()
                    if cc >= 1:
                        v.wait_ge(s_hc, cc)
                    fence(v.tensor_scalar_mul(acc0[:], cvin[:, 125:125 + T],
                                              cws[:, cc * 4:cc * 4 + 1]))
                    fence(v.scalar_tensor_tensor(acc1[:], cvin[:, 126:126 + T],
                                                 cws[:, cc * 4 + 1:cc * 4 + 2], acc0[:],
                                                 ALU.mult, ALU.add))
                    fence(v.scalar_tensor_tensor(acc0[:], cvin[:, 127:127 + T],
                                                 cws[:, cc * 4 + 2:cc * 4 + 3], acc1[:],
                                                 ALU.mult, ALU.add))
                    v.scalar_tensor_tensor(acc1[:], cvin[:, 128:128 + T],
                                           cws[:, cc * 4 + 3:cc * 4 + 4], acc0[:],
                                           ALU.mult, ALU.add).then_inc(s_cv, 1)
                else:
                    for (bk, off, sz) in MAIN:
                        j = (off - 128)
                        rr = v.tensor_scalar_add(dtv[:, j:j + sz], psMAIN[bk][:], dtbs[:])
                    rr.then_inc(s_ev, 1)
                    fence_new()
                    v.tensor_copy(vcs[:, 0:1], vcs[:, 1:2]).then_inc(s_cv, 1)
                    v.wait_ge(s_sp, 1)
                    fence(v.tensor_relu(dts[:], dtv[:, 0:T]))
                    v.tensor_tensor(dtv[:, 0:T], dts[:], acc0[:], ALU.add).then_inc(s_dt, 1)

        @block.scalar
        def _(s):
            nfs = [0]

            def fens(r):
                r.then_inc(s_fs, 1)
                nfs[0] += 1
                s.wait_ge(s_fs, nfs[0])

            isg = 0; icv = 0
            for nb in range(NB):
                r = role(nb)
                if r == "gate":
                    s.wait_ge(s_mm, nb + 1)
                    if isg >= 1:
                        s.wait_ge(s_stg, 16 * isg)
                    for (bk, off, sz) in MAIN:
                        j = off - 128
                        rr = s.activation(sgb[:, j:j + sz], psMAIN[bk][:], AF.Silu)
                    rr.then_inc(s_ev, 1)
                    s.copy(scs[:, 0:1], scs[:, 1:2]).then_inc(s_sg, 1)
                    isg += 1
                elif r == "hbc":
                    cc = nb - 64
                    icv += 1
                    s.wait_ge(s_cv, icv)
                    if cc >= 1:
                        s.wait_ge(s_sth, 16 * cc)
                    for j in range(0, T, 512):
                        rr = s.activation(hcb[:, j:j + 512], acc1[:, j:j + 512], AF.Silu,
                                          bias=cbs[:, cc:cc + 1])
                    s.copy(scs[:, 0:1], scs[:, 1:2]).then_inc(s_hc, 1)
                else:
                    s.wait_ge(s_cv, icv + 1)
                    for j in range(0, T, 512):
                        rr = s.activation(dts[:, j:j + 512], cvin[:, j:j + 512], AF.Abs)
                    fens(rr)
                    for j in range(0, T, 512):
                        rr = s.activation(acc1[:, j:j + 512], dts[:, j:j + 512], AF.Exp, scale=-1.0)
                    fens(rr)
                    for j in range(0, T, 512):
                        s.activation(acc0[:, j:j + 512], acc1[:, j:j + 512], AF.Ln, bias=1.0)
                    s.copy(scs[:, 0:1], scs[:, 1:2]).then_inc(s_sp, 1)

        @block.tensor
        def _(t):
            t.wait_ge(s_fen, 64)
            for nb in range(NB):
                p = nb % 2
                r = role(nb)
                t.wait_ge(s_w[p], 16 * (nb // 2 + 1))
                if nb >= 1:
                    t.wait_ge(s_ev, nb)
                blocks = ([HALO] + MAIN) if r == "hbc" else MAIN
                for kc in range(32):
                    lhsT = wsb[:, p * 4096 + kc * 128: p * 4096 + (kc + 1) * 128]
                    for (bk, off, sz) in blocks:
                        ps = psH if bk == 0 else psMAIN[bk]
                        rr = t.matmul(ps[:, 0:sz], lhsT,
                                      hsb[:, kc * TH + off: kc * TH + off + sz],
                                      start=(kc == 0), stop=(kc == 31))
                rr.then_inc(s_mm, 1)
    return nc


def _stage_proj_w(w):
    W4 = np.asarray(w, np.float32).reshape(145, 128, 32, 128)
    return np.ascontiguousarray(W4.transpose(3, 0, 2, 1))


def _stage_conv(conv_w, conv_b, norm_w, dt_bias):
    cw = np.ascontiguousarray(np.asarray(conv_w, np.float32).reshape(80, 128, 4)
                              .transpose(1, 0, 2).reshape(128, 320))
    cb = np.ascontiguousarray(np.asarray(conv_b, np.float32).reshape(80, 128).T)
    nw = np.ascontiguousarray(np.asarray(norm_w, np.float32).reshape(32, 128).T)
    db = np.ascontiguousarray(np.asarray(dt_bias, np.float32).reshape(128, 1))
    return cw, cb, nw, db


def run_proj(hf_shards, WinT, nwf, convw, convb, dtb):
    nc = build_proj_launch()
    in_maps = [{"hf": s, "WinT": WinT, "nwf": nwf, "convw": convw,
                "convb": convb, "dtb": dtb} for s in hf_shards]
    res = run_bass_kernel_spmd(nc, in_maps, core_ids=list(range(len(hf_shards))))
    return res.results



def build_ssd_launch(NU: int = 4, NCH: int = 32):
    SQ = NCH * 128
    nc = bass.Bass("TRN2", target_bir_lowering=False)
    hid4 = nc.dram_tensor("hid4", [NU, 8, 128, SQ], F32, kind="ExternalInput").ap()
    sg4 = nc.dram_tensor("sg4", [NU, 8, 128, SQ], BF16, kind="ExternalInput").ap()
    B4 = nc.dram_tensor("B4", [NU, 128, SQ], F32, kind="ExternalInput").ap()
    C4 = nc.dram_tensor("C4", [NU, 128, SQ], F32, kind="ExternalInput").ap()
    dt4 = nc.dram_tensor("dt4", [NU, 16, SQ], F32, kind="ExternalInput").ap()
    alog = nc.dram_tensor("alog", [NU, 16, 1], F32, kind="ExternalInput").ap()
    dpb = nc.dram_tensor("dpb", [128, NU * 16], F32, kind="ExternalInput").ap()
    idf = nc.dram_tensor("idf", [128, 128], F32, kind="ExternalInput").ap()
    idbm = nc.dram_tensor("idbm", [128, 128], BF16, kind="ExternalInput").ap()
    triu = nc.dram_tensor("triu", [128, 128], F32, kind="ExternalInput").ap()
    ones16 = nc.dram_tensor("ones16", [16, 128], F32, kind="ExternalInput").ap()
    i16 = nc.dram_tensor("i16", [16, 16], F32, kind="ExternalInput").ap()
    idb64 = nc.dram_tensor("idb64", [128, 64], BF16, kind="ExternalInput").ap()
    negrow = nc.dram_tensor("negrow", [2, 2048], F32, kind="ExternalInput").ap()
    y4 = nc.dram_tensor("y4", [NU, NCH, 128, 1024], BF16, kind="ExternalOutput").ap()

    NC_TOT = NU * NCH

    def ne_of(u, c):
        return u * (NCH - 1) + c

    with ExitStack() as ctx:
        hidc = ctx.enter_context(nc.sbuf_tensor("hidc", [128, 2 * 1024], BF16))
        sgc = ctx.enter_context(nc.sbuf_tensor("sgc", [128, 2 * 1024], BF16))
        sgt = ctx.enter_context(nc.sbuf_tensor("sgt", [128, 1024], BF16))
        Bb = ctx.enter_context(nc.sbuf_tensor("Bb", [128, SQ], BF16))
        Cb = ctx.enter_context(nc.sbuf_tensor("Cb", [128, SQ], BF16))
        dtu = ctx.enter_context(nc.sbuf_tensor("dtu", [16, SQ], F32))
        alg = ctx.enter_context(nc.sbuf_tensor("alg", [16, 1], F32))
        nega = ctx.enter_context(nc.sbuf_tensor("nega", [16, 1], F32))
        acv = ctx.enter_context(nc.sbuf_tensor("acv", [16, 128], F32))
        acs = ctx.enter_context(nc.sbuf_tensor("acs", [16, 128], F32))
        ons16 = ctx.enter_context(nc.sbuf_tensor("ons16", [16, 128], F32))
        i16s = ctx.enter_context(nc.sbuf_tensor("i16s", [16, 16], F32))
        diag = ctx.enter_context(nc.sbuf_tensor("diag", [16, 16], F32))
        ell = ctx.enter_context(nc.sbuf_tensor("ell", [16, 2], F32))
        dop = ctx.enter_context(nc.sbuf_tensor("dop", [2, 2 * 2048], F32))
        dtT = ctx.enter_context(nc.sbuf_tensor("dtT", [128, 16], F32))
        acsT = ctx.enter_context(nc.sbuf_tensor("acsT", [128, 16], F32))
        EA = ctx.enter_context(nc.sbuf_tensor("EA", [128, 16], F32))
        Ebc = ctx.enter_context(nc.sbuf_tensor("Ebc", [128, 16], F32))
        Msk = ctx.enter_context(nc.sbuf_tensor("Msk", [128, 128], F32))
        BTs = ctx.enter_context(nc.sbuf_tensor("BTs", [128, 128], BF16))
        LTm = ctx.enter_context(nc.sbuf_tensor("LTm", [128, 2 * 128], F32))
        LTe = ctx.enter_context(nc.sbuf_tensor("LTe", [128, 2 * 128], F32))
        Ylh = ctx.enter_context(nc.sbuf_tensor("Ylh", [128, 2 * 128], BF16))
        ddv = ctx.enter_context(nc.sbuf_tensor("ddv", [128, 2], F32))
        hdtp = ctx.enter_context(nc.sbuf_tensor("hdtp", [128, 1024], BF16))
        hcpp = ctx.enter_context(nc.sbuf_tensor("hcpp", [128, 1024], BF16))
        srhp = ctx.enter_context(nc.sbuf_tensor("srhp", [128, 1024], BF16))
        prevb = ctx.enter_context(nc.sbuf_tensor("prevb", [128, 2 * 1024], F32))
        prevbf = ctx.enter_context(nc.sbuf_tensor("prevbf", [128, 1024], BF16))
        stb = ctx.enter_context(nc.sbuf_tensor("stb", [128, 2 * 1024], F32))
        ysb = ctx.enter_context(nc.sbuf_tensor("ysb", [128, 1024], F32))
        y2b = ctx.enter_context(nc.sbuf_tensor("y2b", [128, 1024], F32))
        ygb = ctx.enter_context(nc.sbuf_tensor("ygb", [128, 1024], F32))
        jnk = ctx.enter_context(nc.sbuf_tensor("jnk", [128, 1024], F32))
        ssq = ctx.enter_context(nc.sbuf_tensor("ssq", [128, 1], F32))
        rgs = ctx.enter_context(nc.sbuf_tensor("rgs", [128, 2], F32))
        ybf = ctx.enter_context(nc.sbuf_tensor("ybf", [128, 2 * 1024], BF16))
        dps = ctx.enter_context(nc.sbuf_tensor("dps", [128, NU * 16], F32))
        idfs = ctx.enter_context(nc.sbuf_tensor("idfs", [128, 128], F32))
        idbs = ctx.enter_context(nc.sbuf_tensor("idbs", [128, 128], BF16))
        trius = ctx.enter_context(nc.sbuf_tensor("trius", [128, 128], F32))
        idb64s = ctx.enter_context(nc.sbuf_tensor("idb64s", [128, 64], BF16))
        epsc = ctx.enter_context(nc.sbuf_tensor("epsc", [128, 1], F32))
        vcs = ctx.enter_context(nc.sbuf_tensor("vcs", [128, 2], F32))
        scs = ctx.enter_context(nc.sbuf_tensor("scs", [128, 2], F32))
        psY0 = ctx.enter_context(nc.psum_tensor("psY0", [128, 512], F32))
        psY1 = ctx.enter_context(nc.psum_tensor("psY1", [128, 512], F32))
        psDa = ctx.enter_context(nc.psum_tensor("psDa", [128, 512], F32))
        psDb = ctx.enter_context(nc.psum_tensor("psDb", [128, 512], F32))
        psTa = ctx.enter_context(nc.psum_tensor("psTa", [128, 1024], BF16))
        psTb = ctx.enter_context(nc.psum_tensor("psTb", [128, 1024], BF16))
        psSa = ctx.enter_context(nc.psum_tensor("psSa", [128, 512], F32))
        psSb = ctx.enter_context(nc.psum_tensor("psSb", [128, 512], F32))
        sems = {}
        for nm in ["cld", "hl0", "hl1", "sl0", "sl1", "ucl", "acs", "el", "dgv",
                   "dtt", "act", "ebm", "ebe", "prv", "bt", "btv", "mt", "msk",
                   "sgte", "dmm", "ltm", "lte", "ylh", "ht", "hde", "yd",
                   "sm", "ste", "yo", "yev", "ya", "yg", "sq", "ybo", "yst0", "yst1",
                   "fen", "fes", "dml"]:
            sems[nm] = ctx.enter_context(nc.semaphore("s_" + nm))
        block = ctx.enter_context(nc.Block())
        S = sems
        s_hl = [S["hl0"], S["hl1"]]
        s_sl = [S["sl0"], S["sl1"]]
        psD = [psDa, psDb]
        psT_ = [psTa, psTb]
        psY = [psY0, psY1]
        psS_ = [psSa, psSb]
        hid3 = hidc[:].rearrange("p (b r t) -> p b r t", b=2, r=8)

        @block.sync
        def _(sync):
            for ap_, sb in [(idf, idfs), (idbm, idbs), (triu, trius), (ones16, ons16),
                            (i16, i16s), (dpb, dps), (idb64, idb64s)]:
                sync.dma_start(out=sb[:], in_=ap_[:]).then_inc(S["cld"], 16)
            for ci in range(NC_TOT):
                u, c = divmod(ci, NCH)
                pb = ci % 2
                sync.wait_ge(S["ybo"], ci + 1)
                sync.dma_start(out=y4[u, c], in_=ybf[:, pb * 1024:(pb + 1) * 1024]).then_inc(S["yst0" if pb == 0 else "yst1"], 16)
            sync.wait_ge(S["yst0"], 16 * ((NC_TOT + 1) // 2))
            sync.wait_ge(S["yst1"], 16 * (NC_TOT // 2))

        @block.gpsimd
        def _(g):
            for u in range(NU):
                if u >= 1:
                    g.wait_ge(S["ybo"], u * NCH)
                if u == 0:
                    g.dma_start(out=dop[0:2, 2048:4096], in_=negrow[:]).then_inc(S["dml"], 16)
                g.dma_start(out=Bb[:], in_=B4[u]).then_inc(S["ucl"], 16)
                g.dma_start(out=Cb[:], in_=C4[u]).then_inc(S["ucl"], 16)
                g.dma_start(out=dtu[:], in_=dt4[u]).then_inc(S["ucl"], 16)
                g.dma_start(out=alg[:], in_=alog[u]).then_inc(S["ucl"], 16)
                for c in range(NCH):
                    ci = u * NCH + c
                    p = ci % 2
                    if ci >= 2:
                        g.wait_ge(S["ht"], 16 * (ci - 1))
                        g.wait_ge(S["sgte"], ci - 1)
                    g.dma_start(out=hid3[:, p], in_=hid4[u, :, :, c * 128:(c + 1) * 128]
                                .rearrange("r p t -> p r t")).then_inc(s_hl[p], 16)
                    g.dma_start(out=sgc[:, p * 1024:(p + 1) * 1024].rearrange("p (r t) -> p r t", r=8),
                                in_=sg4[u, :, :, c * 128:(c + 1) * 128].rearrange("r p t -> p r t")
                                ).then_inc(s_sl[p], 16)
                    g.wait_ge(S["acs"], ci + 1)
                    if ci >= 1:
                        g.wait_ge(S["dmm"], 16 * ci)
                    g.dma_start(out=dop[1:2, 0:2048], in_=acs[:, :]).then_inc(S["dml"], 16)
                    g.dma_start(out=dop[0:1, 2048:4096], in_=acs[:, :]).then_inc(S["dml"], 16)

        @block.vector
        def _(v):
            nfen = [0]

            def fence(r):
                r.then_inc(S["fen"], 1)
                nfen[0] += 1
                v.wait_ge(S["fen"], nfen[0])

            v.wait_ge(S["cld"], 16 * 7)
            v.memset(epsc[:], EPS)
            v.memset(dop[0:1, 0:2048], 1.0)
            fence(v.memset(vcs[:], 0.0))
            for u in range(NU):
                v.wait_ge(S["ucl"], 16 * 4 * (u + 1))
                v.wait_ge(S["el"], u * (NCH + 1) + 1)
                fence(v.tensor_scalar_mul(nega[:], ell[:, 1:2], -1.0))
                v.memset(prevb[:, 0:1024], 0.0)
                fence(v.memset(prevb[:, 1024:2048], 0.0))
                for c in range(NCH):
                    ci = u * NCH + c
                    cp = c % 2
                    ne = ne_of(u, c)
                    if ci >= 1:
                        v.wait_ge(S["dml"], 16 + 32 * ci)
                        v.wait_ge(S["bt"], 2 * (ci - 1) + 1)
                        v.wait_ge(S["el"], u * (NCH + 1) + c + 1)
                    fence(v.tensor_scalar_mul(acv[:], dtu[:, c * 128:(c + 1) * 128], nega[:]))
                    v.tensor_tensor_scan(acs[:], ons16[:], acv[:], 0.0,
                                         ALU.mult, ALU.add).then_inc(S["acs"], 1)
                    v.wait_ge(S["mt"], ci + 1)
                    v.tensor_tensor(Msk[:], psSa[:, 0:128], trius[:], ALU.mult).then_inc(S["msk"], 1)
                    v.wait_ge(S["bt"], 2 * (ci + 1))
                    v.tensor_copy(BTs[:], psTa[:, 0:128]).then_inc(S["btv"], 1)
                    v.tensor_copy(sgt[:], psTb[:]).then_inc(S["sgte"], 1)
                    if c >= 1:
                        v.tensor_scalar_mul(diag[:], i16s[:], ell[:, 0:1]).then_inc(S["dgv"], 1)
                        v.wait_ge(S["ebe"], ne)
                        for h in range(16):
                            o = h * 64
                            r = v.scalar_tensor_tensor(
                                prevb[:, cp * 1024 + o: cp * 1024 + o + 64],
                                prevb[:, (1 - cp) * 1024 + o: (1 - cp) * 1024 + o + 64],
                                Ebc[:, h:h + 1],
                                stb[:, (1 - cp) * 1024 + o: (1 - cp) * 1024 + o + 64],
                                ALU.mult, ALU.add)
                        fence(r)
                        v.tensor_copy(prevbf[:], prevb[:, cp * 1024:(cp + 1) * 1024]).then_inc(S["prv"], 1)
                    v.wait_ge(S["dtt"], ci + 1)
                    for h in range(16):
                        hp = h % 2
                        v.wait_ge(S["dmm"], ci * 16 + h + 1)
                        v.tensor_scalar_min(LTm[:, hp * 128:(hp + 1) * 128],
                                            psD[hp][:, 0:128], 0.0).then_inc(S["ltm"], 1)
                        v.wait_ge(S["lte"], ci * 16 + h + 1)
                        fence(v.tensor_tensor(ddv[:, hp:hp + 1],
                                              LTe[:, hp * 128 + 127: hp * 128 + 128],
                                              dtT[:, h:h + 1], ALU.mult))
                        v.tensor_tensor(Ylh[:, hp * 128:(hp + 1) * 128], Msk[:],
                                        LTe[:, hp * 128:(hp + 1) * 128], ALU.mult)
                        v.wait_ge(S["ht"], 16 * ci + h + 1)
                        r = v.tensor_scalar_mul(srhp[:, h * 64:(h + 1) * 64],
                                                psT_[hp][:, 0:64], ddv[:, hp:hp + 1])
                        fence(r)
                        v.tensor_copy(vcs[:, 0:1], vcs[:, 1:2]).then_inc(S["ylh"], 1)
                    v.wait_ge(S["sm"], 2 * (ci + 1))
                    v.tensor_copy(stb[:, cp * 1024: cp * 1024 + 512], psSa[:])
                    fence(v.tensor_copy(stb[:, cp * 1024 + 512: cp * 1024 + 1024], psSb[:]))
                    v.tensor_copy(vcs[:, 0:1], vcs[:, 1:2]).then_inc(S["ste"], 1)
                    v.wait_ge(S["yd"], ci + 1)
                    v.tensor_copy(ysb[:, 0:512], psY0[:])
                    fence(v.tensor_copy(ysb[:, 512:1024], psY1[:]))
                    v.tensor_copy(vcs[:, 0:1], vcs[:, 1:2]).then_inc(S["yev"], 1)
                    if c >= 1:
                        v.wait_ge(S["yo"], 2 * ne)
                        v.wait_ge(S["act"], ci + 1)
                        for h in range(16):
                            o = h * 64
                            ps = psS_[h // 8]
                            r = v.scalar_tensor_tensor(ysb[:, o:o + 64],
                                                       ps[:, (h % 8) * 64:(h % 8) * 64 + 64],
                                                       EA[:, h:h + 1], ysb[:, o:o + 64],
                                                       ALU.mult, ALU.add)
                        fence(r)
                        v.tensor_copy(vcs[:, 0:1], vcs[:, 1:2]).then_inc(S["ya"], 1)
                    v.wait_ge(S["hde"], 16 * (ci + 1))
                    for h in range(16):
                        o = h * 64
                        r = v.scalar_tensor_tensor(y2b[:, o:o + 64], hcpp[:, o:o + 64],
                                                   dps[:, u * 16 + h: u * 16 + h + 1],
                                                   ysb[:, o:o + 64], ALU.mult, ALU.add)
                    fence(r)
                    v.wait_ge(S["sgte"], ci + 1)
                    fence(v.tensor_tensor(ygb[:], y2b[:], sgt[:], ALU.mult))
                    v.tensor_copy(vcs[:, 0:1], vcs[:, 1:2]).then_inc(S["yg"], 1)
                    v.wait_ge(S["sq"], ci + 1)
                    pb = ci % 2
                    if ci >= 2:
                        v.wait_ge(S["yst0" if pb == 0 else "yst1"], 16 * (ci // 2))
                    v.tensor_scalar_mul(ybf[:, pb * 1024:(pb + 1) * 1024], ygb[:],
                                        rgs[:, 0:1]).then_inc(S["ybo"], 1)

        @block.scalar
        def _(s):
            nfes = [0]

            def fens(r):
                r.then_inc(S["fes"], 1)
                nfes[0] += 1
                s.wait_ge(S["fes"], nfes[0])

            for u in range(NU):
                s.wait_ge(S["ucl"], 16 * 4 * (u + 1))
                s.activation(ell[:, 1:2], alg[:], AF.Exp).then_inc(S["el"], 1)
                for c in range(NCH):
                    ci = u * NCH + c
                    ne = ne_of(u, c)
                    s.wait_ge(S["bt"], 2 * ci + 1)
                    s.copy(dtT[:], psDa[:, 256:272])
                    fens(s.copy(acsT[:], psDa[:, 272:288]))
                    s.copy(scs[:, 0:1], scs[:, 1:2]).then_inc(S["dtt"], 1)
                    s.activation(EA[:], acsT[:], AF.Exp).then_inc(S["act"], 1)
                    if c >= 1:
                        s.wait_ge(S["dgv"], ne)
                    s.activation(ell[:, 0:1], acs[:, 127:128], AF.Exp).then_inc(S["el"], 1)
                    if c >= 1:
                        s.wait_ge(S["ebm"], ne)
                        s.copy(Ebc[:], psDa[:, 288:304]).then_inc(S["ebe"], 1)
                    for h in range(16):
                        hp = h % 2
                        s.wait_ge(S["ltm"], ci * 16 + h + 1)
                        s.activation(LTe[:, hp * 128:(hp + 1) * 128],
                                     LTm[:, hp * 128:(hp + 1) * 128], AF.Exp).then_inc(S["lte"], 1)
                        s.wait_ge(S["ht"], 16 * ci + h + 1)
                        s.wait_ge(S["dtt"], ci + 1)
                        s.activation(hdtp[:, h * 64:(h + 1) * 64], psT_[hp][:, 0:64],
                                     AF.Identity, scale=dtT[:, h:h + 1])
                        s.copy(hcpp[:, h * 64:(h + 1) * 64], psT_[hp][:, 0:64]).then_inc(S["hde"], 1)
                    s.wait_ge(S["yg"], ci + 1)
                    s.activation(jnk[:], ygb[:], AF.Square, accum_out=ssq[:])
                    fens(s.activation(rgs[:, 1:2], ssq[:], AF.Ln, scale=1.0 / 1024.0, bias=epsc[:]))
                    s.activation(rgs[:, 0:1], rgs[:, 1:2], AF.Exp, scale=-0.5).then_inc(S["sq"], 1)

        @block.tensor
        def _(t):
            t.wait_ge(S["cld"], 16 * 7)
            for u in range(NU):
                t.wait_ge(S["ucl"], 16 * 4 * (u + 1))
                for c in range(NCH):
                    ci = u * NCH + c
                    p = ci % 2
                    cp = c % 2
                    ne = ne_of(u, c)
                    t.wait_ge(S["acs"], ci + 1)
                    if ci >= 1:
                        t.wait_ge(S["ltm"], 16 * ci)
                        t.wait_ge(S["dtt"], ci)
                        if ci >= NCH or (ci % NCH) >= 1 or True:
                            pass
                    t.transpose(psDa[:, 256:272], dtu[:, c * 128:(c + 1) * 128], idfs[0:16, 0:16])
                    t.transpose(psDa[:, 272:288], acs[:], idfs[0:16, 0:16]).then_inc(S["bt"], 1)
                    if ci >= 1:
                        t.wait_ge(S["ste"], ci)
                        t.wait_ge(S["msk"], ci)
                        if (ci % NCH) != 0:
                            t.wait_ge(S["ya"], ne_of(*divmod(ci - 1, NCH)))
                        elif u >= 1:
                            t.wait_ge(S["ya"], ne_of(u - 1, NCH - 1))
                    t.matmul(psSa[:, 0:128], Bb[:, c * 128:(c + 1) * 128],
                             Cb[:, c * 128:(c + 1) * 128], start=True, stop=True).then_inc(S["mt"], 1)
                    t.wait_ge(s_hl[p], 16 * (ci // 2 + 1))
                    t.wait_ge(s_sl[p], 16 * (ci // 2 + 1))
                    if ci >= 1:
                        t.wait_ge(S["hde"], 16 * ci)
                        t.wait_ge(S["ylh"], 16 * ci)
                        t.wait_ge(S["btv"], ci)
                        t.wait_ge(S["sgte"], ci)
                    t.transpose(psTa[:, 0:128], Bb[:, c * 128:(c + 1) * 128], idbs[:])
                    for r8 in range(8):
                        rr = t.transpose(psTb[:, r8 * 128:(r8 + 1) * 128],
                                         sgc[:, p * 1024 + r8 * 128: p * 1024 + (r8 + 1) * 128],
                                         idbs[:])
                    rr.then_inc(S["bt"], 1)
                    if c >= 1:
                        t.wait_ge(S["dgv"], ne)
                        if ne >= 2:
                            t.wait_ge(S["ebe"], ne - 1)
                            t.wait_ge(S["prv"], ne - 1)
                        t.matmul(psDa[:, 288:304], ons16[:], diag[:],
                                 start=True, stop=True).then_inc(S["ebm"], 1)
                    t.wait_ge(S["dml"], 16 + 32 * (ci + 1))
                    t.wait_ge(S["dtt"], ci + 1)
                    if c >= 1:
                        t.wait_ge(S["ebe"], ne)
                    for h in range(16):
                        hp = h % 2
                        if ci * 16 + h >= 2:
                            t.wait_ge(S["ltm"], ci * 16 + h - 1)
                        t.matmul(psD[hp][:, 0:128], dop[:, h * 128:(h + 1) * 128],
                                 dop[:, 2048 + h * 128: 2048 + (h + 1) * 128],
                                 start=True, stop=True).then_inc(S["dmm"], 1)
                        if ci * 16 + h >= 2:
                            t.wait_ge(S["hde"], ci * 16 + h - 1)
                            t.wait_ge(S["ylh"], ci * 16 + h - 1)
                        if h == 0:
                            t.wait_ge(S["btv"], ci + 1)
                        if h == 1:
                            t.wait_ge(S["sgte"], ci + 1)
                        t.transpose(psT_[hp][:, 0:64],
                                    hid3[(h % 2) * 64:(h % 2) * 64 + 64, p, h // 2, :],
                                    idb64s[(h % 2) * 64:(h % 2) * 64 + 64, :]).then_inc(S["ht"], 1)
                        t.wait_ge(S["ylh"], ci * 16 + h + 1)
                        t.wait_ge(S["hde"], ci * 16 + h + 1)
                        if h == 0:
                            t.wait_ge(S["yev"], ci)
                        r = t.matmul(psY[h // 8][:, (h % 8) * 64:(h % 8) * 64 + 64],
                                     Ylh[:, hp * 128:(hp + 1) * 128],
                                     hdtp[:, h * 64:(h + 1) * 64],
                                     start=True, stop=True)
                        if h == 15:
                            r.then_inc(S["yd"], 1)
                    t.wait_ge(S["ylh"], 16 * (ci + 1))
                    t.wait_ge(S["btv"], ci + 1)
                    t.wait_ge(S["msk"], ci + 1)
                    t.matmul(psSa[:], BTs[:], srhp[:, 0:512], start=True, stop=True)
                    t.matmul(psSb[:], BTs[:], srhp[:, 512:1024], start=True, stop=True).then_inc(S["sm"], 2)
                    if c >= 1:
                        t.wait_ge(S["ste"], ci + 1)
                        t.wait_ge(S["prv"], ne)
                        t.matmul(psSa[:], Cb[:, c * 128:(c + 1) * 128],
                                 prevbf[:, 0:512], start=True, stop=True)
                        t.matmul(psSb[:], Cb[:, c * 128:(c + 1) * 128],
                                 prevbf[:, 512:1024],
                                 start=True, stop=True).then_inc(S["yo"], 2)
    return nc



def build_final_launch(T: int = TC):
    TH = T // 2
    nc = bass.Bass("TRN2", target_bir_lowering=False)
    ygf = nc.dram_tensor("ygf", [8192, T], BF16, kind="ExternalInput").ap()
    xn1f = nc.dram_tensor("xn1f", [D, T], BF16, kind="ExternalInput").ap()
    x1f = nc.dram_tensor("x1f", [D, T], F32, kind="ExternalInput").ap()
    OutWT = nc.dram_tensor("OutWT", [128, 32, 64, 128], F32, kind="ExternalInput").ap()
    A1 = nc.dram_tensor("A1", [D, D], I32, kind="ExternalInput").ap()
    ws1 = nc.dram_tensor("ws1", [128, 32], F32, kind="ExternalInput").ap()
    wz1 = nc.dram_tensor("wz1", [128, 32], F32, kind="ExternalInput").ap()
    bias1f = nc.dram_tensor("bias1f", [128, 32], F32, kind="ExternalInput").ap()
    gnwf = nc.dram_tensor("gnwf", [128, 64], F32, kind="ExternalInput").ap()
    identb = nc.dram_tensor("identb", [128, 128], BF16, kind="ExternalInput").ap()
    outf = nc.dram_tensor("outf", [D, T], F32, kind="ExternalOutput").ap()
    w1scr = nc.dram_tensor("w1scr", [128, 32, 32, 128], BF16).ap()
    oacc = nc.dram_tensor("oacc", [D, T], F32).ap()

    with ExitStack() as ctx:
        bigb = ctx.enter_context(nc.sbuf_tensor("bigb", [128, 64 * TH], BF16))
        ygh = bigb[:]
        xnh = bigb[:, 0:32 * TH]
        qvb = bigb[:].bitcast(I32)
        wsb = ctx.enter_context(nc.sbuf_tensor("wsb", [128, 2 * 64 * 128], BF16))
        w1sb = ctx.enter_context(nc.sbuf_tensor("w1sb", [128, 2 * 32 * 128], BF16))
        ws1s = ctx.enter_context(nc.sbuf_tensor("ws1s", [128, 32], F32))
        wz1s = ctx.enter_context(nc.sbuf_tensor("wz1s", [128, 32], F32))
        zs1 = ctx.enter_context(nc.sbuf_tensor("zs1", [128, 32], F32))
        b1s = ctx.enter_context(nc.sbuf_tensor("b1s", [128, 32], F32))
        gnws = ctx.enter_context(nc.sbuf_tensor("gnws", [128, 64], F32))
        idb = ctx.enter_context(nc.sbuf_tensor("idb", [128, 128], BF16))
        tmpb = ctx.enter_context(nc.sbuf_tensor("tmpb", [128, TH], BF16))
        oa = ctx.enter_context(nc.sbuf_tensor("oa", [128, 2 * TH], F32))
        ob = ctx.enter_context(nc.sbuf_tensor("ob", [128, 2 * TH], F32))
        dq = oa[:].bitcast(BF16)[:, 0:D]
        w1T = ob[:].bitcast(BF16)[:, 0:D]
        xs = ctx.enter_context(nc.sbuf_tensor("xs", [128, 2 * TH], F32))
        vcs = ctx.enter_context(nc.sbuf_tensor("vcs", [128, 2], F32))
        psA0 = ctx.enter_context(nc.psum_tensor("psA0", [128, 512], F32))
        psA1 = ctx.enter_context(nc.psum_tensor("psA1", [128, 512], F32))
        psB0 = ctx.enter_context(nc.psum_tensor("psB0", [128, 512], F32))
        psB1 = ctx.enter_context(nc.psum_tensor("psB1", [128, 512], F32))
        psT0 = ctx.enter_context(nc.psum_tensor("psT0", [128, 1024], BF16))
        psT1 = ctx.enter_context(nc.psum_tensor("psT1", [128, 1024], BF16))
        sems = {}
        for nm in ["cld", "ql0", "ql1", "dqv", "tp", "tx", "w1st", "ygl", "gnf",
                   "wl0", "wl1", "mma", "eva", "oast", "xnl", "w1l0", "w1l1",
                   "x1l0", "x1l1", "oal0", "oal1", "mmf", "evf", "ost", "fen"]:
            sems[nm] = ctx.enter_context(nc.semaphore("s_" + nm))
        block = ctx.enter_context(nc.Block())
        S = sems
        psT = [psT0, psT1]
        s_ql = [S["ql0"], S["ql1"]]
        s_wl = [S["wl0"], S["wl1"]]
        s_w1l = [S["w1l0"], S["w1l1"]]
        s_x1l = [S["x1l0"], S["x1l1"]]
        s_oal = [S["oal0"], S["oal1"]]

        @block.sync
        def _(sync):
            for ap_, sb in [(ws1, ws1s), (wz1, wz1s), (bias1f, b1s), (gnwf, gnws),
                            (identb, idb)]:
                sync.dma_start(out=sb[:], in_=ap_[:]).then_inc(S["cld"], 16)
            for ncn in range(32):
                sync.wait_ge(S["tx"], 32 * (ncn + 1))
                sync.dma_start(out=w1scr[:, ncn], in_=w1T[:].rearrange("p (a b) -> p a b", b=128)
                               ).then_inc(S["w1st"], 16)
            for th in range(2):
                if th == 1:
                    sync.wait_ge(S["mma"], 32)
                for kc in range(64):
                    sync.dma_start(out=ygh[:, kc * TH:(kc + 1) * TH],
                                   in_=ygf[kc * 128:(kc + 1) * 128, th * TH:(th + 1) * TH]
                                   ).then_inc(S["ygl"], 16)
                for nb in range(32):
                    i = th * 32 + nb
                    sync.wait_ge(S["eva"], i + 1)
                    sync.dma_start(out=oacc[nb * 128:(nb + 1) * 128, th * TH:(th + 1) * TH],
                                   in_=oa[:, (i % 2) * TH:(i % 2 + 1) * TH]).then_inc(S["oast"], 16)
            for th in range(2):
                if th == 0:
                    sync.wait_ge(S["mma"], 64)
                else:
                    sync.wait_ge(S["mmf"], 32)
                for kc in range(32):
                    sync.dma_start(out=xnh[:, kc * TH:(kc + 1) * TH],
                                   in_=xn1f[kc * 128:(kc + 1) * 128, th * TH:(th + 1) * TH]
                                   ).then_inc(S["xnl"], 16)
                for nb in range(32):
                    i = th * 32 + nb
                    p = i % 2
                    if i >= 2:
                        sync.wait_ge(S["evf"], i - 1)
                    sync.dma_start(out=xs[:, p * TH:(p + 1) * TH],
                                   in_=x1f[nb * 128:(nb + 1) * 128, th * TH:(th + 1) * TH]
                                   ).then_inc(s_x1l[p], 16)
                    sync.wait_ge(S["oast"], 16 * (th * 32 + nb + 1) if th == 0 else 16 * 64)
                    sync.dma_start(out=ob[:, p * TH:(p + 1) * TH],
                                   in_=oacc[nb * 128:(nb + 1) * 128, th * TH:(th + 1) * TH]
                                   ).then_inc(s_oal[p], 16)
                    sync.wait_ge(S["evf"], i + 1)
                    sync.dma_start(out=outf[nb * 128:(nb + 1) * 128, th * TH:(th + 1) * TH],
                                   in_=oa[:, (i % 2) * TH:(i % 2 + 1) * TH]).then_inc(S["ost"], 16)
            sync.wait_ge(S["ost"], 16 * 64)

        @block.gpsimd
        def _(g):
            for ncn in range(32):
                p = ncn % 2
                if ncn >= 2:
                    g.wait_ge(S["dqv"], ncn - 1)
                g.dma_start(out=qvb[:, p * D:(p + 1) * D],
                            in_=A1[ncn * 128:(ncn + 1) * 128, :]).then_inc(s_ql[p], 16)
            for th in range(2):
                for nb in range(32):
                    i = th * 32 + nb
                    p = i % 2
                    if i >= 2:
                        g.wait_ge(S["mma"], i - 1)
                    g.dma_start(out=wsb[:, p * 8192:(p + 1) * 8192].rearrange("p (a b) -> p a b", b=128),
                                in_=OutWT[:, nb]).then_inc(s_wl[p], 16)
            for th in range(2):
                for nb in range(32):
                    i = th * 32 + nb
                    p = i % 2
                    if i == 0:
                        g.wait_ge(S["w1st"], 16 * 32)
                    if i >= 2:
                        g.wait_ge(S["mmf"], i - 1)
                    g.dma_start(out=w1sb[:, p * 4096:(p + 1) * 4096].rearrange("p (a b) -> p a b", b=128),
                                in_=w1scr[:, nb]).then_inc(s_w1l[p], 16)

        @block.vector
        def _(v):
            nfen = [0]

            def fence(r):
                r.then_inc(S["fen"], 1)
                nfen[0] += 1
                v.wait_ge(S["fen"], nfen[0])

            v.wait_ge(S["cld"], 16 * 5)
            fence(v.tensor_tensor(zs1[:], wz1s[:], ws1s[:], ALU.mult))
            fence(v.tensor_scalar_mul(zs1[:], zs1[:], -1.0))
            for ncn in range(32):
                p = ncn % 2
                v.wait_ge(s_ql[p], 16 * (ncn // 2 + 1))
                if ncn >= 1:
                    v.wait_ge(S["tp"], 32 * ncn)
                q3 = qvb[:, p * D:(p + 1) * D].rearrange("p (a b) -> p a b", b=4)
                fence(v.tensor_single_scalar(q3[:, :, 1], q3[:, :, 1], 6,
                                             ALU.logical_shift_right))
                fence(v.tensor_single_scalar(qvb[:, p * D:(p + 1) * D],
                                             qvb[:, p * D:(p + 1) * D], 63, ALU.bitwise_and))
                for gx in range(4):
                    r = v.tensor_scalar(dq[:, gx * 1024:(gx + 1) * 1024],
                                        qvb[:, p * D + gx * 1024: p * D + (gx + 1) * 1024],
                                        ws1s[:, ncn:ncn + 1], zs1[:, ncn:ncn + 1],
                                        ALU.mult, ALU.add)
                r.then_inc(S["dqv"], 1)
            for th in range(2):
                v.wait_ge(S["ygl"], 16 * 64 * (th + 1))
                for kc in range(64):
                    fence(v.tensor_scalar_mul(tmpb[:], ygh[:, kc * TH:(kc + 1) * TH],
                                              gnws[:, kc:kc + 1]))
                    fence(v.tensor_copy(ygh[:, kc * TH:(kc + 1) * TH], tmpb[:]))
                v.tensor_copy(vcs[:, 0:1], vcs[:, 1:2]).then_inc(S["gnf"], 1)
                for nb in range(32):
                    i = th * 32 + nb
                    v.wait_ge(S["mma"], i + 1)
                    if i >= 2:
                        v.wait_ge(S["oast"], 16 * (i - 1))
                    v.tensor_copy(oa[:, (i % 2) * TH: (i % 2) * TH + 512], psA0[:])
                    r = v.tensor_copy(oa[:, (i % 2) * TH + 512:(i % 2) * TH + 1024], psA1[:])
                    r.then_inc(S["eva"], 1)
            for th in range(2):
                for nb in range(32):
                    i = th * 32 + nb
                    p = i % 2
                    v.wait_ge(S["mmf"], i + 1)
                    v.wait_ge(s_oal[p], 16 * (i // 2 + 1))
                    v.wait_ge(s_x1l[p], 16 * (i // 2 + 1))
                    if i < 2:
                        v.wait_ge(S["oast"], 16 * (63 + i))
                    if i >= 2:
                        v.wait_ge(S["ost"], 16 * (i - 1))
                    for bk, ps in [(0, psB0), (1, psB1)]:
                        o0 = p * TH + bk * 512
                        fence(v.tensor_scalar_add(oa[:, o0:o0 + 512], ps[:],
                                                  b1s[:, nb:nb + 1]))
                    fence(v.tensor_tensor(oa[:, p * TH:(p + 1) * TH], oa[:, p * TH:(p + 1) * TH],
                                          ob[:, p * TH:(p + 1) * TH], ALU.add))
                    fence(v.tensor_tensor(oa[:, p * TH:(p + 1) * TH], oa[:, p * TH:(p + 1) * TH],
                                          xnh[:, nb * TH:(nb + 1) * TH], ALU.add))
                    r = v.tensor_tensor(oa[:, p * TH:(p + 1) * TH], oa[:, p * TH:(p + 1) * TH],
                                        xs[:, p * TH:(p + 1) * TH], ALU.add)
                    r.then_inc(S["evf"], 1)

        @block.scalar
        def _(s):
            for ncn in range(32):
                for kc in range(32):
                    s.wait_ge(S["tp"], ncn * 32 + kc + 1)
                    s.copy(w1T[:, kc * 128:(kc + 1) * 128],
                           psT[kc % 2][:, 0:128]).then_inc(S["tx"], 1)

        @block.tensor
        def _(t):
            t.wait_ge(S["cld"], 16 * 5)
            for ncn in range(32):
                t.wait_ge(S["dqv"], ncn + 1)
                if ncn >= 1:
                    t.wait_ge(S["w1st"], 16 * ncn)
                for kc in range(32):
                    if ncn * 32 + kc >= 2:
                        t.wait_ge(S["tx"], ncn * 32 + kc - 1)
                    t.transpose(psT[kc % 2][:, 0:128], dq[:, kc * 128:(kc + 1) * 128],
                                idb[:]).then_inc(S["tp"], 1)
            for th in range(2):
                t.wait_ge(S["gnf"], th + 1)
                for nb in range(32):
                    i = th * 32 + nb
                    p = i % 2
                    t.wait_ge(s_wl[p], 16 * (i // 2 + 1))
                    if i >= 1:
                        t.wait_ge(S["eva"], i)
                    for kc in range(64):
                        lhsT = wsb[:, p * 8192 + kc * 128: p * 8192 + (kc + 1) * 128]
                        t.matmul(psA0[:], lhsT, ygh[:, kc * TH: kc * TH + 512],
                                 start=(kc == 0), stop=(kc == 63))
                        r = t.matmul(psA1[:], lhsT, ygh[:, kc * TH + 512: kc * TH + 1024],
                                     start=(kc == 0), stop=(kc == 63))
                    r.then_inc(S["mma"], 1)
            for th in range(2):
                t.wait_ge(S["xnl"], 16 * 32 * (th + 1))
                for nb in range(32):
                    i = th * 32 + nb
                    p = i % 2
                    t.wait_ge(s_w1l[p], 16 * (i // 2 + 1))
                    if i >= 1:
                        t.wait_ge(S["evf"], i)
                    for kc in range(32):
                        lhsT = w1sb[:, p * 4096 + kc * 128: p * 4096 + (kc + 1) * 128]
                        t.matmul(psB0[:], lhsT, xnh[:, kc * TH: kc * TH + 512],
                                 start=(kc == 0), stop=(kc == 31))
                        r = t.matmul(psB1[:], lhsT, xnh[:, kc * TH + 512: kc * TH + 1024],
                                     start=(kc == 0), stop=(kc == 31))
                    r.then_inc(S["mmf"], 1)
    return nc


def _stage_final(out_proj_w, qweight, weight_scale, weight_zero_point, bias, gnorm_w):
    W4 = np.asarray(out_proj_w, np.float32).T.reshape(64, 128, 32, 128)
    OutWT = np.ascontiguousarray(W4.transpose(1, 2, 0, 3))
    q = np.asarray(qweight)
    cols = np.empty(4096, np.int64)
    for k in range(4096):
        g, j = divmod(k, 4)
        cols[k] = 3 * g + (0 if j == 0 else (0 if j == 1 else (1 if j == 2 else 2)))
    A1 = np.ascontiguousarray(q[:, cols]).astype(np.int32)
    ws1 = np.ascontiguousarray(np.asarray(weight_scale, np.float32).reshape(32, 128).T)
    wz1 = np.ascontiguousarray(np.asarray(weight_zero_point, np.float32).reshape(32, 128).T)
    b1 = np.ascontiguousarray(np.asarray(bias, np.float32).reshape(32, 128).T)
    gnw = np.ascontiguousarray(np.asarray(gnorm_w, np.float32).reshape(64, 128).T)
    return OutWT, A1, ws1, wz1, b1, gnw


def run_final(yg_shards, xn1_shards, x1_shards, OutWT, A1, ws1, wz1, b1, gnw, identb):
    nc = build_final_launch()
    in_maps = [{"ygf": a, "xn1f": b, "x1f": c, "OutWT": OutWT, "A1": A1,
                "ws1": ws1, "wz1": wz1, "bias1f": b1, "gnwf": gnw, "identb": identb}
               for a, b, c in zip(yg_shards, xn1_shards, x1_shards)]
    res = run_bass_kernel_spmd(nc, in_maps, core_ids=list(range(len(in_maps))))
    return [r["outf"] for r in res.results]



def kernel(**inputs):
    return kernel_fused(**inputs)


def kernel_unfused(L0_attn_data, L0_attn_weight, L0_attn_mask, L0_ffn_weight_q,
           L0_ffn_scales_and_zeros, L1_attn_norm_w, L1_attn_in_proj_w,
           L1_attn_conv_w, L1_attn_conv_b, L1_attn_dt_bias, L1_attn_A_log,
           L1_attn_Dp, L1_attn_gnorm_w, L1_attn_out_proj_w, L1_ffn_qweight,
           L1_ffn_weight_scale, L1_ffn_weight_zero_point, L1_ffn_bias):
    x0 = np.ascontiguousarray(np.asarray(L0_attn_data, np.float32).reshape(16384, D))
    x0_shards = [np.ascontiguousarray(x0[c * TC:(c + 1) * TC]) for c in range(NCORES)]

    xn0_tm = run_ln0(x0_shards)

    Ebm, Pbm, identb = _make_l0_consts()
    szr = _stage_szr(L0_ffn_scales_and_zeros)
    o0 = run_l0([_t(s) for s in xn0_tm], np.asarray(L0_attn_weight, np.float32),
                np.asarray(L0_ffn_weight_q, np.int32),
                np.asarray(L0_attn_mask, np.int32), szr, Ebm, Pbm, identb)

    x1_tm, xn1_tm, h_tm = run_ln1(x0_shards, [_t(s) for s in o0])

    h_full = np.concatenate(h_tm, axis=0)
    hf_shards = []
    for c in range(NCORES):
        b, hb = divmod(c, 2)
        base = b * S + hb * TC
        if hb == 0:
            halo = np.zeros((128, D), dtype=BF16NP)
        else:
            halo = h_full[base - 128: base]
        hf_shards.append(_t(np.concatenate([halo, h_full[base: base + TC]], axis=0)))
    WinT = _stage_proj_w(L1_attn_in_proj_w)
    cw, cb, nw, db = _stage_conv(L1_attn_conv_w, L1_attn_conv_b, L1_attn_norm_w,
                                 L1_attn_dt_bias)
    dres = run_proj(hf_shards, WinT, nw, cw, cb, db)

    A_log = np.asarray(L1_attn_A_log, np.float32)
    Dp = np.asarray(L1_attn_Dp, np.float32)
    hid_b, sg_b, B_b, C_b, dt_b = [], [], [], [], []
    for b in range(B_SZ):
        hid_b.append(np.concatenate([dres[2 * b]["hid"], dres[2 * b + 1]["hid"]], axis=2))
        sg_b.append(np.concatenate([dres[2 * b]["sg"], dres[2 * b + 1]["sg"]], axis=2))
        B_b.append(np.concatenate([dres[2 * b]["Bc"], dres[2 * b + 1]["Bc"]], axis=2))
        C_b.append(np.concatenate([dres[2 * b]["Cc"], dres[2 * b + 1]["Cc"]], axis=2))
        dt_b.append(np.concatenate([dres[2 * b]["dto"], dres[2 * b + 1]["dto"]], axis=1))
    units = [(b, g) for b in range(B_SZ) for g in range(8)]
    idf = np.eye(128, dtype=np.float32)
    idb128 = np.eye(128, dtype=BF16NP)
    triu = np.triu(np.ones((128, 128), np.float32))
    ones16 = np.ones((16, 128), np.float32)
    i16 = np.eye(16, dtype=np.float32)
    idb64 = np.vstack([np.eye(64, dtype=BF16NP)] * 2)
    negrow = np.vstack([np.zeros((1, 2048), np.float32), -np.ones((1, 2048), np.float32)])
    in_maps = []
    for c in range(NCORES):
        us = units[4 * c: 4 * c + 4]
        ins = {
            "hid4": np.stack([hid_b[b][8 * g: 8 * g + 8] for b, g in us]),
            "sg4": np.stack([sg_b[b][8 * g: 8 * g + 8] for b, g in us]),
            "B4": np.stack([B_b[b][g] for b, g in us]),
            "C4": np.stack([C_b[b][g] for b, g in us]),
            "dt4": np.stack([dt_b[b][16 * g: 16 * g + 16] for b, g in us]),
            "alog": np.stack([A_log[16 * g: 16 * g + 16][:, None] for b, g in us]),
            "dpb": np.concatenate([np.tile(Dp[16 * g: 16 * g + 16], (128, 1))
                                   for b, g in us], axis=1),
            "idf": idf, "idbm": idb128, "triu": triu, "ones16": ones16,
            "i16": i16, "idb64": idb64, "negrow": negrow,
        }
        in_maps.append(ins)
    nc_e = build_ssd_launch()
    res_e = run_bass_kernel_spmd(nc_e, in_maps, core_ids=list(range(NCORES)))
    y_full = np.empty((16384, 8192), dtype=BF16NP)
    for c in range(NCORES):
        y4 = res_e.results[c]["y4"]
        for j, (b, g) in enumerate(units[4 * c: 4 * c + 4]):
            y_full[b * S:(b + 1) * S, g * 1024:(g + 1) * 1024] = y4[j].reshape(S, 1024)

    OutWT, A1, ws1, wz1, b1f, gnwf = _stage_final(
        L1_attn_out_proj_w, L1_ffn_qweight, L1_ffn_weight_scale,
        L1_ffn_weight_zero_point, L1_ffn_bias, L1_attn_gnorm_w)
    outs = run_final([_t(y_full[c * TC:(c + 1) * TC]) for c in range(NCORES)],
                     [_t(s) for s in xn1_tm], [_t(s) for s in x1_tm],
                     OutWT, A1, ws1, wz1, b1f, gnwf, idb128)
    out = np.concatenate([o.T for o in outs], axis=0)
    return np.ascontiguousarray(out.reshape(B_SZ, S, D))



TPAD = 2176


class _Arena:
    def __init__(self, nc, ctx, name, nbytes):
        assert nbytes % 4 == 0
        self.t = ctx.enter_context(nc.sbuf_tensor(name, [128, nbytes // 4], I32))
        self.off = 0
        self.nbytes = nbytes

    def reset(self):
        self.off = 0

    def take(self, nelem, dtype):
        esz = {F32: 4, I32: 4, BF16: 2}[dtype]
        nb = nelem * esz
        nb_al = (nb + 63) & ~63
        assert self.off + nb_al <= self.nbytes, f"arena overflow {self.off}+{nb_al}>{self.nbytes}"
        if dtype == I32:
            v = self.t[:, self.off // 4: self.off // 4 + nelem]
        elif dtype == F32:
            v = self.t[:].bitcast(F32)[:, self.off // 4: self.off // 4 + nelem]
        else:
            v = self.t[:].bitcast(BF16)[:, self.off // 2: self.off // 2 + nelem]
        self.off += nb_al
        return v


def build_fused(dbg_stage=0):
    nc = bass.Bass("TRN2", target_bir_lowering=False)
    D_ = D
    x0s = nc.dram_tensor("x0s", [TPAD, D_], F32, kind="ExternalInput").ap()
    Wnk = nc.dram_tensor("Wnk", [D_, D_], F32, kind="ExternalInput").ap()
    Qnk = nc.dram_tensor("Qnk", [D_, D_], I32, kind="ExternalInput").ap()
    maskA = nc.dram_tensor("maskA", [D_, D_], I32, kind="ExternalInput").ap()
    szr = nc.dram_tensor("szr", [128, 2048], F32, kind="ExternalInput").ap()
    Ebm = nc.dram_tensor("Ebm", [128, 4096], BF16, kind="ExternalInput").ap()
    Pbm = nc.dram_tensor("Pbm", [128, 4096], BF16, kind="ExternalInput").ap()
    WinT = nc.dram_tensor("WinT", [128, 145, 32, 128], F32, kind="ExternalInput").ap()
    nwf = nc.dram_tensor("nwf", [128, 32], F32, kind="ExternalInput").ap()
    convw = nc.dram_tensor("convw", [128, 320], F32, kind="ExternalInput").ap()
    convb = nc.dram_tensor("convb", [128, 80], F32, kind="ExternalInput").ap()
    dtb = nc.dram_tensor("dtb", [128, 1], F32, kind="ExternalInput").ap()
    alog8 = nc.dram_tensor("alog8", [16, 8], F32, kind="ExternalInput").ap()
    dpb = nc.dram_tensor("dpb", [128, 128], F32, kind="ExternalInput").ap()
    OutWT = nc.dram_tensor("OutWT", [128, 32, 64, 128], F32, kind="ExternalInput").ap()
    A1 = nc.dram_tensor("A1", [D_, D_], I32, kind="ExternalInput").ap()
    ws1 = nc.dram_tensor("ws1", [128, 32], F32, kind="ExternalInput").ap()
    wz1 = nc.dram_tensor("wz1", [128, 32], F32, kind="ExternalInput").ap()
    bias1f = nc.dram_tensor("bias1f", [128, 32], F32, kind="ExternalInput").ap()
    gnwf = nc.dram_tensor("gnwf", [128, 64], F32, kind="ExternalInput").ap()
    identb = nc.dram_tensor("identb", [128, 128], BF16, kind="ExternalInput").ap()
    idf = nc.dram_tensor("idf", [128, 128], F32, kind="ExternalInput").ap()
    triu = nc.dram_tensor("triu", [128, 128], F32, kind="ExternalInput").ap()
    ones16 = nc.dram_tensor("ones16", [16, 128], F32, kind="ExternalInput").ap()
    i16 = nc.dram_tensor("i16", [16, 16], F32, kind="ExternalInput").ap()
    idb64 = nc.dram_tensor("idb64", [128, 64], BF16, kind="ExternalInput").ap()
    negrow = nc.dram_tensor("negrow", [2, 2048], F32, kind="ExternalInput").ap()
    smask = nc.dram_tensor("smask", [128, 1], F32, kind="ExternalInput").ap()
    sgidx = nc.dram_tensor("sgidx", [128, 8], I32, kind="ExternalInput").ap()
    out = nc.dram_tensor("out", [TC, D_], F32, kind="ExternalOutput").ap()
    o0f = nc.dram_tensor("o0f", [D_, TPAD], F32).ap()
    x1blk = nc.dram_tensor("x1blk", [17, 32, 128, 128], F32).ap()
    hblk = nc.dram_tensor("hblk", [17, 32, 128, 128], BF16).ap()
    xn1blk = nc.dram_tensor("xn1blk", [17, 32, 128, 128], BF16).ap()
    sgd = nc.dram_tensor("sgd", [64, 128, TC], BF16).ap()
    hidd = nc.dram_tensor("hidd", [64, 128, TPAD], F32).ap()
    Bcd = nc.dram_tensor("Bcd", [8, 128, TPAD], F32).ap()
    Ccd = nc.dram_tensor("Ccd", [8, 128, TPAD], F32).ap()
    dtod = nc.dram_tensor("dtod", [128, TPAD], F32).ap()
    ygfd = nc.dram_tensor("ygfd", [8192, TC], BF16).ap()
    w1scr = nc.dram_tensor("w1scr", [128, 32, 32, 128], BF16).ap()
    oaccd = nc.dram_tensor("oaccd", [D_, TC], F32).ap()
    if dbg_stage:
        dbgo = nc.dram_tensor("dbgo", [8192, 2048], F32, kind="ExternalOutput").ap()

    with ExitStack() as ctx:
        ar = _Arena(nc, ctx, "AR", 201600)
        idbs = ctx.enter_context(nc.sbuf_tensor("idbs", [128, 128], BF16))
        idfs = ctx.enter_context(nc.sbuf_tensor("idfs", [128, 128], F32))
        trius = ctx.enter_context(nc.sbuf_tensor("trius", [128, 128], F32))
        ons16 = ctx.enter_context(nc.sbuf_tensor("ons16", [16, 128], F32))
        i16s = ctx.enter_context(nc.sbuf_tensor("i16s", [16, 16], F32))
        idb64s = ctx.enter_context(nc.sbuf_tensor("idb64s", [128, 64], BF16))
        dps = ctx.enter_context(nc.sbuf_tensor("dps", [128, 128], F32))
        cst = ctx.enter_context(nc.sbuf_tensor("cst", [128, 4], F32))
        vcs = ctx.enter_context(nc.sbuf_tensor("vcs", [128, 2], F32))
        scs = ctx.enter_context(nc.sbuf_tensor("scs", [128, 2], F32))
        smsk = ctx.enter_context(nc.sbuf_tensor("smsk", [128, 1], F32))
        nwsp = ctx.enter_context(nc.sbuf_tensor("nwsp", [128, 32], F32))
        P0 = ctx.enter_context(nc.psum_tensor("P0", [128, 512], F32))
        P1 = ctx.enter_context(nc.psum_tensor("P1", [128, 512], F32))
        P2 = ctx.enter_context(nc.psum_tensor("P2", [128, 512], F32))
        P3 = ctx.enter_context(nc.psum_tensor("P3", [128, 512], F32))
        P4 = ctx.enter_context(nc.psum_tensor("P4", [128, 512], F32))
        P5 = ctx.enter_context(nc.psum_tensor("P5", [128, 512], F32))
        PT0 = ctx.enter_context(nc.psum_tensor("PT0", [128, 1024], BF16))
        PT1 = ctx.enter_context(nc.psum_tensor("PT1", [128, 1024], BF16))
        PF = [P0, P1, P2, P3, P4, P5]
        PT = [PT0, PT1]

        SM = {}

        def sem(name):
            SM[name] = ctx.enter_context(nc.semaphore("s_" + name))
            return SM[name]

        big = ar.take(69632, BF16)
        xsb = big
        hfm = big
        ar_mark = ar.off

        block = ctx.enter_context(nc.Block())

        for nm in ["cldA", "ldA0", "ldA1", "bnA", "rsA", "afA", "tpA", "txA",
                   "mldB0", "mldB1", "mrdB", "msumB", "bmB", "pldB", "eldB0",
                   "eldB1", "szlB0", "szlB1", "wldB0", "wldB1", "mxB", "wsB",
                   "tpB", "txB", "mmB", "evB", "stB", "fenV", "fesS",
                   "o0C0", "o0C1", "ldC0", "ldC1", "bnC", "rsC", "toC", "adC",
                   "afC", "tpC", "txC", "stC",
                   "wD0", "wD1", "mmD", "evD", "cvD", "sgD", "hcD", "spD",
                   "dtD", "stgD", "sthD", "stdD",
                   "uclE", "hlE0", "hlE1", "slE0", "slE1", "acsE", "elE",
                   "dmlE", "btE", "btvE", "dttE", "mtE", "mskE", "sgtE",
                   "dmmE", "ltmE", "lteE", "ylhE", "htE", "hdeE", "ydE",
                   "smE", "steE", "dgvE", "ebmE", "ebeE",
                   "yoE", "yevE", "yaE", "ygE", "sqE",
                   "ystE0", "actE", "pvE", "ugE", "corE",
                   "ybfE", "ytE", "ytxE", "ttC",
                   "qlF0", "qlF1", "yglF",
                   "pg"]:
            sem(nm)
        for fn, pn in [("dqF", "bnA"),
                       ("tpF", "tpA"), ("txF", "txA"), ("w1stF", "rsA"),
                       ("gnfF", "mrdB"), ("wlF0", "wldB0"),
                       ("wlF1", "wldB1"), ("mmaF", "mmB"), ("evaF", "evB"),
                       ("oastF", "stB"), ("xnlF", "msumB"), ("w1lF0", "szlB0"),
                       ("w1lF1", "szlB1"), ("x1lF0", "ldC0"), ("x1lF1", "ldC1"),
                       ("oalF0", "o0C0"), ("oalF1", "o0C1"), ("mmfF", "mxB"),
                       ("evfF", "wsB"), ("ostF", "stC"), ("otF", "toC"),
                       ("otxF", "adC")]:
            SM[fn] = SM[pn]
        _emit_fused(nc, block, SM, locals())
    return nc


def _emit_fused(nc, block, S, env):
    g = lambda k: env[k]
    ar = g("ar"); ar_mark = g("ar_mark")
    xsb, hfm = g("xsb"), g("hfm")
    PF, PT = g("PF"), g("PT")
    P0, P1, P2, P3, P4, P5 = PF
    PT0, PT1 = PT
    idbs, idfs, trius, ons16, i16s, idb64s, dps, cst, vcs, scs, smsk = (
        g("idbs"), g("idfs"), g("trius"), g("ons16"), g("i16s"), g("idb64s"),
        g("dps"), g("cst"), g("vcs"), g("scs"), g("smsk"))
    nwsp = g("nwsp")
    (x0s, Wnk, Qnk, maskA, szr, Ebm, Pbm, WinT, nwf, convw, convb, dtb, alog8,
     dpb, OutWT, A1, ws1, wz1, bias1f, gnwf, identb, idf, triu, ones16, i16,
     idb64, negrow, smask, sgidx, out) = (
        g("x0s"), g("Wnk"), g("Qnk"), g("maskA"), g("szr"), g("Ebm"), g("Pbm"),
        g("WinT"), g("nwf"), g("convw"), g("convb"), g("dtb"), g("alog8"),
        g("dpb"), g("OutWT"), g("A1"), g("ws1"), g("wz1"), g("bias1f"),
        g("gnwf"), g("identb"), g("idf"), g("triu"), g("ones16"), g("i16"),
        g("idb64"), g("negrow"), g("smask"), g("sgidx"), g("out"))
    (o0f, x1blk, xn1blk, sgd, hidd, Bcd, Ccd, dtod,
     ygfd, w1scr, oaccd) = (
        g("o0f"), g("x1blk"), g("xn1blk"), g("sgd"), g("hidd"), g("Bcd"),
        g("Ccd"), g("dtod"), g("ygfd"), g("w1scr"), g("oaccd"))
    hblk = g("hblk")

    T = TPAD
    NT = 17
    NG, NCH2 = 8, 17

    ar.off = ar_mark
    A_xin = ar.take(2 * D, F32)
    A_xnt = ar.take(2 * D, BF16)
    A_stats = ar.take(2 * 48, F32)
    A_mv = ar.take(4, F32)
    A_tiny = ar.take(16, F32)
    assert ar.off <= ar.nbytes

    ar.off = ar_mark
    B_wbf = ar.take(2 * D, BF16)
    B_qbf = ar.take(2 * D, BF16)
    B_wsum = ar.take(D, BF16)
    B_wT = ar.take(D, BF16)
    B_szc = ar.take(2 * 64, F32)
    B_zpc = ar.take(2 * 32, F32)
    B_esb = ar.take(2 * 128, BF16)
    B_mred = ar.take(2 * 128, BF16)
    B_bm = ar.take(128, BF16)
    B_psl = ar.take(128, BF16)
    B_osb = ar.take(T, F32)
    assert ar.off <= ar.nbytes

    ar.off = 0
    C_xa = ar.take(2 * D, F32)
    C_o0c = ar.take(2 * D, F32)
    C_x1 = ar.take(D, F32)
    C_xnt = ar.take(D, BF16)
    C_ht = ar.take(D, BF16)
    C_stats = ar.take(48, F32)
    C_mv = ar.take(4, F32)
    C_tiny = ar.take(16, F32)
    C_stg = ar.take(D, BF16)
    C_stg1 = ar.take(D, F32)
    C_stgH = ar.take(D, BF16)
    assert ar.off <= ar.nbytes

    ar.off = ar_mark
    D_wsb = ar.take(2 * 4096, BF16)
    D_cvin = ar.take(T + 3, F32)
    D_acc0 = ar.take(T, F32)
    D_acc1 = ar.take(T, F32)
    D_dtsgb = ar.take(T, F32)
    D_sgb = D_dtsgb.bitcast(BF16)[:, 0:TC]
    D_hcb = ar.take(T, F32)
    D_dts = D_dtsgb
    D_cws = ar.take(320, F32)
    D_cbs = ar.take(80, F32)
    D_dtbs = ar.take(1, F32)
    D_dtv = D_cvin
    assert ar.off <= ar.nbytes

    ar.off = 0
    E_Bb = ar.take(T, BF16)
    E_Cb = ar.take(T, BF16)
    E_dtu = ar.take(T, F32)
    E_acs2 = ar.take(128, F32)
    E_acv = ar.take(128, F32)
    E_hidc = ar.take(2 * 1024, BF16)
    E_sgc = ar.take(2 * 1024, BF16)
    E_sgt = ar.take(1024, BF16)
    E_hdtp = ar.take(1024, BF16)
    E_hcpp = ar.take(1024, BF16)
    E_srhp = ar.take(1024, BF16)
    E_prevb = ar.take(2 * 1024, F32)
    E_stb = ar.take(2 * 1024, F32)
    E_prevbf = ar.take(1024, BF16)
    E_LTm = ar.take(2 * 128, F32)
    E_LTe = ar.take(2 * 128, F32)
    E_Ylh = ar.take(2 * 128, BF16)
    E_Msk = ar.take(128, F32)
    E_BTs = ar.take(128, BF16)
    E_ddv = ar.take(2, F32)
    E_dtT = ar.take(16, F32)
    E_acsT = ar.take(16, F32)
    E_EA = ar.take(16, F32)
    E_Ebc = ar.take(16, F32)
    E_ell = ar.take(2, F32)
    E_alg = ar.take(16, F32)
    E_nega = ar.take(1, F32)
    E_diag = ar.take(16, F32)
    E_dop = ar.take(2 * 2048, F32)
    E_ysb = ar.take(1024, F32)
    E_y2b = ar.take(1024, F32)
    E_ygb = ar.take(1024, F32)
    E_ssq = ar.take(1, F32)
    E_rgs = ar.take(2, F32)
    E_ybf = ar.take(1024, BF16)
    E_ygT = ar.take(1024, BF16)
    assert ar.off <= ar.nbytes, f"E arena {ar.off}"

    ar.off = ar_mark
    F_un = ar.take(8192, I32)
    F_wsb = F_un.bitcast(BF16)
    F_w1sb = F_un.bitcast(BF16)[:, 0:2 * 4096]
    F_ob = F_un.bitcast(F32)[:, 4096:6144]
    F_xs = F_un.bitcast(F32)[:, 6144:8192]
    F_oa = ar.take(2 * 1024, F32)
    F_tmp = ar.take(1024, F32)
    F_otb = F_tmp
    F_ws1s = ar.take(32, F32)
    F_wz1s = ar.take(32, F32)
    F_zs1 = ar.take(32, F32)
    F_b1s = ar.take(32, F32)
    F_gnws = ar.take(64, F32)
    assert ar.off <= ar.nbytes, f"F arena {ar.off}"
    F_ygh = xsb[:, 0:64 * 1024]
    F_qv = g("big")[:, 0:16384].bitcast(I32)
    F_dq = xsb[:, 32 * 1024:32 * 1024 + D]
    F_w1T = xsb[:, 40 * 1024:40 * 1024 + D]
    F_xnh = xsb[:, 0:32 * 1024]

    FB = {"qlF0": 0, "qlF1": 0, "dqF": 17, "tpF": 544, "txF": 1568,
          "w1stF": 17, "yglF": 0, "gnfF": 32, "wlF0": 512, "wlF1": 512,
          "mmaF": 32, "evaF": 32, "oastF": 512, "xnlF": 32, "w1lF0": 256,
          "w1lF1": 256, "x1lF0": 144, "x1lF1": 128, "oalF0": 576,
          "oalF1": 512, "mmfF": 32, "evfF": 32, "ostF": 3264, "otF": 544,
          "otxF": 544}
    MAINB = [(0, 128, 512), (1, 640, 512), (2, 1152, 512), (3, 1664, 512)]
    HALOB = (4, 0, 128)
    psDm = [P2, P3]

    def role_d(nb):
        return "gate" if nb < 64 else ("hbc" if nb < 144 else "dt")

    NPH = 6

    hid3E = E_hidc.rearrange("p (b r t) -> p b r t", b=2, r=8)

    @block.sync
    def _(sync):
        for ap_, sb in [(identb, idbs), (idf, idfs), (triu, trius),
                        (ones16, ons16), (i16, i16s), (idb64, idb64s),
                        (dpb, dps), (smask, smsk), (nwf, nwsp)]:
            sync.dma_start(out=sb[:], in_=ap_[:]).then_inc(S["cldA"], 16)
        for i in range(NT):
            p = i % 2
            if i >= 2:
                sync.wait_ge(S["afA"], i - 1)
            sync.dma_start(out=A_xin[:, p * D:(p + 1) * D],
                           in_=x0s[i * 128:(i + 1) * 128, :]).then_inc(
                S["ldA0" if p == 0 else "ldA1"], 16)
        sync.nop().then_inc(S["pg"], 1)
        sync.wait_ge(S["pg"], 5)
        for t in range(32):
            pe = t % 2
            if t >= 2:
                sync.wait_ge(S["msumB"], t - 1)
            sync.dma_start(out=B_esb[:, pe * 128:(pe + 1) * 128],
                           in_=Ebm[:, t * 128:(t + 1) * 128]).then_inc(
                S["eldB0" if pe == 0 else "eldB1"], 16)
        for ncn in range(32):
            sync.wait_ge(S["evB"], ncn + 1)
            sync.dma_start(out=o0f[ncn * 128:(ncn + 1) * 128, :],
                           in_=B_osb[:]).then_inc(S["stB"], 16)
        sync.wait_ge(S["stB"], 16 * 32)
        sync.nop().then_inc(S["pg"], 1)
        sync.wait_ge(S["pg"], 10)
        for i in range(NT):
            p = i % 2
            if i >= 2:
                sync.wait_ge(S["adC"], 32 * (i - 1))
            sync.dma_start(out=C_xa[:, p * D:(p + 1) * D],
                           in_=x0s[i * 128:(i + 1) * 128, :]).then_inc(
                S["ldC0" if p == 0 else "ldC1"], 16)
            for q4 in range(4):
                sync.dma_start(
                    out=C_o0c[:, p * D + q4 * 1024: p * D + (q4 + 1) * 1024]
                    .rearrange("p (a t) -> p a t", t=128),
                    in_=o0f[q4 * 1024:(q4 + 1) * 1024, i * 128:(i + 1) * 128]
                    .rearrange("(a p) t -> p a t", p=128)).then_inc(
                    S["o0C0" if p == 0 else "o0C1"], 16)
            sync.wait_ge(S["tpC"], 96 * (i + 1))
            for q4 in range(4):
                sync.dma_start(
                    out=xn1blk[i, q4 * 8:(q4 + 1) * 8].rearrange("a p t -> p a t"),
                    in_=C_stg[:, q4 * 1024:(q4 + 1) * 1024]
                    .rearrange("p (a t) -> p a t", t=128)).then_inc(S["stC"], 16)
                sync.dma_start(
                    out=x1blk[i, q4 * 8:(q4 + 1) * 8].rearrange("a p t -> p a t"),
                    in_=C_stg1[:, q4 * 1024:(q4 + 1) * 1024]
                    .rearrange("p (a t) -> p a t", t=128)).then_inc(S["stC"], 16)
                sync.dma_start(
                    out=hblk[i, q4 * 8:(q4 + 1) * 8].rearrange("a p t -> p a t"),
                    in_=C_stgH[:, q4 * 1024:(q4 + 1) * 1024]
                    .rearrange("p (a t) -> p a t", t=128)).then_inc(S["stC"], 16)
        sync.wait_ge(S["stC"], 16 * 12 * NT)
        sync.nop().then_inc(S["pg"], 1)
        sync.wait_ge(S["pg"], 15)
        for kc in range(32):
            sync.dma_start(
                out=hfm[:, kc * T: kc * T + 1024].rearrange("p (a t) -> p a t", t=128),
                in_=hblk[0:8, kc].rearrange("a p t -> p a t")).then_inc(S["txA"], 16)
            sync.dma_start(
                out=hfm[:, kc * T + 1024: kc * T + 2176].rearrange("p (a t) -> p a t", t=128),
                in_=hblk[8:17, kc].rearrange("a p t -> p a t")).then_inc(S["txA"], 16)
        sync.dma_start(out=D_cws[:], in_=convw[:]).then_inc(S["cldA"], 16)
        sync.dma_start(out=D_cbs[:], in_=convb[:]).then_inc(S["cldA"], 16)
        sync.dma_start(out=D_dtbs[:], in_=dtb[:]).then_inc(S["cldA"], 16)
        isg = 0; ihc = 0
        for nb in range(145):
            r = role_d(nb)
            if r == "gate":
                isg += 1
                sync.wait_ge(S["sgD"], isg)
                sync.dma_start(out=sgd[nb], in_=D_sgb[:]).then_inc(S["stgD"], 16)
            elif r == "hbc":
                cc = nb - 64
                ihc += 1
                sync.wait_ge(S["hcD"], ihc)
                dst = hidd[cc] if cc < 64 else (Bcd[cc - 64] if cc < 72 else Ccd[cc - 72])
                sync.dma_start(out=dst, in_=D_hcb[:]).then_inc(S["sthD"], 16)
            else:
                sync.wait_ge(S["dtD"], 1)
                sync.dma_start(out=dtod[:], in_=D_cvin[:, 0:T]).then_inc(S["stdD"], 16)
        sync.wait_ge(S["stgD"], 16 * 64)
        sync.wait_ge(S["sthD"], 16 * 80)
        sync.wait_ge(S["stdD"], 16)
        sync.nop().then_inc(S["pg"], 1)
        sync.wait_ge(S["pg"], 20)
        for gix in range(NG):
            for c in range(1, NCH2):
                ny = gix * 16 + c
                sync.wait_ge(S["ytxE"], 8 * ny)
                sync.dma_start(out=ygfd[gix * 1024:(gix + 1) * 1024,
                                        (c - 1) * 128: c * 128]
                               .rearrange("(r p) t -> p r t", p=128),
                               in_=E_ygT[:].rearrange("p (r t) -> p r t", t=128)
                               ).then_inc(S["ystE0"], 16)
        sync.wait_ge(S["ystE0"], 16 * NG * 16)
        sync.nop().then_inc(S["pg"], 1)
        sync.wait_ge(S["pg"], 25)
        for ap_, sb in [(ws1, F_ws1s), (wz1, F_wz1s), (bias1f, F_b1s), (gnwf, F_gnws)]:
            sync.dma_start(out=sb[:], in_=ap_[:]).then_inc(S["cldA"], 16)
        for ncn in range(32):
            sync.wait_ge(S["txF"], FB["txF"] + 32 * (ncn + 1))
            sync.dma_start(out=w1scr[:, ncn], in_=F_w1T.rearrange("p (a b) -> p a b", b=128)
                           ).then_inc(S["w1stF"], 16)
        for th in range(2):
            if th == 1:
                sync.wait_ge(S["mmaF"], FB["mmaF"] + 32)
            for kc in range(64):
                sync.dma_start(out=F_ygh[:, kc * 1024:(kc + 1) * 1024],
                               in_=ygfd[kc * 128:(kc + 1) * 128, th * 1024:(th + 1) * 1024]
                               ).then_inc(S["yglF"], 16)
            for nb in range(32):
                i = th * 32 + nb
                sync.wait_ge(S["evaF"], FB["evaF"] + i + 1)
                sync.dma_start(out=oaccd[nb * 128:(nb + 1) * 128, th * 1024:(th + 1) * 1024],
                               in_=F_oa[:, (i % 2) * 1024:(i % 2 + 1) * 1024]).then_inc(S["oastF"], 16)
        for th in range(2):
            if th == 0:
                sync.wait_ge(S["mmaF"], FB["mmaF"] + 64)
            else:
                sync.wait_ge(S["mmfF"], FB["mmfF"] + 32)
            for kc in range(32):
                sync.dma_start(out=F_xnh[:, kc * 1024:(kc + 1) * 1024]
                               .rearrange("p (a t) -> p a t", t=128),
                               in_=xn1blk[1 + th * 8: 9 + th * 8, kc].rearrange("a p t -> p a t")
                               ).then_inc(S["xnlF"], 16)
            for nb in range(32):
                i = th * 32 + nb
                p = i % 2
                if i >= 2:
                    sync.wait_ge(S["evfF"], FB["evfF"] + i - 1)
                sync.dma_start(out=F_xs[:, p * 1024:(p + 1) * 1024]
                               .rearrange("p (a t) -> p a t", t=128),
                               in_=x1blk[1 + th * 8: 9 + th * 8, nb].rearrange("a p t -> p a t")
                               ).then_inc(S["x1lF0" if p == 0 else "x1lF1"], 16)
                sync.wait_ge(S["oastF"], FB["oastF"] + 16 * (th * 32 + nb + 1) if th == 0 else 16 * 64)
                sync.dma_start(out=F_ob[:, p * 1024:(p + 1) * 1024],
                               in_=oaccd[nb * 128:(nb + 1) * 128, th * 1024:(th + 1) * 1024]
                               ).then_inc(S["oalF0" if p == 0 else "oalF1"], 16)
                sync.wait_ge(S["otxF"], FB["otxF"] + 8 * (i + 1))
                sync.dma_start(out=out[th * 1024:(th + 1) * 1024, nb * 128:(nb + 1) * 128]
                               .rearrange("(a p) n -> p a n", p=128),
                               in_=F_otb[:].rearrange("p (a n) -> p a n", n=128)
                               ).then_inc(S["ostF"], 16)
        sync.wait_ge(S["ostF"], FB["ostF"] + 16 * 64)
        sync.nop().then_inc(S["pg"], 1)

    @block.gpsimd
    def _(gp):
        gp.nop().then_inc(S["pg"], 1)
        gp.wait_ge(S["pg"], 5)
        for t in range(32):
            p = t % 2
            if t >= 2:
                gp.wait_ge(S["mrdB"], t - 1)
            gp.dma_start(out=B_wbf[:, p * D:(p + 1) * D],
                         in_=maskA[t * 128:(t + 1) * 128, :]).then_inc(
                S["mldB0" if p == 0 else "mldB1"], 16)
        for ncn in range(32):
            p = ncn % 2
            if ncn >= 1:
                gp.wait_ge(S["mxB"], ncn)
            gp.dma_start(out=B_psl[:], in_=Pbm[:, ncn * 128:(ncn + 1) * 128]).then_inc(S["pldB"], 16)
            if ncn >= 2:
                gp.wait_ge(S["wsB"], ncn - 1)
            gp.dma_start(out=B_szc[:, p * 64:(p + 1) * 64],
                         in_=szr[:, ncn * 64:(ncn + 1) * 64]).then_inc(
                S["szlB0" if p == 0 else "szlB1"], 16)
            if ncn == 0:
                gp.wait_ge(S["bmB"], 1)
            gp.dma_start(out=B_wbf[:, p * D:(p + 1) * D],
                         in_=Wnk[ncn * 128:(ncn + 1) * 128, :]).then_inc(
                S["wldB0" if p == 0 else "wldB1"], 16)
            gp.dma_start(out=B_qbf[:, p * D:(p + 1) * D],
                         in_=Qnk[ncn * 128:(ncn + 1) * 128, :]).then_inc(
                S["wldB0" if p == 0 else "wldB1"], 16)
        gp.nop().then_inc(S["pg"], 1)
        gp.wait_ge(S["pg"], 10)
        gp.nop().then_inc(S["pg"], 1)
        gp.wait_ge(S["pg"], 15)
        for nb in range(145):
            p = nb % 2
            if nb >= 2:
                gp.wait_ge(S["mmD"], nb - 1)
            gp.dma_start(out=D_wsb[:, p * 4096:(p + 1) * 4096]
                         .rearrange("p (a b) -> p a b", b=128),
                         in_=WinT[:, nb]).then_inc(S["wD0" if p == 0 else "wD1"], 16)
        gp.nop().then_inc(S["pg"], 1)
        gp.wait_ge(S["pg"], 20)
        gp.dma_start(out=E_dop[0:2, 2048:4096], in_=negrow[:]).then_inc(S["dmlE"], 16)
        for gix in range(NG):
            if gix >= 1:
                gp.wait_ge(S["ugE"], gix)
            gp.dma_start(out=E_Bb[:], in_=Bcd[gix]).then_inc(S["uclE"], 16)
            gp.dma_start(out=E_Cb[:], in_=Ccd[gix]).then_inc(S["uclE"], 16)
            gp.dma_start(out=E_dtu[0:16, :], in_=dtod[gix * 16:(gix + 1) * 16, :]).then_inc(S["uclE"], 16)
            gp.dma_start(out=E_alg[0:16, 0:8], in_=alog8[:, :]).then_inc(S["uclE"], 16)
            for c in range(NCH2):
                ci = gix * NCH2 + c
                p = ci % 2
                if ci >= 2:
                    gp.wait_ge(S["htE"], 16 * (ci - 1))
                gp.dma_start(out=hid3E[:, p],
                             in_=hidd[gix * 8:(gix + 1) * 8, :, c * 128:(c + 1) * 128]
                             .rearrange("r p t -> p r t")).then_inc(
                    S["hlE0" if p == 0 else "hlE1"], 16)
                if c >= 1:
                    ny = gix * 16 + c
                    if ny >= 3:
                        gp.wait_ge(S["sgtE"], ny - 2)
                    gp.dma_start(out=E_sgc[:, (ny % 2) * 1024:(ny % 2 + 1) * 1024]
                                 .rearrange("p (r t) -> p r t", r=8),
                                 in_=sgd[gix * 8:(gix + 1) * 8, :, (c - 1) * 128: c * 128]
                                 .rearrange("r p t -> p r t")).then_inc(
                        S["slE0" if ny % 2 == 0 else "slE1"], 16)
                gp.wait_ge(S["acsE"], ci + 1)
                if ci >= 1:
                    gp.wait_ge(S["dmmE"], 16 * ci)
                gp.dma_start(out=E_dop[1:2, 0:2048], in_=E_acs2[0:16, :]).then_inc(S["dmlE"], 16)
                gp.dma_start(out=E_dop[0:1, 2048:4096], in_=E_acs2[0:16, :]).then_inc(S["dmlE"], 16)
        gp.nop().then_inc(S["pg"], 1)
        gp.wait_ge(S["pg"], 25)
        for ncn in range(32):
            p = ncn % 2
            if ncn >= 2:
                gp.wait_ge(S["dqF"], FB["dqF"] + ncn - 1)
            gp.dma_start(out=F_qv[:, p * D:(p + 1) * D],
                         in_=A1[ncn * 128:(ncn + 1) * 128, :]).then_inc(
                S["qlF0" if p == 0 else "qlF1"], 16)
        for th in range(2):
            for nb in range(32):
                i = th * 32 + nb
                p = i % 2
                if i >= 2:
                    gp.wait_ge(S["mmaF"], FB["mmaF"] + i - 1)
                gp.dma_start(out=F_wsb[:, p * 8192:(p + 1) * 8192]
                             .rearrange("p (a b) -> p a b", b=128),
                             in_=OutWT[:, nb]).then_inc(S["wlF0" if p == 0 else "wlF1"], 16)
        for th in range(2):
            for nb in range(32):
                i = th * 32 + nb
                p = i % 2
                if i == 0:
                    gp.wait_ge(S["w1stF"], FB["w1stF"] + 16 * 32)
                    gp.wait_ge(S["mmaF"], FB["mmaF"] + 64)
                if i >= 2:
                    gp.wait_ge(S["mmfF"], FB["mmfF"] + i - 1)
                gp.dma_start(out=F_w1sb[:, p * 4096:(p + 1) * 4096]
                             .rearrange("p (a b) -> p a b", b=128),
                             in_=w1scr[:, nb]).then_inc(S["w1lF0" if p == 0 else "w1lF1"], 16)
        gp.nop().then_inc(S["pg"], 1)

    @block.vector
    def _(v):
        nfen = [0]

        def fence(r):
            r.then_inc(S["fenV"], 1)
            nfen[0] += 1
            v.wait_ge(S["fenV"], nfen[0])

        def fence_new():
            fence(v.tensor_copy(vcs[:, 0:1], vcs[:, 1:2]))

        v.memset(vcs[:], 0.0)
        v.memset(cst[:, 0:1], EPS)
        fence(v.memset(cst[:, 1:2], EPS * EPS))
        for i in range(NT):
            p = i % 2
            q = p * 8
            v.wait_ge(S["ldA0" if p == 0 else "ldA1"], 16 * (i // 2 + 1))
            if i >= 2:
                v.wait_ge(S["rsA"], i - 1)
            for j in range(8):
                r = v.bn_stats(A_stats[:, p * 48 + j * 6: p * 48 + (j + 1) * 6],
                               A_xin[:, p * D + j * 512: p * D + (j + 1) * 512])
            fence(r)
            v.bn_aggr(A_mv[:, p * 2:(p + 1) * 2], A_stats[:, p * 48:(p + 1) * 48]).then_inc(S["bnA"], 1)
            v.wait_ge(S["rsA"], i + 1)
            if i >= 2:
                v.wait_ge(S["tpA"], 32 * (i - 1))
            v.tensor_scalar(A_xnt[:, p * D:(p + 1) * D], A_xin[:, p * D:(p + 1) * D],
                            A_tiny[:, q + 2:q + 3], A_tiny[:, q + 0:q + 1],
                            ALU.add, ALU.mult).then_inc(S["afA"], 1)
        v.nop().then_inc(S["pg"], 1)
        v.wait_ge(S["pg"], 5)
        for t in range(32):
            p = t % 2
            v.wait_ge(S["mldB0" if p == 0 else "mldB1"], 16 * (t // 2 + 1))
            if t >= 2:
                v.wait_ge(S["msumB"], t - 1)
            with nc.allow_low_precision(reason="mask 0/1 sums<=32 exact in bf16"):
                v.reduce_sum(B_mred[:, p * 128:(p + 1) * 128],
                             B_wbf[:, p * D:(p + 1) * D].rearrange("p (a b) -> p a b", b=32),
                             axis=mybir.AxisListType.X).then_inc(S["mrdB"], 1)
        v.wait_ge(S["msumB"], 32)
        v.tensor_scalar(B_bm[:], P5[:, 0:128], 0.0, None, ALU.is_gt).then_inc(S["bmB"], 1)
        for ncn in range(32):
            p = ncn % 2
            v.wait_ge(S["wldB0" if p == 0 else "wldB1"], 32 * (ncn // 2 + 1))
            v.wait_ge(S["szlB0" if p == 0 else "szlB1"], 16 * (ncn // 2 + 1))
            szc3 = B_szc[:, p * 64:(p + 1) * 64].rearrange("p (g c) -> p g c", c=2)
            fence(v.scalar_tensor_tensor(B_zpc[:, p * 32:(p + 1) * 32], szc3[:, :, 0], -8.0,
                                         szc3[:, :, 1], ALU.mult, ALU.add))
            if ncn >= 1:
                v.wait_ge(S["tpB"], 32 * ncn)
            for gx in range(32):
                v.tensor_scalar(B_wsum[:, gx * 128:(gx + 1) * 128],
                                B_qbf[:, p * D + gx * 128: p * D + (gx + 1) * 128],
                                szc3[:, gx:gx + 1, 0], B_zpc[:, p * 32 + gx: p * 32 + gx + 1],
                                ALU.mult, ALU.add)
            v.wait_ge(S["mxB"], ncn + 1)
            wmv = B_qbf[:, p * D:(p + 1) * D]
            fence(v.tensor_tensor(wmv.rearrange("p (a b) -> p a b", b=32),
                                  B_wbf[:, p * D:(p + 1) * D].rearrange("p (a b) -> p a b", b=32),
                                  P5[:, 0:128].unsqueeze(2).broadcast_to([128, 128, 32]),
                                  ALU.mult))
            v.tensor_tensor(B_wsum[:], B_wsum[:], wmv, ALU.add).then_inc(S["wsB"], 1)
            v.wait_ge(S["mmB"], ncn + 1)
            if ncn >= 1:
                v.wait_ge(S["stB"], 16 * ncn)
            for tb, (off, sz) in enumerate([(0, 512), (512, 512), (1024, 512), (1536, 512), (2048, 128)]):
                r = v.tensor_copy(B_osb[:, off:off + sz], PF[tb][:, 0:sz])
            r.then_inc(S["evB"], 1)
        v.nop().then_inc(S["pg"], 1)
        v.wait_ge(S["pg"], 10)
        for i in range(NT):
            p = i % 2
            v.wait_ge(S["ldC0" if p == 0 else "ldC1"], 16 * (i // 2 + 1))
            if i >= 1:
                v.wait_ge(S["tpC"], 96 * i)
            for ncx in range(32):
                v.wait_ge(S["toC"], i * 32 + ncx + 1)
                r = v.tensor_tensor(C_x1[:, ncx * 128:(ncx + 1) * 128],
                                    PF[ncx % 2][:, 0:128],
                                    C_xa[:, p * D + ncx * 128: p * D + (ncx + 1) * 128],
                                    ALU.add)
                r.then_inc(S["adC"], 1)
            fence_new()
            for j in range(8):
                r = v.bn_stats(C_stats[:, j * 6:(j + 1) * 6], C_x1[:, j * 512:(j + 1) * 512])
            fence(r)
            v.bn_aggr(C_mv[:, 0:2], C_stats[:]).then_inc(S["bnC"], 1)
            v.wait_ge(S["rsC"], i + 1)
            v.tensor_scalar(C_xnt[:], C_x1[:], C_tiny[:, 2:3], C_tiny[:, 0:1],
                            ALU.add, ALU.mult)
            r = v.tensor_scalar(C_ht[:], C_x1[:], C_tiny[:, 3:4], C_tiny[:, 1:2],
                                ALU.add, ALU.mult)
            r.then_inc(S["afC"], 1)
        v.nop().then_inc(S["pg"], 1)
        v.wait_ge(S["pg"], 15)
        v.wait_ge(S["cldA"], 16 * 12)
        for nb in range(145):
            r = role_d(nb)
            if r == "gate":
                continue
            v.wait_ge(S["mmD"], nb + 1)
            if r == "hbc":
                cc = nb - 64
                v.memset(D_cvin[:, 0:3], 0.0)
                v.tensor_copy(D_cvin[:, 3:131], P4[:, 0:128])
                for (bk, off, sz) in MAINB:
                    rr = v.tensor_copy(D_cvin[:, 3 + off:3 + off + sz], PF[bk][:])
                rr.then_inc(S["evD"], 1)
                fence_new()
                if cc >= 1:
                    v.wait_ge(S["hcD"], cc)
                fence(v.tensor_scalar_mul(D_acc0[:], D_cvin[:, 0:T],
                                          D_cws[:, cc * 4:cc * 4 + 1]))
                fence(v.scalar_tensor_tensor(D_acc1[:], D_cvin[:, 1:1 + T],
                                             D_cws[:, cc * 4 + 1:cc * 4 + 2], D_acc0[:],
                                             ALU.mult, ALU.add))
                fence(v.scalar_tensor_tensor(D_acc0[:], D_cvin[:, 2:2 + T],
                                             D_cws[:, cc * 4 + 2:cc * 4 + 3], D_acc1[:],
                                             ALU.mult, ALU.add))
                v.scalar_tensor_tensor(D_acc1[:], D_cvin[:, 3:3 + T],
                                       D_cws[:, cc * 4 + 3:cc * 4 + 4], D_acc0[:],
                                       ALU.mult, ALU.add).then_inc(S["cvD"], 1)
            else:
                rr = v.tensor_scalar_add(D_dtv[:, 0:128], P4[:, 0:128], D_dtbs[:])
                for (bk, off, sz) in MAINB:
                    rr = v.tensor_scalar_add(D_dtv[:, off:off + sz], PF[bk][:], D_dtbs[:])
                rr.then_inc(S["evD"], 1)
                fence_new()
                v.tensor_copy(vcs[:, 0:1], vcs[:, 1:2]).then_inc(S["cvD"], 1)
                v.wait_ge(S["spD"], 1)
                fence(v.tensor_relu(D_dts[:], D_dtv[:, 0:T]))
                v.tensor_tensor(D_dtv[:, 0:T], D_dts[:], D_acc0[:], ALU.add).then_inc(S["dtD"], 1)
        v.nop().then_inc(S["pg"], 1)
        v.wait_ge(S["pg"], 20)
        v.memset(E_dop[0:1, 0:2048], 1.0)
        fence(v.memset(vcs[:], 0.0))
        for gix in range(NG):
            v.wait_ge(S["uclE"], 16 * 4 * (gix + 1))
            v.wait_ge(S["elE"], gix * 17 + 1)
            fence(v.tensor_scalar_mul(E_nega[0:16, :], E_ell[0:16, 1:2], -1.0))
            v.memset(E_prevb[:, 0:1024], 0.0)
            fence(v.memset(E_prevb[:, 1024:2048], 0.0))
            for c in range(NCH2):
                ci = gix * NCH2 + c
                cp = c % 2
                ny = gix * 16 + c
                if ci >= 1:
                    v.wait_ge(S["dmlE"], 16 + 32 * ci)
                    v.wait_ge(S["btE"], 2 * (ci - 1) + 1)
                    v.wait_ge(S["elE"], gix * 17 + c + 1)
                fence(v.tensor_scalar_mul(E_acv[0:16, :], E_dtu[0:16, c * 128:(c + 1) * 128],
                                          E_nega[0:16, :]))
                v.tensor_tensor_scan(E_acs2[0:16, :], ons16[:], E_acv[0:16, :], 0.0,
                                     ALU.mult, ALU.add).then_inc(S["acsE"], 1)
                v.wait_ge(S["btE"], 2 * (ci + 1))
                v.tensor_copy(E_BTs[:], PT0[:, 0:128]).then_inc(S["btvE"], 1)
                if c >= 1:
                    v.wait_ge(S["mtE"], ny)
                    v.tensor_tensor(E_Msk[:], P4[:, 0:128], trius[:], ALU.mult).then_inc(S["mskE"], 1)
                    v.wait_ge(S["slE0" if ny % 2 == 0 else "slE1"], 16 * ((ny - 1) // 2 + 1))
                    v.tensor_copy(E_sgt[:], PT1[:]).then_inc(S["sgtE"], 1)
                    v.wait_ge(S["elE"], gix * 17 + c + 1)
                    v.tensor_scalar_mul(E_diag[0:16, :], i16s[:],
                                        E_ell[0:16, 0:1]).then_inc(S["dgvE"], 1)
                    v.wait_ge(S["ebeE"], gix * 16 + c)
                    for h in range(16):
                        o = h * 64
                        r = v.scalar_tensor_tensor(
                            E_prevb[:, cp * 1024 + o: cp * 1024 + o + 64],
                            E_prevb[:, (1 - cp) * 1024 + o: (1 - cp) * 1024 + o + 64],
                            E_Ebc[:, h:h + 1],
                            E_stb[:, (1 - cp) * 1024 + o: (1 - cp) * 1024 + o + 64],
                            ALU.mult, ALU.add)
                    fence(r)
                    r = v.tensor_copy(E_prevbf[:], E_prevb[:, cp * 1024:(cp + 1) * 1024])
                    r.then_inc(S["corE"], 1)
                v.tensor_copy(vcs[:, 0:1], vcs[:, 1:2]).then_inc(S["pvE"], 1)
                v.wait_ge(S["dttE"], ci + 1)
                for h in range(16):
                    hp = h % 2
                    v.wait_ge(S["dmmE"], ci * 16 + h + 1)
                    v.tensor_scalar_min(E_LTm[:, hp * 128:(hp + 1) * 128],
                                        psDm[hp][:, 0:128], 0.0).then_inc(S["ltmE"], 1)
                    v.wait_ge(S["lteE"], ci * 16 + h + 1)
                    fence(v.tensor_tensor(E_ddv[:, hp:hp + 1],
                                          E_LTe[:, hp * 128 + 127: hp * 128 + 128],
                                          E_dtT[:, h:h + 1], ALU.mult))
                    if c >= 1:
                        v.tensor_tensor(E_Ylh[:, hp * 128:(hp + 1) * 128], E_Msk[:],
                                        E_LTe[:, hp * 128:(hp + 1) * 128], ALU.mult)
                    v.wait_ge(S["htE"], 16 * ci + h + 1)
                    r = v.tensor_scalar_mul(E_srhp[:, h * 64:(h + 1) * 64],
                                            PT[hp][:, 0:64], E_ddv[:, hp:hp + 1])
                    fence(r)
                    v.tensor_copy(vcs[:, 0:1], vcs[:, 1:2]).then_inc(S["ylhE"], 1)
                v.wait_ge(S["smE"], 2 * (ci + 1))
                v.tensor_copy(E_stb[:, cp * 1024: cp * 1024 + 512], P0[:])
                fence(v.tensor_copy(E_stb[:, cp * 1024 + 512: cp * 1024 + 1024], P1[:]))
                v.tensor_copy(vcs[:, 0:1], vcs[:, 1:2]).then_inc(S["steE"], 1)
                if c >= 1:
                    v.wait_ge(S["ydE"], ny)
                    v.tensor_copy(E_ysb[:, 0:512], P4[:])
                    fence(v.tensor_copy(E_ysb[:, 512:1024], P5[:]))
                    v.tensor_copy(vcs[:, 0:1], vcs[:, 1:2]).then_inc(S["yevE"], 1)
                    v.wait_ge(S["yoE"], 2 * ny)
                    v.wait_ge(S["actE"], ci + 1)
                    for h in range(16):
                        o = h * 64
                        psx = P0 if h < 8 else P1
                        r = v.scalar_tensor_tensor(E_ysb[:, o:o + 64],
                                                   psx[:, (h % 8) * 64:(h % 8) * 64 + 64],
                                                   E_EA[:, h:h + 1], E_ysb[:, o:o + 64],
                                                   ALU.mult, ALU.add)
                    fence(r)
                    v.tensor_copy(vcs[:, 0:1], vcs[:, 1:2]).then_inc(S["yaE"], 1)
                    v.wait_ge(S["hdeE"], 16 * (ci + 1))
                    for h in range(16):
                        o = h * 64
                        r = v.scalar_tensor_tensor(E_y2b[:, o:o + 64], E_hcpp[:, o:o + 64],
                                                   dps[:, gix * 16 + h: gix * 16 + h + 1],
                                                   E_ysb[:, o:o + 64], ALU.mult, ALU.add)
                    fence(r)
                    v.wait_ge(S["sgtE"], ny)
                    fence(v.tensor_tensor(E_ygb[:], E_y2b[:], E_sgt[:], ALU.mult))
                    v.tensor_copy(vcs[:, 0:1], vcs[:, 1:2]).then_inc(S["ygE"], 1)
                    v.wait_ge(S["sqE"], ny)
                    if ny >= 2:
                        v.wait_ge(S["ytE"], 8 * (ny - 1))
                    v.tensor_scalar_mul(E_ybf[:], E_ygb[:], E_rgs[:, 0:1]).then_inc(S["ybfE"], 1)
            v.tensor_copy(vcs[:, 0:1], vcs[:, 1:2]).then_inc(S["ugE"], 1)
        v.nop().then_inc(S["pg"], 1)
        v.wait_ge(S["pg"], 25)
        v.wait_ge(S["cldA"], 16 * 16)
        fence(v.tensor_tensor(F_zs1[:], F_wz1s[:], F_ws1s[:], ALU.mult))
        fence(v.tensor_scalar_mul(F_zs1[:], F_zs1[:], -1.0))
        for ncn in range(32):
            p = ncn % 2
            v.wait_ge(S["qlF0" if p == 0 else "qlF1"], FB["qlF0" if p == 0 else "qlF1"] + 16 * (ncn // 2 + 1))
            if ncn >= 1:
                v.wait_ge(S["tpF"], FB["tpF"] + 32 * ncn)
            q3 = F_qv[:, p * D:(p + 1) * D].rearrange("p (a b) -> p a b", b=4)
            fence(v.tensor_single_scalar(q3[:, :, 1], q3[:, :, 1], 6, ALU.logical_shift_right))
            fence(v.tensor_single_scalar(F_qv[:, p * D:(p + 1) * D],
                                         F_qv[:, p * D:(p + 1) * D], 63, ALU.bitwise_and))
            for gx in range(4):
                r = v.tensor_scalar(F_dq[gx * 1024:(gx + 1) * 1024] if False else F_dq[:, gx * 1024:(gx + 1) * 1024],
                                    F_qv[:, p * D + gx * 1024: p * D + (gx + 1) * 1024],
                                    F_ws1s[:, ncn:ncn + 1], F_zs1[:, ncn:ncn + 1],
                                    ALU.mult, ALU.add)
            r.then_inc(S["dqF"], 1)
        for th in range(2):
            v.wait_ge(S["yglF"], FB["yglF"] + 16 * 64 * (th + 1))
            for kc in range(64):
                fence(v.tensor_scalar_mul(F_tmp[:].bitcast(BF16)[:, 0:1024],
                                          F_ygh[:, kc * 1024:(kc + 1) * 1024],
                                          F_gnws[:, kc:kc + 1]))
                fence(v.tensor_copy(F_ygh[:, kc * 1024:(kc + 1) * 1024],
                                    F_tmp[:].bitcast(BF16)[:, 0:1024]))
            v.tensor_copy(vcs[:, 0:1], vcs[:, 1:2]).then_inc(S["gnfF"], 1)
            for nb in range(32):
                i = th * 32 + nb
                v.wait_ge(S["mmaF"], FB["mmaF"] + i + 1)
                if i >= 2:
                    v.wait_ge(S["oastF"], FB["oastF"] + 16 * (i - 1))
                v.tensor_copy(F_oa[:, (i % 2) * 1024: (i % 2) * 1024 + 512], P0[:])
                r = v.tensor_copy(F_oa[:, (i % 2) * 1024 + 512:(i % 2) * 1024 + 1024], P1[:])
                r.then_inc(S["evaF"], 1)
        for th in range(2):
            for nb in range(32):
                i = th * 32 + nb
                p = i % 2
                v.wait_ge(S["mmfF"], FB["mmfF"] + i + 1)
                v.wait_ge(S["oalF0" if p == 0 else "oalF1"], FB["oalF0" if p == 0 else "oalF1"] + 16 * (i // 2 + 1))
                v.wait_ge(S["x1lF0" if p == 0 else "x1lF1"], FB["x1lF0" if p == 0 else "x1lF1"] + 16 * (i // 2 + 1))
                if i < 2:
                    v.wait_ge(S["oastF"], FB["oastF"] + 16 * (63 + i))
                if i >= 2:
                    v.wait_ge(S["otF"], FB["otF"] + 8 * (i - 1))
                for bk, psx in [(0, P2), (1, P3)]:
                    o0_ = p * 1024 + bk * 512
                    fence(v.tensor_scalar_add(F_oa[:, o0_:o0_ + 512], psx[:], F_b1s[:, nb:nb + 1]))
                fence(v.tensor_tensor(F_oa[:, p * 1024:(p + 1) * 1024], F_oa[:, p * 1024:(p + 1) * 1024],
                                      F_ob[:, p * 1024:(p + 1) * 1024], ALU.add))
                fence(v.tensor_tensor(F_oa[:, p * 1024:(p + 1) * 1024], F_oa[:, p * 1024:(p + 1) * 1024],
                                      F_xnh[:, nb * 1024:(nb + 1) * 1024], ALU.add))
                r = v.tensor_tensor(F_oa[:, p * 1024:(p + 1) * 1024], F_oa[:, p * 1024:(p + 1) * 1024],
                                    F_xs[:, p * 1024:(p + 1) * 1024], ALU.add)
                r.then_inc(S["evfF"], 1)
        v.nop().then_inc(S["pg"], 1)

    @block.scalar
    def _(s):
        nfes = [0]

        def fens(r):
            r.then_inc(S["fesS"], 1)
            nfes[0] += 1
            s.wait_ge(S["fesS"], nfes[0])

        for i in range(NT):
            p = i % 2
            q = p * 8
            s.wait_ge(S["bnA"], i + 1)
            s.activation(A_tiny[:, q + 5:q + 6], A_mv[:, p * 2 + 1:p * 2 + 2], AF.Ln, bias=cst[:, 0:1])
            s.activation(A_tiny[:, q + 0:q + 1], A_tiny[:, q + 5:q + 6], AF.Exp, scale=-0.5)
            s.activation(A_tiny[:, q + 2:q + 3], A_mv[:, p * 2:p * 2 + 1], AF.Identity, scale=-1.0)
            s.copy(scs[:, 0:1], A_tiny[:, q + 0:q + 1]).then_inc(S["rsA"], 1)
            for kc in range(32):
                s.wait_ge(S["tpA"], i * 32 + kc + 1)
                s.copy(xsb[:, kc * T + i * 128: kc * T + (i + 1) * 128],
                       PT[kc % 2][:, 0:128]).then_inc(S["txA"], 1)
        s.nop().then_inc(S["pg"], 1)
        s.wait_ge(S["pg"], 5)
        for ncn in range(32):
            if ncn >= 1:
                s.wait_ge(S["mmB"], ncn)
            for kc in range(32):
                s.wait_ge(S["tpB"], ncn * 32 + kc + 1)
                s.copy(B_wT[:, kc * 128:(kc + 1) * 128],
                       PT[kc % 2][:, 0:128]).then_inc(S["txB"], 1)
        s.nop().then_inc(S["pg"], 1)
        s.wait_ge(S["pg"], 10)
        for i in range(NT):
            s.wait_ge(S["bnC"], i + 1)
            s.activation(C_tiny[:, 5:6], C_mv[:, 1:2], AF.Ln, bias=cst[:, 0:1])
            s.activation(C_tiny[:, 0:1], C_tiny[:, 5:6], AF.Exp, scale=-0.5)
            s.activation(C_tiny[:, 2:3], C_mv[:, 0:1], AF.Identity, scale=-1.0)
            s.activation(C_tiny[:, 5:6], C_mv[:, 1:2], AF.Ln, scale=1.0 + EPS, bias=cst[:, 1:2])
            s.activation(C_tiny[:, 1:2], C_tiny[:, 5:6], AF.Exp, scale=-0.5)
            s.activation(C_tiny[:, 3:4], C_mv[:, 0:1], AF.Identity, scale=-1.0)
            s.copy(scs[:, 0:1], C_tiny[:, 0:1]).then_inc(S["rsC"], 1)
            if i >= 1:
                s.wait_ge(S["stC"], 192 * i)
            for kc in range(32):
                s.wait_ge(S["ttC"], i * 96 + 3 * kc + 1)
                s.activation(C_stgH[:, kc * 128:(kc + 1) * 128],
                             PT[kc % 2][:, 0:128], AF.Identity, scale=nwsp[:, kc:kc + 1])
                s.wait_ge(S["ttC"], i * 96 + 3 * kc + 2)
                s.copy(C_stg[:, kc * 128:(kc + 1) * 128], PT[kc % 2][:, 128:256])
                s.wait_ge(S["ttC"], i * 96 + 3 * kc + 3)
                s.copy(C_stg1[:, kc * 128:(kc + 1) * 128],
                       PF[2 + kc % 2][:, 0:128]).then_inc(S["tpC"], 3)
        s.nop().then_inc(S["pg"], 1)
        s.wait_ge(S["pg"], 15)
        isg = 0; icv = 0
        for nb in range(145):
            r = role_d(nb)
            if r == "gate":
                s.wait_ge(S["mmD"], nb + 1)
                if isg >= 1:
                    s.wait_ge(S["stgD"], 16 * isg)
                for (bk, off, sz) in MAINB:
                    j = off - 128
                    rr = s.activation(D_sgb[:, j:j + sz], PF[bk][:], AF.Silu)
                rr.then_inc(S["evD"], 1)
                s.copy(scs[:, 1:2], scs[:, 0:1]).then_inc(S["sgD"], 1)
                isg += 1
            elif r == "hbc":
                cc = nb - 64
                icv += 1
                s.wait_ge(S["cvD"], icv)
                if cc >= 1:
                    s.wait_ge(S["sthD"], 16 * cc)
                for j in range(0, T, 512):
                    sz_ = min(512, T - j)
                    rr = s.activation(D_hcb[:, j:j + sz_], D_acc1[:, j:j + sz_], AF.Silu,
                                      bias=D_cbs[:, cc:cc + 1])
                s.copy(scs[:, 1:2], scs[:, 0:1]).then_inc(S["hcD"], 1)
            else:
                s.wait_ge(S["cvD"], icv + 1)
                s.wait_ge(S["stgD"], 16 * 64)
                for j in range(0, T, 512):
                    sz_ = min(512, T - j)
                    rr = s.activation(D_dts[:, j:j + sz_], D_cvin[:, j:j + sz_], AF.Abs)
                fens(rr)
                for j in range(0, T, 512):
                    sz_ = min(512, T - j)
                    rr = s.activation(D_acc1[:, j:j + sz_], D_dts[:, j:j + sz_], AF.Exp, scale=-1.0)
                fens(rr)
                for j in range(0, T, 512):
                    sz_ = min(512, T - j)
                    s.activation(D_acc0[:, j:j + sz_], D_acc1[:, j:j + sz_], AF.Ln, bias=1.0)
                s.copy(scs[:, 1:2], scs[:, 0:1]).then_inc(S["spD"], 1)
        s.nop().then_inc(S["pg"], 1)
        s.wait_ge(S["pg"], 20)
        for gix in range(NG):
            s.wait_ge(S["uclE"], 16 * 4 * (gix + 1))
            s.activation(E_ell[0:16, 1:2], E_alg[0:16, gix:gix + 1], AF.Exp).then_inc(S["elE"], 1)
            for c in range(NCH2):
                ci = gix * NCH2 + c
                ny = gix * 16 + c
                s.wait_ge(S["btE"], 2 * ci + 1)
                if ci >= 1:
                    s.wait_ge(S["ylhE"], 16 * ci)
                s.copy(E_dtT[:], P5[:, 256:272])
                fens(s.copy(E_acsT[:], P5[:, 272:288]))
                s.copy(scs[:, 1:2], scs[:, 0:1]).then_inc(S["dttE"], 1)
                s.wait_ge(S["yaE"], gix * 16 + c - 1 if c >= 1 else gix * 16)
                s.activation(E_EA[:], E_acsT[:], AF.Exp).then_inc(S["actE"], 1)
                if c <= NCH2 - 2:
                    s.wait_ge(S["dgvE"], gix * 16 + c)
                    s.activation(E_ell[0:16, 0:1], E_acs2[0:16, 127:128],
                                 AF.Exp).then_inc(S["elE"], 1)
                if c >= 1:
                    s.wait_ge(S["ebmE"], gix * 16 + c)
                    s.wait_ge(S["pvE"], ci)
                    s.copy(E_Ebc[:], P5[:, 288:304]).then_inc(S["ebeE"], 1)
                for h in range(16):
                    hp = h % 2
                    s.wait_ge(S["ltmE"], ci * 16 + h + 1)
                    s.activation(E_LTe[:, hp * 128:(hp + 1) * 128],
                                 E_LTm[:, hp * 128:(hp + 1) * 128], AF.Exp).then_inc(S["lteE"], 1)
                    s.wait_ge(S["htE"], 16 * ci + h + 1)
                    s.wait_ge(S["dttE"], ci + 1)
                    s.activation(E_hdtp[:, h * 64:(h + 1) * 64], PT[hp][:, 0:64],
                                 AF.Identity, scale=E_dtT[:, h:h + 1])
                    s.copy(E_hcpp[:, h * 64:(h + 1) * 64], PT[hp][:, 0:64]).then_inc(S["hdeE"], 1)
                if c >= 1:
                    s.wait_ge(S["ygE"], ny)
                    s.activation(E_y2b[:], E_ygb[:], AF.Square, accum_out=E_ssq[:])
                    fens(s.activation(E_rgs[:, 1:2], E_ssq[:], AF.Ln, scale=1.0 / 1024.0,
                                      bias=cst[:, 0:1]))
                    s.activation(E_rgs[:, 0:1], E_rgs[:, 1:2], AF.Exp, scale=-0.5).then_inc(S["sqE"], 1)
                    if ny >= 2:
                        s.wait_ge(S["ystE0"], 16 * (ny - 1))
                    for r8 in range(8):
                        s.wait_ge(S["ytE"], 8 * (ny - 1) + r8 + 1)
                        s.copy(E_ygT[:, r8 * 128:(r8 + 1) * 128],
                               PT[r8 % 2][:, 128:256]).then_inc(S["ytxE"], 1)
        s.nop().then_inc(S["pg"], 1)
        s.wait_ge(S["pg"], 25)
        for ncn in range(32):
            for kc in range(32):
                s.wait_ge(S["tpF"], FB["tpF"] + ncn * 32 + kc + 1)
                s.copy(F_w1T[:, kc * 128:(kc + 1) * 128], PT[kc % 2][:, 0:128]).then_inc(S["txF"], 1)
        for th in range(2):
            for nb in range(32):
                i = th * 32 + nb
                for r8 in range(8):
                    s.wait_ge(S["otF"], FB["otF"] + 8 * i + r8 + 1)
                    s.copy(F_otb[:, r8 * 128:(r8 + 1) * 128],
                           PF[4 + r8 % 2][:, 0:128]).then_inc(S["otxF"], 1)
        s.nop().then_inc(S["pg"], 1)

    @block.tensor
    def _(t):
        t.wait_ge(S["cldA"], 16 * 9)
        for i in range(NT):
            p = i % 2
            t.wait_ge(S["afA"], i + 1)
            for kc in range(32):
                if i * 32 + kc >= 2:
                    t.wait_ge(S["txA"], i * 32 + kc - 1)
                t.transpose(PT[kc % 2][:, 0:128],
                            A_xnt[:, p * D + kc * 128: p * D + (kc + 1) * 128],
                            idbs[:]).then_inc(S["tpA"], 1)
        t.nop().then_inc(S["pg"], 1)
        t.wait_ge(S["pg"], 5)
        for tt in range(32):
            t.wait_ge(S["mrdB"], tt + 1)
            t.wait_ge(S["eldB0" if tt % 2 == 0 else "eldB1"], 16 * (tt // 2 + 1))
            t.matmul(P5[:, 0:128], B_esb[:, (tt % 2) * 128:(tt % 2 + 1) * 128],
                     B_mred[:, (tt % 2) * 128:(tt % 2 + 1) * 128],
                     start=(tt == 0), stop=(tt == 31)).then_inc(S["msumB"], 1)
        for ncn in range(32):
            pw = ncn % 2
            t.wait_ge(S["bmB"], 1)
            t.wait_ge(S["pldB"], 16 * (ncn + 1))
            if ncn >= 1:
                t.wait_ge(S["wsB"], ncn)
            t.matmul(P5[:, 0:128], B_psl[:], B_bm[:], start=True, stop=True).then_inc(S["mxB"], 1)
            t.wait_ge(S["wsB"], ncn + 1)
            for kc in range(32):
                if ncn * 32 + kc >= 2:
                    t.wait_ge(S["txB"], ncn * 32 + kc - 1)
                t.transpose(PT[kc % 2][:, 0:128], B_wsum[:, kc * 128:(kc + 1) * 128],
                            idbs[:]).then_inc(S["tpB"], 1)
            t.wait_ge(S["txB"], 32 * (ncn + 1))
            if ncn >= 1:
                t.wait_ge(S["evB"], ncn)
            for kc in range(32):
                for tb, (off, sz) in enumerate([(0, 512), (512, 512), (1024, 512), (1536, 512), (2048, 128)]):
                    r = t.matmul(PF[tb][:, 0:sz],
                                 B_wT[:, kc * 128:(kc + 1) * 128],
                                 xsb[:, kc * T + off: kc * T + off + sz],
                                 start=(kc == 0), stop=(kc == 31))
            r.then_inc(S["mmB"], 1)
        t.nop().then_inc(S["pg"], 1)
        t.wait_ge(S["pg"], 10)
        for i in range(NT):
            p = i % 2
            t.wait_ge(S["o0C0" if p == 0 else "o0C1"], 16 * 4 * (i // 2 + 1))
            for ncx in range(32):
                if i * 32 + ncx >= 2:
                    t.wait_ge(S["adC"], i * 32 + ncx - 1)
                t.transpose(PF[ncx % 2][:, 0:128],
                            C_o0c[:, p * D + ncx * 128: p * D + (ncx + 1) * 128]
                            .bitcast(F32) if False else C_o0c[:, p * D + ncx * 128: p * D + (ncx + 1) * 128],
                            idfs[:]).then_inc(S["toC"], 1)
            t.wait_ge(S["afC"], i + 1)
            for kc in range(32):
                base = i * 96 + 3 * kc
                if base >= 4:
                    t.wait_ge(S["tpC"], base - 3)
                t.transpose(PT[kc % 2][:, 0:128], C_ht[:, kc * 128:(kc + 1) * 128],
                            idbs[:]).then_inc(S["ttC"], 1)
                t.transpose(PT[kc % 2][:, 128:256], C_xnt[:, kc * 128:(kc + 1) * 128],
                            idbs[:]).then_inc(S["ttC"], 1)
                t.transpose(PF[2 + kc % 2][:, 0:128], C_x1[:, kc * 128:(kc + 1) * 128],
                            idfs[:]).then_inc(S["ttC"], 1)
        t.nop().then_inc(S["pg"], 1)
        t.wait_ge(S["pg"], 15)
        t.wait_ge(S["txA"], 544 + 16 * 64)
        for nb in range(145):
            p = nb % 2
            r_ = role_d(nb)
            t.wait_ge(S["wD0" if p == 0 else "wD1"], 16 * (nb // 2 + 1))
            if nb >= 1:
                t.wait_ge(S["evD"], nb)
            blocks = ([HALOB] + MAINB) if r_ in ("hbc", "dt") else MAINB
            for kc in range(32):
                lhsT = D_wsb[:, p * 4096 + kc * 128: p * 4096 + (kc + 1) * 128]
                for (bk, off, sz) in blocks:
                    psx = P4 if bk == 4 else PF[bk]
                    r = t.matmul(psx[:, 0:sz], lhsT,
                                 hfm[:, kc * T + off: kc * T + off + sz],
                                 start=(kc == 0), stop=(kc == 31))
            r.then_inc(S["mmD"], 1)
        t.nop().then_inc(S["pg"], 1)
        t.wait_ge(S["pg"], 20)
        for gix in range(NG):
            t.wait_ge(S["uclE"], 16 * 4 * (gix + 1))
            for c in range(NCH2):
                ci = gix * NCH2 + c
                cp = c % 2
                p = ci % 2
                ny = gix * 16 + c
                t.wait_ge(S["acsE"], ci + 1)
                if ci >= 1:
                    t.wait_ge(S["ltmE"], 16 * ci)
                    t.wait_ge(S["dttE"], ci)
                    t.wait_ge(S["ebeE"], gix * 16 + c - 1 if c >= 1 else gix * 16)
                    if ny >= 1 and c >= 1:
                        t.wait_ge(S["yevE"], ny - 1)
                    elif gix >= 1:
                        t.wait_ge(S["yevE"], gix * 16)
                t.transpose(P5[:, 256:272], E_dtu[0:16, c * 128:(c + 1) * 128], idfs[0:16, 0:16])
                t.transpose(P5[:, 272:288], E_acs2[0:16, :], idfs[0:16, 0:16]).then_inc(S["btE"], 1)
                t.wait_ge(S["hlE0" if p == 0 else "hlE1"], 16 * (ci // 2 + 1))
                if ci >= 1:
                    t.wait_ge(S["hdeE"], 16 * ci)
                    t.wait_ge(S["ylhE"], 16 * ci)
                    t.wait_ge(S["btvE"], ci)
                    if ny >= 1:
                        t.wait_ge(S["ytxE"], 8 * (ny - 1 if c >= 1 else ny))
                t.transpose(PT0[:, 0:128], E_Bb[:, c * 128:(c + 1) * 128], idbs[:])
                if c >= 1:
                    if ny >= 2:
                        t.wait_ge(S["mskE"], ny - 1)
                        t.wait_ge(S["yaE"], ny - 1)
                    t.wait_ge(S["steE"], ci)
                    t.matmul(P4[:, 0:128], E_Bb[:, c * 128:(c + 1) * 128],
                             E_Cb[:, c * 128:(c + 1) * 128], start=True, stop=True).then_inc(S["mtE"], 1)
                    if ny >= 2:
                        t.wait_ge(S["sgtE"], ny - 1)
                    t.wait_ge(S["slE0" if ny % 2 == 0 else "slE1"],
                              16 * ((ny - 1) // 2 + 1))
                    for r8 in range(8):
                        rr = t.transpose(PT1[:, r8 * 128:(r8 + 1) * 128],
                                         E_sgc[:, (ny % 2) * 1024 + r8 * 128:
                                               (ny % 2) * 1024 + (r8 + 1) * 128],
                                         idbs[:])
                    rr.then_inc(S["btE"], 1)
                else:
                    t.nop().then_inc(S["btE"], 1)
                if c >= 1:
                    t.wait_ge(S["dgvE"], gix * 16 + c)
                    if gix * 16 + c >= 2:
                        t.wait_ge(S["ebeE"], gix * 16 + c - 1)
                        t.wait_ge(S["pvE"], ci)
                    t.matmul(P5[:, 288:304], ons16[:], E_diag[0:16, :],
                             start=True, stop=True).then_inc(S["ebmE"], 1)
                t.wait_ge(S["dmlE"], 16 + 32 * (ci + 1))
                t.wait_ge(S["dttE"], ci + 1)

                def ydiag(h):
                    t.wait_ge(S["ylhE"], ci * 16 + h + 1)
                    t.wait_ge(S["hdeE"], ci * 16 + h + 1)
                    if h == 0:
                        t.wait_ge(S["yevE"], ny - 1)
                        t.wait_ge(S["mskE"], ny)
                    if h == 8:
                        t.wait_ge(S["ebeE"], gix * 16 + c)
                    rydm = t.matmul((P4 if h < 8 else P5)[:, (h % 8) * 64:(h % 8) * 64 + 64],
                                    E_Ylh[:, (h % 2) * 128:(h % 2 + 1) * 128],
                                    E_hdtp[:, h * 64:(h + 1) * 64],
                                    start=True, stop=True)
                    if h == 15:
                        rydm.then_inc(S["ydE"], 1)
                for h in range(16):
                    hp = h % 2
                    if c >= 1 and h >= 2:
                        ydiag(h - 2)
                    if ci * 16 + h >= 2:
                        t.wait_ge(S["ltmE"], ci * 16 + h - 1)
                    t.matmul(psDm[hp][:, 0:128], E_dop[0:2, h * 128:(h + 1) * 128],
                             E_dop[0:2, 2048 + h * 128: 2048 + (h + 1) * 128],
                             start=True, stop=True).then_inc(S["dmmE"], 1)
                    if ci * 16 + h >= 2:
                        t.wait_ge(S["hdeE"], ci * 16 + h - 1)
                        t.wait_ge(S["ylhE"], ci * 16 + h - 1)
                    if h == 0:
                        t.wait_ge(S["btvE"], ci + 1)
                    if h == 1 and c >= 1:
                        t.wait_ge(S["sgtE"], ny)
                    t.transpose(PT[hp][:, 0:64],
                                hid3E[(h % 2) * 64:(h % 2) * 64 + 64, p, h // 2, :],
                                idb64s[(h % 2) * 64:(h % 2) * 64 + 64, :]).then_inc(S["htE"], 1)
                if c >= 1:
                    for h in range(14, 16):
                        ydiag(h)
                t.wait_ge(S["ylhE"], 16 * (ci + 1))
                t.wait_ge(S["btvE"], ci + 1)
                if ci >= 1:
                    t.wait_ge(S["steE"], ci)
                if ny >= 1 and c >= 1:
                    t.wait_ge(S["yaE"], ny - 1)
                elif gix >= 1:
                    t.wait_ge(S["yaE"], gix * 16)
                t.matmul(P0[:], E_BTs[:], E_srhp[:, 0:512], start=True, stop=True)
                t.matmul(P1[:], E_BTs[:], E_srhp[:, 512:1024], start=True, stop=True).then_inc(S["smE"], 2)
                if c >= 1:
                    t.wait_ge(S["corE"], ny)
                    t.wait_ge(S["steE"], ci + 1)
                    t.matmul(P0[:], E_Cb[:, c * 128:(c + 1) * 128], E_prevbf[:, 0:512],
                             start=True, stop=True)
                    t.matmul(P1[:], E_Cb[:, c * 128:(c + 1) * 128], E_prevbf[:, 512:1024],
                             start=True, stop=True).then_inc(S["yoE"], 2)
                    t.wait_ge(S["ybfE"], ny)
                    if ny >= 2:
                        t.wait_ge(S["ytxE"], 8 * (ny - 1))
                    for r8 in range(8):
                        if r8 >= 2:
                            t.wait_ge(S["ytxE"], 8 * (ny - 1) + r8 - 1)
                        t.transpose(PT[r8 % 2][:, 128:256],
                                    E_ybf[:, r8 * 128:(r8 + 1) * 128],
                                    idbs[:]).then_inc(S["ytE"], 1)
        t.nop().then_inc(S["pg"], 1)
        t.wait_ge(S["pg"], 25)
        for ncn in range(32):
            t.wait_ge(S["dqF"], FB["dqF"] + ncn + 1)
            if ncn >= 1:
                t.wait_ge(S["w1stF"], FB["w1stF"] + 16 * ncn)
            for kc in range(32):
                if ncn * 32 + kc >= 2:
                    t.wait_ge(S["txF"], FB["txF"] + ncn * 32 + kc - 1)
                t.transpose(PT[kc % 2][:, 0:128], F_dq[:, kc * 128:(kc + 1) * 128],
                            idbs[:]).then_inc(S["tpF"], 1)
        for th in range(2):
            t.wait_ge(S["gnfF"], FB["gnfF"] + th + 1)
            for nb in range(32):
                i = th * 32 + nb
                p = i % 2
                t.wait_ge(S["wlF0" if p == 0 else "wlF1"], FB["wlF0" if p == 0 else "wlF1"] + 16 * (i // 2 + 1))
                if i >= 1:
                    t.wait_ge(S["evaF"], FB["evaF"] + i)
                for kc in range(64):
                    lhsT = F_wsb[:, p * 8192 + kc * 128: p * 8192 + (kc + 1) * 128]
                    t.matmul(P0[:], lhsT, F_ygh[:, kc * 1024: kc * 1024 + 512],
                             start=(kc == 0), stop=(kc == 63))
                    r = t.matmul(P1[:], lhsT, F_ygh[:, kc * 1024 + 512: kc * 1024 + 1024],
                                 start=(kc == 0), stop=(kc == 63))
                r.then_inc(S["mmaF"], 1)
        for th in range(2):
            t.wait_ge(S["xnlF"], FB["xnlF"] + 16 * 32 * (th + 1))
            for nb in range(32):
                i = th * 32 + nb
                p = i % 2
                t.wait_ge(S["w1lF0" if p == 0 else "w1lF1"], FB["w1lF0" if p == 0 else "w1lF1"] + 16 * (i // 2 + 1))
                if i >= 1:
                    t.wait_ge(S["evfF"], FB["evfF"] + i)
                for kc in range(32):
                    lhsT = F_w1sb[:, p * 4096 + kc * 128: p * 4096 + (kc + 1) * 128]
                    t.matmul(P2[:], lhsT, F_xnh[:, kc * 1024: kc * 1024 + 512],
                             start=(kc == 0), stop=(kc == 31))
                    r = t.matmul(P3[:], lhsT, F_xnh[:, kc * 1024 + 512: kc * 1024 + 1024],
                                 start=(kc == 0), stop=(kc == 31))
                r.then_inc(S["mmfF"], 1)
                t.wait_ge(S["evfF"], FB["evfF"] + i + 1)
                if i >= 1:
                    t.wait_ge(S["otxF"], FB["otxF"] + 8 * i)
                for r8 in range(8):
                    if r8 >= 2:
                        t.wait_ge(S["otxF"], FB["otxF"] + 8 * i + r8 - 1)
                    t.transpose(PF[4 + r8 % 2][:, 0:128],
                                F_oa[:, p * 1024 + r8 * 128: p * 1024 + (r8 + 1) * 128],
                                idfs[:]).then_inc(S["otF"], 1)
        t.nop().then_inc(S["pg"], 1)


def kernel_fused(L0_attn_data, L0_attn_weight, L0_attn_mask, L0_ffn_weight_q,
                 L0_ffn_scales_and_zeros, L1_attn_norm_w, L1_attn_in_proj_w,
                 L1_attn_conv_w, L1_attn_conv_b, L1_attn_dt_bias, L1_attn_A_log,
                 L1_attn_Dp, L1_attn_gnorm_w, L1_attn_out_proj_w, L1_ffn_qweight,
                 L1_ffn_weight_scale, L1_ffn_weight_zero_point, L1_ffn_bias):
    x0 = np.ascontiguousarray(np.asarray(L0_attn_data, np.float32).reshape(16384, D))
    Ebm, Pbm, identb = _make_l0_consts()
    szr = _stage_szr(L0_ffn_scales_and_zeros)
    WinT = _stage_proj_w(L1_attn_in_proj_w)
    cw, cb, nw, db = _stage_conv(L1_attn_conv_w, L1_attn_conv_b, L1_attn_norm_w,
                                 L1_attn_dt_bias)
    OutWT, A1, ws1, wz1, b1f, gnwf = _stage_final(
        L1_attn_out_proj_w, L1_ffn_qweight, L1_ffn_weight_scale,
        L1_ffn_weight_zero_point, L1_ffn_bias, L1_attn_gnorm_w)
    A_log = np.asarray(L1_attn_A_log, np.float32)
    Dp = np.asarray(L1_attn_Dp, np.float32)
    alog8 = np.ascontiguousarray(A_log.reshape(8, 16).T)
    dpb = np.tile(Dp, (128, 1)).astype(np.float32)
    idf = np.eye(128, dtype=np.float32)
    triu = np.triu(np.ones((128, 128), np.float32))
    ones16 = np.ones((16, 128), np.float32)
    i16 = np.eye(16, dtype=np.float32)
    idb64 = np.vstack([np.eye(64, dtype=BF16NP)] * 2)
    negrow = np.vstack([np.zeros((1, 2048), np.float32), -np.ones((1, 2048), np.float32)])
    Wnk = np.asarray(L0_attn_weight, np.float32)
    Qnk = np.asarray(L0_ffn_weight_q, np.int32)
    maskA = np.asarray(L0_attn_mask, np.int32)

    common = dict(Wnk=Wnk, Qnk=Qnk, maskA=maskA, szr=szr, Ebm=Ebm, Pbm=Pbm,
                  WinT=WinT, nwf=nw, convw=cw, convb=cb, dtb=db, alog8=alog8,
                  dpb=dpb, OutWT=OutWT, A1=A1, ws1=ws1, wz1=wz1, bias1f=b1f,
                  gnwf=gnwf, identb=identb, idf=idf, triu=triu, ones16=ones16,
                  i16=i16, idb64=idb64, negrow=negrow)
    in_maps = []
    for c in range(NCORES):
        b, hb = divmod(c, 2)
        base = b * S + hb * TC
        if hb == 0:
            halo = np.zeros((128, D), np.float32)
        else:
            halo = x0[base - 128: base]
        x0s = np.ascontiguousarray(np.concatenate([halo, x0[base: base + TC]], axis=0))
        smask = np.full((128, 1), 1.0 if hb == 1 else 0.0, np.float32)
        partner = (c - 1) % 8
        sgidx = (partner * 1024 + np.arange(8)[None, :] * 128
                 + np.arange(128)[:, None]).astype(np.int32)
        ins = dict(common)
        ins.update(x0s=x0s, smask=smask, sgidx=np.ascontiguousarray(sgidx))
        in_maps.append(ins)
    nc = build_fused()
    res = run_bass_kernel_spmd(nc, in_maps, core_ids=list(range(NCORES)))
    out = np.concatenate([res.results[c]["out"] for c in range(NCORES)], axis=0)
    return np.ascontiguousarray(out.reshape(B_SZ, S, D))
